# version 12
# baseline (speedup 1.0000x reference)
import numpy as np
import ml_dtypes
from contextlib import ExitStack

import concourse.bass as bass
import concourse.mybir as mybir
from concourse.bass_utils import run_bass_kernel_spmd

F32 = mybir.dt.float32
BF16 = mybir.dt.bfloat16
I32 = mybir.dt.int32
ALU = mybir.AluOpType
AF = mybir.ActivationFunctionType
AX = mybir.AxisListType
ds = bass.ds
BF = ml_dtypes.bfloat16

NCORE = 8
TOK = 1024
DM = 2048
SEQ = 4096
NH0 = 16
SCALE0 = float(np.float32(1.0) / np.sqrt(np.float32(128.0)))
MAGIC = 12582912.0
LNEPS = 1e-5
QKV_SH = 768
WOUT_SH = 256
DFF0 = 5632
DFF0P = 6144
W1_SH = 704
W3_SH = 256
NF4 = [-1.0, -0.6962, -0.5251, -0.3949, -0.2844, -0.1848, -0.0911, 0.0,
       0.0796, 0.1609, 0.2461, 0.3379, 0.4407, 0.5626, 0.723, 1.0]
NF4_MID = [(NF4[i] + NF4[i + 1]) / 2.0 for i in range(15)]
NF4_ASC = {i: float(np.float32(np.float32(NF4[i + 1]) - np.float32(NF4[i])))
           for i in range(7, 15)}
NF4_DSC = {i: float(np.float32(np.float32(NF4[i]) - np.float32(NF4[i + 1])))
           for i in range(7)}
NF4_NBLK = 180224
GOFF = 6.906768
LNC1 = float(np.log(128e-6))
KV1_K = 1024 * 1024
KV1_V = 1024 * 1024
KV1_G = 8 * 2048
KV1SZ = KV1_K + KV1_V + KV1_G


class Sem:
    __slots__ = ("name", "sched", "handle")

    def __init__(self, name):
        self.name, self.sched, self.handle = name, 0, None


class Trk:
    __slots__ = ("wid", "pending")

    def __init__(self):
        self.wid = None
        self.pending = []

    def wwaits(self):
        w = list(self.pending)
        if self.wid is not None:
            w.append(self.wid)
        self.pending = []
        return w

    def rwaits(self):
        return [self.wid] if self.wid is not None else []


class Prog:
    def __init__(self):
        self.ops = []
        self.sems = []

    def sem(self, name):
        s = Sem(name)
        self.sems.append(s)
        return s

    def op(self, engine, fn, waits=(), sem=None, n_inc=1):
        seen = {}
        for s, v in waits:
            if v > 0 and seen.get(s, 0) < v:
                seen[s] = v
        w = list(seen.items())
        ident = None
        if sem is not None:
            sem.sched += n_inc
            ident = (sem, sem.sched)
        self.ops.append((engine, fn, w, (sem, n_inc) if sem else None))
        return ident

    def dma(self, engine, out_ap, in_ap, waits=(), sem=None, slow=False):
        def fn(eng, o=out_ap, i=in_ap):
            if slow:
                return eng.dma_start(out=o, in_=i,
                                     allow_slow_non_contiguous=True)
            return eng.dma_start(out=o, in_=i)
        return self.op(engine, fn, waits=waits, sem=sem, n_inc=16)

    def emit(self, nc, ctx):
        for s in self.sems:
            s.handle = ctx.enter_context(nc.semaphore(s.name))
        block = ctx.enter_context(nc.Block())
        by_eng = {}
        for rec in self.ops:
            by_eng.setdefault(rec[0], []).append(rec)

        def run_engine(eng, recs):
            last_wait = {}
            for _, fn, waits, inc in recs:
                for s, v in waits:
                    if last_wait.get(s, 0) < v:
                        eng.wait_ge(s.handle, v)
                        last_wait[s] = v
                inst = fn(eng)
                if inc is not None:
                    inst.then_inc(inc[0].handle, inc[1])

        for name in ("sync", "tensor", "scalar", "vector", "gpsimd"):
            if name in by_eng:
                recs = by_eng[name]
                getattr(block, name)(lambda eng, recs=recs: run_engine(eng, recs))


PHASES = []


def build_nc():
    nc = bass.Bass("TRN2", target_bir_lowering=False)
    p = Prog()

    s_mm = p.sem("s_mm")
    s_ev = p.sem("s_ev")
    s_ve = p.sem("s_ve")
    s_cc = p.sem("s_cc")
    s_gp = p.sem("s_gp")
    s_fin = p.sem("s_fin")

    dma_sems = {}

    def dsem(name):
        if name not in dma_sems:
            dma_sems[name] = p.sem(name)
        return dma_sems[name]

    ESEM = {"tensor": s_mm, "scalar": s_ev, "vector": s_ve, "gpsimd": s_gp}
    PHASES.clear()

    def mark(name):
        PHASES.append((name, {s.name: s.sched for s in
                              (s_mm, s_ev, s_ve, s_cc, s_gp)}))

    def eop(engine, fn, reads=(), writes=(), extra=(), need_id=True):
        waits = list(extra)
        for t in reads:
            waits += t.rwaits()
        wl = [t.wwaits() for t in writes]
        for x in wl:
            waits += x
        ident = p.op(engine, fn, waits=waits,
                     sem=(ESEM[engine] if need_id else None))
        if ident is not None:
            for t in reads:
                t.pending.append(ident)
            for t in writes:
                t.wid = ident
        return ident

    POOL = {}
    for _n in ("d_eye", "d_ones", "d_onesc", "d_sel", "d_bout", "d_wg",
               "d_wg2", "d_wgl", "d_tril", "d_caus", "d_mr", "d_bc1",
               "d_bc2", "d_bc3"):
        POOL[_n] = "d_m1"
    for _n in ("d_arin", "d_arout", "d_pgd", "d_ptT", "d_ptm", "d_g1l",
               "d_g1c"):
        POOL[_n] = "d_m2"
    for _n in ("d_pad1", "d_pad2", "d_pad3"):
        POOL[_n] = "d_pad"
    for _w in ("w1", "w2", "w3"):
        POOL[f"d_scl{_w}"] = "d_scl"
        POOL[f"d_sbm{_w}"] = "d_sbm"
        POOL[f"d_deq{_w}"] = "d_deq"
        POOL[f"d_deqA{_w}"] = "d_dqA"
        POOL[f"d_q{_w}"] = "d_qq"
        POOL[f"d_stw{_w}0"] = "d_sw0"
        POOL[f"d_stw{_w}1"] = "d_sw1"
    for _n in ("d_c1_wq1", "d_c1_wc1", "d_c1_wk1", "d_c1_wv1"):
        POOL[_n] = "d_c1"
    for _t in range(8):
        POOL[f"d_x{_t}"] = "d_x"
    pool_chain = {}

    def dop(sem_name, out_ap, in_ap, reads=(), writes=(), extra=(), engine="sync"):
        name = POOL.get(sem_name, sem_name)
        writes = list(writes)
        if name != sem_name:
            writes.append(pool_chain.setdefault(name, Trk()))
        waits = list(extra)
        for t in reads:
            waits += t.rwaits()
        for t in writes:
            waits += t.wwaits()
        ident = p.dma(engine, out_ap, in_ap, waits=waits, sem=dsem(name),
                      slow=sem_name in ("d_bc1", "d_bc2", "d_bc3", "d_cold0", "d_cold1"))
        for t in reads:
            t.pending.append(ident)
        for t in writes:
            t.wid = ident
        return ident

    def inp(name, shape, dt=F32):
        return nc.dram_tensor(name, list(shape), dt, kind="ExternalInput").ap()

    x_in = inp("x_shard", [TOK, DM])
    wqkv_sl = inp("wqkv_sl", [QKV_SH, DM])
    wout_sl = inp("wout_sl", [WOUT_SH, DM])
    b_out_in = inp("b_out", [1, DM])
    w1_sl = inp("w1_sl", [W1_SH, DM], BF16)
    w2_sl = inp("w2_sl", [W1_SH, DM], BF16)
    w3_sl = inp("w3_sl", [W3_SH, DFF0], BF16)
    wq1_sl = inp("wq1_sl", [256, DM])
    wc1_sl = inp("wc1_sl", [256, DM])
    wk1_sl = inp("wk1_sl", [256, 1024])
    wv1_sl = inp("wv1_sl", [256, 1024])
    wg_full = inp("wg_full", [DM, 8])
    bc_in = inp("bc_in", [1, DM])
    w1b_sl = inp("w1b_sl", [1024, DM])
    w2b_sl = inp("w2b_sl", [DM, 1024])
    caus_in = inp("caus_in", [1, 256])
    maskR_in = inp("maskR_in", [8, 4])

    out_x = nc.dram_tensor("out", [TOK, DM], F32, kind="ExternalOutput").ap()

    qkvT_contrib = nc.dram_tensor("qkvT_contrib", [DM, QKV_SH], BF16)
    qkvT_arena = nc.dram_tensor("qkvT_arena", [NCORE, DM, QKV_SH], BF16)
    woutT_contrib = nc.dram_tensor("woutT_contrib", [DM, WOUT_SH], BF16)
    woutT_arena = nc.dram_tensor("woutT_arena", [NCORE, DM, WOUT_SH], BF16)

    w1q_dram = nc.dram_tensor("w1q_dram", [W1_SH, DM], BF16)
    w2q_dram = nc.dram_tensor("w2q_dram", [W1_SH, DM], BF16)
    w3q_dram = nc.dram_tensor("w3q_dram", [W3_SH, DFF0], BF16)
    scl_dram = nc.dram_tensor("scl_dram", [3 * 22528], BF16)
    w1T_contrib = nc.dram_tensor("w1T_contrib", [DM, 768], BF16)
    w2T_contrib = nc.dram_tensor("w2T_contrib", [DM, 768], BF16)
    w3T_contrib = nc.dram_tensor("w3T_contrib", [DFF0P, 256], BF16)
    w1T_arena = nc.dram_tensor("w1T_arena", [NCORE, DM, 768], BF16)
    w2T_arena = nc.dram_tensor("w2T_arena", [NCORE, DM, 768], BF16)
    w3T_arena = nc.dram_tensor("w3T_arena", [NCORE, DFF0P, 256], BF16)
    w3T_flat = nc.dram_tensor("w3T_flat", [DFF0P, DM], BF16)
    wq1_contrib = nc.dram_tensor("wq1_contrib", [256, DM], BF16)
    wq1_arena = nc.dram_tensor("wq1_arena", [DM, DM], BF16)
    wc1_contrib = nc.dram_tensor("wc1_contrib", [256, DM], BF16)
    wc1_arena = nc.dram_tensor("wc1_arena", [DM, DM], BF16)
    wk1_contrib = nc.dram_tensor("wk1_contrib", [256, 1024], BF16)
    wk1_arena = nc.dram_tensor("wk1_arena", [DM, 1024], BF16)
    wv1_contrib = nc.dram_tensor("wv1_contrib", [256, 1024], BF16)
    wv1_arena = nc.dram_tensor("wv1_arena", [DM, 1024], BF16)
    kv1_contrib = nc.dram_tensor("kv1_contrib", [KV1SZ], BF16)
    kv1_gather = nc.dram_tensor("kv1_gather", [4, KV1SZ], BF16)
    w1e_contrib = nc.dram_tensor("w1e_contrib", [DM, 1024], BF16)
    w1e_arena = nc.dram_tensor("w1e_arena", [8, DM, 1024], BF16)
    w2e_contrib = nc.dram_tensor("w2e_contrib", [1024, DM], BF16)
    w2e_arena = nc.dram_tensor("w2e_arena", [8 * 1024, DM], BF16)
    vrep_dram = nc.dram_tensor("vrep_dram", [16 * 128, 32 * 128], BF16)
    vrep1_dram = nc.dram_tensor("vrep1_dram", [8 * 128, 32 * 128], BF16)
    pg_dram = nc.dram_tensor("pg_dram", [8, 1024], F32)
    wg_dram = nc.dram_tensor("wg_dram", [128, 128], BF16)
    ar_in = nc.dram_tensor("ar_in", [1, 16], F32)
    ar_out = nc.dram_tensor("ar_out", [1, 16], F32)
    kv0_contrib = nc.dram_tensor("kv0_contrib", [2 * DM * TOK], BF16)
    kv0_gather = nc.dram_tensor("kv0_gather", [4, 2 * DM * TOK], BF16)

    kv1_c_k = lambda: kv1_contrib.ap()[0:KV1_K].rearrange("(d t) -> d t", d=1024)
    kv1_c_v = lambda: kv1_contrib.ap()[KV1_K:KV1_K + KV1_V].rearrange(
        "(t d) -> t d", t=TOK)
    kv1_c_g = lambda: kv1_contrib.ap()[KV1_K + KV1_V:].rearrange(
        "(h t) -> h t", h=8)
    kv1_g_k = lambda r: kv1_gather.ap()[r, 0:KV1_K].rearrange(
        "(d t) -> d t", d=1024)
    kv1_g_v = lambda r: kv1_gather.ap()[r, KV1_K:KV1_K + KV1_V].rearrange(
        "(t d) -> t d", t=TOK)
    kv1_g_g = lambda r: kv1_gather.ap()[r, KV1_K + KV1_V:].rearrange(
        "(h t) -> h t", h=8)
    kv0_c_k = lambda: kv0_contrib.ap()[0:DM * TOK].rearrange("(d t) -> d t", d=DM)
    kv0_c_v = lambda: kv0_contrib.ap()[DM * TOK:].rearrange("(t d) -> t d", t=TOK)
    kv0_g_k = lambda r: kv0_gather.ap()[r, 0:DM * TOK].rearrange("(d t) -> d t", d=DM)
    kv0_g_v = lambda r: kv0_gather.ap()[r, DM * TOK:].rearrange("(t d) -> t d", t=TOK)

    eye_d = nc.inline_tensor(np.eye(128, dtype=BF), name="eye128")
    onesq_d = nc.inline_tensor(np.ones((1, 128), dtype=BF), name="ones1x128")
    onescol_d = nc.inline_tensor(np.ones((128, 1), dtype=BF), name="ones128x1")
    sel_np = np.zeros((64, 32 * 128), dtype=BF)
    for j in range(32):
        sel_np[2 * j, j * 128:(j + 1) * 128] = 1
        sel_np[2 * j + 1, j * 128:(j + 1) * 128] = 1
    sel_d = nc.inline_tensor(sel_np, name="sel64")
    tril_np = np.zeros((128, 128), dtype=BF)
    for kk in range(128):
        tril_np[kk, kk:] = 1
    tril_d = nc.inline_tensor(tril_np, name="tril128")

    with ExitStack() as ctx:
        def sb(scope, name, shape, dt=F32):
            return scope.enter_context(nc.sbuf_tensor(name, list(shape), dt))

        def psum(scope, name, shape, dt=F32):
            return scope.enter_context(nc.psum_tensor(name, list(shape), dt))

        x_sb = sb(ctx, "x", [128, 8 * DM])
        lnT_sb = sb(ctx, "lnT", [128, 16 * TOK], BF16)
        wq_sb = sb(ctx, "wq", [128, 2 * 4096], BF16)
        ul_sb = sb(ctx, "ul", [128, 2 * 512], BF16)
        acc2_sb = sb(ctx, "acc2", [128, 2 * 512], F32)
        tl2_sb = sb(ctx, "tl2", [128, 2 * 512], F32)
        phat_sb = sb(ctx, "phat", [128, 2 * 512], BF16)
        kvst_sb = phat_sb
        stat_sb = sb(ctx, "stat", [128, 64], F32)
        bout_sb = sb(ctx, "bout", [1, DM], BF16)
        eyeb = sb(ctx, "eyeb", [128, 128], BF16)
        onesq_sb = sb(ctx, "onesq", [1, 128], BF16)
        onescol_sb = sb(ctx, "onescol", [128, 1], BF16)
        trsp_sb = sb(ctx, "trsp", [128, 2 * 128], BF16)
        const_sb = sb(ctx, "const", [128, 8], F32)
        deqA_sb = sb(ctx, "deqA", [128, 192], BF16)
        sbm_sb = sb(ctx, "sbm", [88, 256], BF16)
        cen_sb = sb(ctx, "cen", [88, 256], BF16)
        box88 = sb(ctx, "box88", [88, 256], I32)
        sabs_sb = sb(ctx, "sabs", [88, 4], F32)
        meanbf_sb = sb(ctx, "meanbf", [1, 8], BF16)
        mean_sb = sb(ctx, "meanp", [88, 4], F32)
        arst_sb = sb(ctx, "arst", [1, 16], F32)
        zpad_sb = sb(ctx, "zpad", [128, 64], BF16)
        qtl_sb = sb(ctx, "qtl", [128, 2 * 128], BF16)

        pP1 = psum(ctx, "pP1", [128, 1024])
        pA = psum(ctx, "pA", [128, 512])
        pB = psum(ctx, "pB", [128, 512])
        pattn = psum(ctx, "pattn", [128, 512])
        ptrA = psum(ctx, "ptrA", [128, 128], BF16)
        ptrB = psum(ctx, "ptrB", [128, 128], BF16)
        ptr = [ptrA, ptrB]
        pAB = [pA, pB]

        T = lambda n=1: [Trk() for _ in range(n)] if n > 1 else Trk()
        t_x = [Trk() for _ in range(8)]
        t_lnT = Trk()
        t_qT = Trk()
        t_wq = [Trk(), Trk()]
        t_lnscr = [Trk(), Trk()]
        t_phat = [Trk(), Trk()]
        t_stat = Trk()
        t_bout = Trk()
        t_const = Trk()
        t_eye = Trk()
        t_ones = Trk()
        t_onesc = Trk()
        t_trsp = [Trk(), Trk()]
        t_pP1 = Trk()
        t_pAB = [Trk(), Trk()]
        t_pattn = Trk()
        t_ptr = [Trk(), Trk()]
        t_qkvC = Trk()
        t_woutC = Trk()
        t_qkvA = Trk()
        t_woutA = Trk()
        t_kvC = Trk()
        t_kvG = Trk()

        eop("vector", lambda v: v.memset(const_sb[:, 0:1], LNEPS),
            writes=[t_const])
        eop("vector", lambda v: v.memset(const_sb[:, 4:5], -192.0),
            writes=[t_const])
        dop("d_eye", eyeb[:, :], eye_d.ap(), writes=[t_eye])
        dop("d_ones", onesq_sb[:, :], onesq_d.ap(), writes=[t_ones])
        dop("d_onesc", onescol_sb[:, :], onescol_d.ap(), writes=[t_onesc])
        for t in range(8):
            dop(f"d_x{t}", x_sb[:, t * DM:(t + 1) * DM],
                x_in[t * 128:(t + 1) * 128, :], writes=[t_x[t]])

        prep_scope = ExitStack()
        wf_sb = sb(prep_scope, "wf", [128, 2 * 2048], F32)
        wc_sb = sb(prep_scope, "wc", [128, 2 * 2048], BF16)
        bout_f = sb(prep_scope, "bout_f", [1, DM], F32)
        slc_sb = sb(prep_scope, "slc", [128, 5632], BF16)
        sclA_sb = sb(prep_scope, "sclA", [128, 192], BF16)
        rcp_sb = sb(prep_scope, "rcp", [128, 192], F32)
        t_slc = Trk()
        t_sclA = Trk()
        t_rcp = Trk()
        t_acc = Trk()
        t_tl = Trk()
        t_deqA = Trk()
        t_sbm = Trk()
        t_cen = Trk()
        t_qsc = Trk()
        t_sabs = Trk()
        t_mean = Trk()
        t_arst = Trk()
        t_zpad = Trk()
        t_scld = Trk()
        t_ar = Trk()
        t_q1 = Trk()
        t_q2 = Trk()
        t_q3 = Trk()
        t_w1C = Trk()
        t_w2C = Trk()
        t_w3C = Trk()
        t_w1A = Trk()
        t_w2A = Trk()
        t_w3A = Trk()
        t_w3F = Trk()
        t_wf = [Trk(), Trk()]
        t_wc = [Trk(), Trk()]
        t_boutf = Trk()
        t_wq1C, t_wc1C, t_wk1C, t_wv1C = Trk(), Trk(), Trk(), Trk()
        t_wq1A, t_wc1A, t_wk1A, t_wv1A = Trk(), Trk(), Trk(), Trk()
        t_wg = Trk()
        t_tril = Trk()
        t_kv1C = Trk()
        t_kv1G = Trk()

        NF4_TILES = {
            "w1": (w1_sl, w1q_dram, 6, 2048, 32, [128] * 5 + [64]),
            "w2": (w2_sl, w2q_dram, 6, 2048, 32, [128] * 5 + [64]),
            "w3": (w3_sl, w3q_dram, 2, 5632, 88, [128, 128]),
        }

        dop("d_bout", bout_f[:, :], b_out_in, writes=[t_boutf])
        eop("vector", lambda v: v.tensor_copy(bout_sb[:, :], bout_f[:, :]),
            reads=[t_boutf], writes=[t_bout])

        def emit_transpose_cast(src_ap, nrows, contrib, t_contrib, tag):
            for rc in range(nrows // 128):
                f = rc % 2
                dop(f"d_wf{f}", wf_sb[:, f * 2048:(f + 1) * 2048],
                    src_ap[rc * 128:(rc + 1) * 128, :], writes=[t_wf[f]])
                c = rc % 2
                eop("vector", (lambda v, f=f, c=c: v.tensor_copy(
                    wc_sb[:, c * 2048:(c + 1) * 2048],
                    wf_sb[:, f * 2048:(f + 1) * 2048])),
                    reads=[t_wf[f]], writes=[t_wc[c]])
                for cc in range(16):
                    ts = cc % 2
                    last = (cc == 15)
                    eop("tensor", (lambda t, c=c, cc=cc, ts=ts: t.transpose(
                        ptr[ts][:, :],
                        wc_sb[:, c * 2048 + cc * 128: c * 2048 + (cc + 1) * 128],
                        eyeb[:, :])),
                        reads=[t_wc[c], t_eye], writes=[t_ptr[ts]])
                    ss = cc % 2
                    eop("scalar", (lambda s, ts=ts, ss=ss: s.copy(
                        trsp_sb[:, ss * 128:(ss + 1) * 128], ptr[ts][:, :])),
                        reads=[t_ptr[ts]], writes=[t_trsp[ss]])
                    dop(f"d_st_{tag}{ss}",
                        contrib[cc * 128:(cc + 1) * 128, rc * 128:(rc + 1) * 128],
                        trsp_sb[:, ss * 128:(ss + 1) * 128],
                        reads=[t_trsp[ss]], writes=[t_contrib])

        emit_transpose_cast(wqkv_sl, QKV_SH, qkvT_contrib.ap(), t_qkvC, "q")
        emit_transpose_cast(wout_sl, WOUT_SH, woutT_contrib.ap(), t_woutC, "o")

        def emit_cast(src_ap, contrib, cols, t_contrib, tag):
            for rc in range(2):
                f = rc % 2
                dop(f"d_wf{f}", wf_sb[:, f * 2048: f * 2048 + cols],
                    src_ap[rc * 128:(rc + 1) * 128, :], writes=[t_wf[f]])
                eop("vector", (lambda v, f=f, cols=cols: v.tensor_copy(
                    wc_sb[:, f * 2048: f * 2048 + cols],
                    wf_sb[:, f * 2048: f * 2048 + cols])),
                    reads=[t_wf[f]], writes=[t_wc[f]])
                dop(f"d_c1_{tag}", contrib[rc * 128:(rc + 1) * 128, :],
                    wc_sb[:, f * 2048: f * 2048 + cols],
                    reads=[t_wc[f]], writes=[t_contrib])

        emit_cast(wq1_sl, wq1_contrib.ap(), 2048, t_wq1C, "wq1")
        emit_cast(wc1_sl, wc1_contrib.ap(), 2048, t_wc1C, "wc1")
        emit_cast(wk1_sl, wk1_contrib.ap(), 1024, t_wk1C, "wk1")
        emit_cast(wv1_sl, wv1_contrib.ap(), 1024, t_wv1C, "wv1")
        dop("d_wg", wf_sb[:, 0:128].rearrange("p (k c) -> p k c", k=16),
            wg_full.rearrange("(k p) c -> p k c", p=128),
            reads=[t_wf[0]], writes=[t_wf[0]])
        eop("vector", lambda v: v.tensor_copy(wc_sb[:, 0:128], wf_sb[:, 0:128]),
            reads=[t_wf[0]], writes=[t_wc[0]])
        dop("d_wg2", wg_dram.ap(), wc_sb[:, 0:128],
            reads=[t_wc[0]], writes=[t_wg])

        eop("vector", lambda v: v.memset(zpad_sb[:, :], 0.0), writes=[t_zpad])
        eop("vector", lambda v: v.memset(arst_sb[:, :], 0.0), writes=[t_arst])

        absw_sb = sb(prep_scope, "absw", [128, 4], F32)
        t_absw = Trk()
        eop("vector", lambda v: v.memset(absw_sb[:, 0:2], 0.0),
            writes=[t_absw])
        for mi, (srcb, nt, cols) in enumerate(
                ((w1b_sl, 8, 2048), (w2b_sl, 16, 1024))):
            for t in range(nt):
                f = t % 2
                dop(f"d_wf{f}", wf_sb[:, f * 2048: f * 2048 + cols],
                    srcb[t * 128:(t + 1) * 128, :], writes=[t_wf[f]])
                eop("vector", (lambda v, f=f, cols=cols: v.tensor_reduce(
                    absw_sb[:, 2:3], wf_sb[:, f * 2048: f * 2048 + cols],
                    AX.X, ALU.add, apply_absolute_value=True)),
                    reads=[t_wf[f]], writes=[t_absw])
                eop("vector", (lambda v, mi=mi: v.tensor_tensor(
                    absw_sb[:, mi:mi + 1], absw_sb[:, mi:mi + 1],
                    absw_sb[:, 2:3], ALU.add)), writes=[t_absw])
        eop("gpsimd", lambda g: g.tensor_reduce(
            arst_sb[0:1, 3:5], absw_sb[0:128, 0:2], AX.C, ALU.add),
            reads=[t_absw], writes=[t_arst])


        t_slc_l = Trk()

        def nf4_passA(name, midx, t_qd):
            srcd, qd, nt, wdt, nblk, rows = NF4_TILES[name]
            eop("vector", lambda v: v.memset(sclA_sb[:, :], 0.0),
                writes=[t_sclA])
            for t in range(nt):
                r = rows[t]
                r0 = sum(rows[:t])
                dop(f"d_slc", slc_sb[0:r, 0:wdt], srcd[r0:r0 + r, :],
                    writes=[t_slc_l])
                w3v = slc_sb[0:r, 0:wdt].rearrange("p (b k) -> p b k", k=64)
                eop("vector", (lambda v, t=t, r=r, w3v=w3v, nblk=nblk:
                    v.tensor_reduce(
                        sclA_sb[0:r, t * nblk:(t + 1) * nblk], w3v, AX.X,
                        ALU.max, apply_absolute_value=True)),
                    reads=[t_slc_l], writes=[t_sclA])
                eop("vector", (lambda v, t=t, r=r, nblk=nblk: v.reciprocal(
                    rcp_sb[0:r, 0:nblk],
                    sclA_sb[0:r, t * nblk:(t + 1) * nblk])),
                    reads=[t_sclA], writes=[t_rcp])
                rb = rcp_sb[0:r, 0:nblk].rearrange(
                    "p (b o) -> p b o", o=1).broadcast_to([r, nblk, 64])
                eop("vector", (lambda v, w3v=w3v, rb=rb: v.tensor_tensor(
                    w3v, w3v, rb, ALU.mult)),
                    reads=[t_rcp], writes=[t_slc_l])
                dop(f"d_q{name}", qd[r0:r0 + r, :], slc_sb[0:r, 0:wdt],
                    reads=[t_slc_l], writes=[t_qd])
            eop("tensor", lambda te: te.matmul(
                pA[0:1, 0:192], onescol_sb[:, :], sclA_sb[:, 0:192],
                start=True, stop=True),
                reads=[t_sclA, t_onesc], writes=[t_pAB[0]])
            eop("vector", (lambda v, m=midx: v.reduce_sum(
                arst_sb[0:1, m:m + 1], pA[0:1, 0:192], AX.X)),
                reads=[t_pAB[0]], writes=[t_arst])
            for t in range(nt):
                r = rows[t]
                off = midx * 22528 + sum(rows[:t]) * nblk
                dop(f"d_scl{name}", scl_dram.ap()[off:off + r * nblk].rearrange(
                    "(p b) -> p b", p=r),
                    sclA_sb[0:r, t * nblk:(t + 1) * nblk],
                    reads=[t_sclA], writes=[t_scld])

        nf4_passA("w1", 0, t_q1)
        nf4_passA("w2", 1, t_q2)
        nf4_passA("w3", 2, t_q3)

        dop("d_arin", ar_in.ap(), arst_sb[:, :], reads=[t_arst])
        p.op("gpsimd", lambda g: g.collective_compute(
            "AllReduce", ALU.add, replica_groups=[list(range(8))],
            ins=[ar_in.ap().opt()], outs=[ar_out.ap().opt()]),
            waits=[(dma_sems["d_m2"], dma_sems["d_m2"].sched)], sem=s_cc)
        t_ar.wid = (s_cc, s_cc.sched)
        dop("d_arout", arst_sb[:, :], ar_out.ap(), reads=[t_ar], writes=[t_arst])
        prep_done = ([(s, s.sched) for s in dma_sems.values()]
                     + [(s_ve, s_ve.sched), (s_ev, s_ev.sched),
                        (s_mm, s_mm.sched)])
        mark('prep')
        prep_scope.close()

        att = ExitStack()
        qT_sb = sb(att, "qT", [128, NH0 * TOK], BF16)
        lnscr_sb = sb(att, "lnscr", [128, 2 * DM], BF16)
        kTh_sb = sb(att, "kTh", [128, SEQ], BF16)
        vh_sb = sb(att, "vh", [128, SEQ], BF16)
        e_sb = sb(att, "eqm", [128, 2 * 1024], BF16)
        sel_sb = sb(att, "sel", [64, 32 * 128], BF16)
        red_sb = sb(att, "red", [128, 8 * 64], F32)
        lngs_sb = sb(att, "lngs", [128, 512], F32)
        lng2_sb = sb(att, "lng2", [128, 8 * 64], BF16)
        lngT_sb = sb(att, "lngT", [64, 1024], BF16)
        cq_sb = sb(att, "cq", [128, 32], F32)
        nb_sb = sb(att, "nb", [128, 32], F32)
        wf2_sb = nb_sb

        def all_store_waits(tags):
            w = []
            for tg in tags:
                tg = POOL.get(tg, tg)
                if tg in dma_sems:
                    w.append((dma_sems[tg], dma_sems[tg].sched))
            return w

        agw = all_store_waits(["d_st_q0", "d_st_q1"])
        id_ag1 = p.op("gpsimd", lambda g: g.collective_compute(
            "AllGather", ALU.bypass, replica_groups=[list(range(8))],
            ins=[qkvT_contrib.ap().opt()], outs=[qkvT_arena.ap().opt()]),
            waits=agw, sem=s_cc)
        t_qkvA.wid = id_ag1
        agw = all_store_waits(["d_st_o0", "d_st_o1"])
        id_ag2 = p.op("gpsimd", lambda g: g.collective_compute(
            "AllGather", ALU.bypass, replica_groups=[list(range(8))],
            ins=[woutT_contrib.ap().opt()], outs=[woutT_arena.ap().opt()]),
            waits=agw, sem=s_cc)
        t_woutA.wid = id_ag2

        def emit_layernorm(t_x_tiles, x_tile, dst_T, t_dst, lnscr, t_lnscr,
                           first_extra):
            for t in range(8):
                eop("vector", (lambda v, t=t: v.reduce_sum(
                    stat_sb[:, t:t + 1], x_tile(t), AX.X)),
                    reads=[t_x_tiles[t]], writes=[t_stat])
                eop("scalar", (lambda s, t=t: s.activation(
                    pP1[:, :], x_tile(t)[:, 0:1024], AF.Square,
                    accum_out=stat_sb[:, 8 + t:9 + t])),
                    reads=[t_x_tiles[t]], writes=[t_pP1, t_stat])
                eop("scalar", (lambda s, t=t: s.activation(
                    pP1[:, :], x_tile(t)[:, 1024:2048], AF.Square,
                    accum_out=stat_sb[:, 16 + t:17 + t])),
                    reads=[t_x_tiles[t]], writes=[t_pP1, t_stat])
            eop("vector", lambda v: v.tensor_scalar(
                stat_sb[:, 24:32], stat_sb[:, 0:8], 1.0 / DM, None, ALU.mult),
                reads=[t_stat], writes=[t_stat])
            eop("vector", lambda v: v.tensor_tensor(
                stat_sb[:, 32:40], stat_sb[:, 24:32], stat_sb[:, 24:32],
                ALU.mult), writes=[t_stat])
            eop("vector", lambda v: v.tensor_tensor(
                stat_sb[:, 40:48], stat_sb[:, 8:16], stat_sb[:, 16:24],
                ALU.add), writes=[t_stat])
            eop("vector", lambda v: v.tensor_scalar(
                stat_sb[:, 48:56], stat_sb[:, 40:48], 1.0 / DM, None, ALU.mult),
                writes=[t_stat])
            eop("vector", lambda v: v.tensor_tensor(
                stat_sb[:, 40:48], stat_sb[:, 48:56], stat_sb[:, 32:40],
                ALU.subtract), writes=[t_stat])
            eop("scalar", lambda s: s.activation(
                stat_sb[:, 48:56], stat_sb[:, 40:48], AF.Sqrt,
                bias=const_sb[:, 0:1]),
                reads=[t_const], writes=[t_stat])
            eop("vector", lambda v: v.reciprocal(
                stat_sb[:, 56:64], stat_sb[:, 48:56]), writes=[t_stat])
            for t in range(8):
                l = t % 2
                eop("vector", (lambda v, t=t, l=l: v.tensor_scalar(
                    lnscr[:, l * DM:(l + 1) * DM], x_tile(t),
                    stat_sb[:, 24 + t:25 + t], stat_sb[:, 56 + t:57 + t],
                    ALU.subtract, ALU.mult)),
                    reads=[t_x_tiles[t], t_stat], writes=[t_lnscr[l]],
                    extra=first_extra)
                for c in range(16):
                    ts = c % 2
                    eop("tensor", (lambda te, l=l, c=c, ts=ts: te.transpose(
                        ptr[ts][:, :],
                        lnscr[:, l * DM + c * 128: l * DM + (c + 1) * 128],
                        eyeb[:, :])),
                        reads=[t_lnscr[l], t_eye], writes=[t_ptr[ts]])
                    eop("scalar", (lambda s, t=t, c=c, ts=ts: s.copy(
                        dst_T[:, c * TOK + t * 128: c * TOK + (t + 1) * 128],
                        ptr[ts][:, :])),
                        reads=[t_ptr[ts]], writes=[t_dst])

        x_tile = lambda t: x_sb[:, t * DM:(t + 1) * DM]
        emit_layernorm(t_x, x_tile, lnT_sb, t_lnT, lnscr_sb, t_lnscr,
                       prep_done)
        mark('ln0')

        def qkv_colblock_ap(c0, width):
            rank = c0 // QKV_SH
            return qkvT_arena.ap()[rank].rearrange("(k p) c -> p k c", p=128)[
                :, :, c0 - rank * QKV_SH: c0 - rank * QKV_SH + width]

        for ob in range(32):
            slot = ob % 2
            dd = wq_sb[:, slot * 4096: slot * 4096 + 2048].rearrange(
                "p (k c) -> p k c", k=16)
            dop(f"d_wq{slot}", dd, qkv_colblock_ap(ob * 128, 128),
                reads=[t_qkvA], writes=[t_wq[slot]])
            for tc in range(2):
                ps = (ob * 2 + tc) % 2
                pb = pAB[ps]
                for k in range(16):
                    eop("tensor", (lambda te, s=slot, k=k, tc=tc, pb=pb: te.matmul(
                        pb[:, :],
                        wq_sb[:, s * 4096 + k * 128: s * 4096 + (k + 1) * 128],
                        lnT_sb[:, k * TOK + tc * 512: k * TOK + (tc + 1) * 512],
                        start=(k == 0), stop=(k == 15))),
                        reads=([t_wq[slot], t_lnT] if k in (0, 15) else []),
                        writes=([t_pAB[ps]] if k in (0, 15) else []),
                        need_id=(k in (0, 15)))
                if ob < 16:
                    eop("scalar", (lambda s, ob=ob, tc=tc, pb=pb: s.copy(
                        qT_sb[:, ob * TOK + tc * 512: ob * TOK + (tc + 1) * 512],
                        pb[:, :])), reads=[t_pAB[ps]], writes=[t_qT])
                else:
                    ss = (ob * 2 + tc) % 2
                    eop("scalar", (lambda s, ss=ss, pb=pb: s.copy(
                        kvst_sb[:, ss * 512:(ss + 1) * 512], pb[:, :])),
                        reads=[t_pAB[ps]], writes=[t_phat[ss]])
                    ko = ob - 16
                    dop(f"d_kv{ss}",
                        kv0_c_k()[ko * 128:(ko + 1) * 128, tc * 512:(tc + 1) * 512],
                        kvst_sb[:, ss * 512:(ss + 1) * 512],
                        reads=[t_phat[ss]], writes=[t_kvC])

        for oc in range(8):
            o0 = 4096 + oc * 256
            slot = oc % 2
            dd = wq_sb[:, slot * 4096: slot * 4096 + 4096].rearrange(
                "p (k c) -> p k c", k=16)
            dop(f"d_wq{slot}", dd, qkv_colblock_ap(o0, 256),
                reads=[t_qkvA], writes=[t_wq[slot]])
            for tt in range(8):
                ps = tt % 2
                pb = pAB[ps]
                for k in range(16):
                    eop("tensor", (lambda te, s=slot, k=k, tt=tt, pb=pb: te.matmul(
                        pb[:, 0:256],
                        lnT_sb[:, k * TOK + tt * 128: k * TOK + (tt + 1) * 128],
                        wq_sb[:, s * 4096 + k * 256: s * 4096 + (k + 1) * 256],
                        start=(k == 0), stop=(k == 15))),
                        reads=([t_wq[slot], t_lnT] if k in (0, 15) else []),
                        writes=([t_pAB[ps]] if k in (0, 15) else []),
                        need_id=(k in (0, 15)))
                ss = tt % 2
                eop("scalar", (lambda s, ss=ss, pb=pb: s.copy(
                    kvst_sb[:, ss * 512: ss * 512 + 256], pb[:, 0:256])),
                    reads=[t_pAB[ps]], writes=[t_phat[ss]])
                dop(f"d_kv{ss}",
                    kv0_c_v()[tt * 128:(tt + 1) * 128, oc * 256:(oc + 1) * 256],
                    kvst_sb[:, ss * 512: ss * 512 + 256],
                    reads=[t_phat[ss]], writes=[t_kvC])

        mark('l0qkv')
        agw = all_store_waits(["d_kv0", "d_kv1"])
        id_agkv = p.op("gpsimd", lambda g: g.collective_compute(
            "AllGather", ALU.bypass,
            replica_groups=[[0, 1, 2, 3], [4, 5, 6, 7]],
            ins=[kv0_contrib.ap().opt()], outs=[kv0_gather.ap().opt()]),
            waits=agw, sem=s_cc)
        t_kvG.wid = id_agkv
        t_vrep = [Trk(), Trk()]
        vrep_v = vrep_dram.ap().rearrange("(hp) (j d) -> hp j d", j=32)
        for hh in range(16):
            for rr in range(4):
                vs = kv0_g_v(rr).rearrange("(j p) d -> p j d", p=128)[
                    :, :, hh * 128:(hh + 1) * 128]
                dop(f"d_vrp{hh % 2}",
                    vrep_v[hh * 128:(hh + 1) * 128, rr * 8:(rr + 1) * 8, :],
                    vs, reads=[t_kvG], writes=[t_vrep[hh % 2]])

        def nf4_ladder(name, t_qd):
            srcd, qd, nt, wdt, nblk, rows = NF4_TILES[name]
            nrows = sum(rows)
            total = nrows * wdt
            CH = 512
            nch_per_row128 = wdt // CH
            for t in range(nt):
                r = rows[t]
                r0 = sum(rows[:t])
                for ch in range(nch_per_row128):
                    idx = len(ladder_q)
                    eng = "vector" if (idx % 3) else "gpsimd"
                    slot = idx % 2

                    def chunk_emit(qd=qd, t_qd=t_qd, r=r, r0=r0, ch=ch,
                                   eng=eng, slot=slot):
                        seg = ul_sb[0:r, slot * 512:(slot + 1) * 512]
                        accv = acc2_sb[0:r, slot * 512:(slot + 1) * 512]
                        tlv = tl2_sb[0:r, slot * 512:(slot + 1) * 512]
                        dop(f"d_ul{slot}", seg,
                            qd[r0:r0 + r, ch * CH:(ch + 1) * CH],
                            reads=[t_qd], writes=[t_ul[slot]],
                            engine="gpsimd")
                        eop(eng, (lambda e, accv=accv: e.memset(accv, 0.0)),
                            writes=[t_acc2[slot]])
                        for (i, cmp_op, dl) in (
                                [(i, ALU.is_gt, NF4_ASC[i]) for i in range(7, 15)]
                                + [(i, ALU.is_le, NF4_DSC[i]) for i in range(7)]):
                            eop(eng, (lambda e, seg=seg, tlv=tlv, i=i, c=cmp_op,
                                      dl=dl: e.tensor_scalar(
                                          tlv, seg, NF4_MID[i], dl, c, ALU.mult)),
                                reads=[t_ul[slot]], writes=[t_tl2[slot]])
                            eop(eng, (lambda e, accv=accv, tlv=tlv:
                                e.tensor_tensor(accv, accv, tlv, ALU.add)),
                                reads=[t_tl2[slot]], writes=[t_acc2[slot]])
                        eop(eng, (lambda e, seg=seg, accv=accv:
                            e.tensor_copy(seg, accv)),
                            reads=[t_acc2[slot]], writes=[t_ul[slot]])
                        dop(f"d_ulo{slot}", qd[r0:r0 + r, ch * CH:(ch + 1) * CH],
                            seg, reads=[t_ul[slot]], writes=[t_qd],
                            engine="gpsimd")

                    ladder_q.append(chunk_emit)

        t_ul = [Trk(), Trk()]
        t_acc2 = [Trk(), Trk()]
        t_tl2 = [Trk(), Trk()]
        mark('kv0ag')
        ladder_q = []
        nf4_ladder("w1", t_q1)
        nf4_ladder("w2", t_q2)
        nf4_ladder("w3", t_q3)

        t_meanbf = Trk()
        t_qtl = [Trk(), Trk()]

        def nf4_deq(name, midx, t_qd, contrib, t_contrib, padmap):
            ems = []
            srcd, qd, nt, wdt, nblk, rows = NF4_TILES[name]
            def prelude():
                eop("vector", (lambda v, m=midx: v.tensor_scalar(
                    arst_sb[0:1, 8 + m:9 + m], arst_sb[0:1, m:m + 1],
                    1.0 / NF4_NBLK, None, ALU.mult)),
                    reads=[t_arst], writes=[t_arst])
                eop("vector", (lambda v, m=midx: v.tensor_copy(
                    meanbf_sb[0:1, m:m + 1], arst_sb[0:1, 8 + m:9 + m])),
                    reads=[t_arst], writes=[t_meanbf])
                eop("tensor", (lambda te, m=midx: te.matmul(
                    pA[0:88, 0:1], onesq_sb[0:1, 0:88], meanbf_sb[0:1, m:m + 1],
                    start=True, stop=True)),
                    reads=[t_meanbf, t_ones], writes=[t_pAB[0]])
                eop("vector", lambda v: v.tensor_copy(mean_sb[0:88, 0:1],
                                                      pA[0:88, 0:1]),
                    reads=[t_pAB[0]], writes=[t_mean])
                dop(f"d_sbm{name}", sbm_sb[:, :],
                    scl_dram.ap()[midx * 22528:(midx + 1) * 22528].rearrange(
                        "(p b) -> p b", p=88),
                    reads=[t_scld], writes=[t_sbm])
                eop("vector", lambda v: v.tensor_scalar(
                    cen_sb[:, :], sbm_sb[:, :], mean_sb[0:88, 0:1], None,
                    ALU.subtract), reads=[t_sbm, t_mean], writes=[t_cen])
                sab_bf = sabs_sb[0:88, 3:4].bitcast(BF16)[0:88, 0:1]
                eop("vector", (lambda v, sab_bf=sab_bf: v.tensor_reduce(
                    sab_bf, cen_sb[:, :], AX.X, ALU.max,
                    apply_absolute_value=True)),
                    reads=[t_cen], writes=[t_sabs])
                two_bf = sabs_sb[0:88, 2:3].bitcast(BF16)[0:88, 0:1]
                eop("vector", (lambda v, sab_bf=sab_bf, two_bf=two_bf:
                    v.tensor_scalar(two_bf, sab_bf, 2.0, None, ALU.mult)),
                    writes=[t_sabs])
                qf_bf = sabs_sb[0:88, 2:3].bitcast(BF16)[0:88, 1:2]
                eop("vector", (lambda v, two_bf=two_bf: v.reciprocal(
                    sabs_sb[0:88, 1:2], two_bf)), writes=[t_sabs])
                eop("vector", (lambda v, qf_bf=qf_bf: v.tensor_scalar(
                    qf_bf, sabs_sb[0:88, 1:2], 256.0, None, ALU.mult)),
                    writes=[t_sabs])
                eop("vector", (lambda v, qf_bf=qf_bf: v.tensor_copy(
                    sabs_sb[0:88, 1:2], qf_bf)), writes=[t_sabs])
                boxf = box88[:, :].bitcast(F32)
                eop("vector", lambda v: v.tensor_scalar(
                    boxf, cen_sb[:, :], sabs_sb[0:88, 1:2], MAGIC,
                    ALU.mult, ALU.add), reads=[t_cen, t_sabs], writes=[t_qsc])
                eop("vector", lambda v: v.tensor_scalar(
                    boxf, boxf, MAGIC, 127.0, ALU.subtract, ALU.min),
                    reads=[t_qsc], writes=[t_qsc])
                eop("vector", lambda v: v.tensor_scalar(
                    boxf, boxf, -128.0, None, ALU.max),
                    writes=[t_qsc])
                eop("vector", lambda v: v.reciprocal(
                    sabs_sb[0:88, 0:1], sabs_sb[0:88, 1:2]), writes=[t_sabs])
                eop("vector", lambda v: v.tensor_scalar(
                    cen_sb[:, :], boxf, sabs_sb[0:88, 0:1], None,
                    ALU.mult), reads=[t_qsc, t_sabs], writes=[t_cen])
                eop("vector", lambda v: v.tensor_scalar(
                    cen_sb[:, :], cen_sb[:, :], mean_sb[0:88, 0:1], None,
                    ALU.add), reads=[t_mean], writes=[t_cen])
                dop(f"d_deq{name}", scl_dram.ap()[midx * 22528:(midx + 1) * 22528
                                                  ].rearrange(
                    "(p b) -> p b", p=88), cen_sb[:, :],
                    reads=[t_cen], writes=[t_scld])
                for t in range(nt):
                    r = rows[t]
                    off = midx * 22528 + sum(rows[:t]) * nblk
                    dop(f"d_deqA{name}",
                        deqA_sb[0:r, t * nblk:(t + 1) * nblk],
                        scl_dram.ap()[off:off + r * nblk].rearrange(
                            "(p b) -> p b", p=r),
                        reads=[t_scld], writes=[t_deqA])
            ems.append(prelude)
            ncc = wdt // 128
            for rc in range(nt):
                r = rows[rc]
                r0 = sum(rows[:rc])
                for cc in range(ncc):
                    qs = (rc * ncc + cc) % 2

                    def unit(r=r, r0=r0, cc=cc, qs=qs, rc=rc):
                        dop(f"d_qt{qs}", qtl_sb[0:r, qs * 128:(qs + 1) * 128],
                            qd[r0:r0 + r, cc * 128:(cc + 1) * 128],
                            reads=[t_qd], writes=[t_qtl[qs]])
                        db = deqA_sb[0:r, rc * nblk + cc * 2:
                                     rc * nblk + cc * 2 + 2
                                     ].rearrange("p (b o) -> p b o", o=1
                                                 ).broadcast_to([r, 2, 64])
                        wv = qtl_sb[0:r, qs * 128:(qs + 1) * 128].rearrange(
                            "p (b k) -> p b k", k=64)
                        eop("vector", (lambda v, wv=wv, db=db: v.tensor_tensor(
                            wv, wv, db, ALU.mult)),
                            reads=[t_deqA], writes=[t_qtl[qs]])
                        ts_ = cc % 2
                        eop("tensor", (lambda te, qs=qs, ts_=ts_, r=r:
                            te.transpose(
                                ptr[ts_][:, 0:r],
                                qtl_sb[0:r, qs * 128:(qs + 1) * 128],
                                eyeb[0:r, 0:r])),
                            reads=[t_qtl[qs], t_eye], writes=[t_ptr[ts_]])
                        ss = cc % 2
                        eop("scalar", (lambda s, ts_=ts_, ss=ss, r=r: s.copy(
                            trsp_sb[:, ss * 128: ss * 128 + r],
                            ptr[ts_][:, 0:r])),
                            reads=[t_ptr[ts_]], writes=[t_trsp[ss]])
                        for (fr0, fr1, pr0) in padmap(cc):
                            dop(f"d_stw{name}{ss}",
                                contrib[pr0: pr0 + (fr1 - fr0), r0: r0 + r],
                                trsp_sb[fr0 - cc * 128: fr1 - cc * 128,
                                        ss * 128: ss * 128 + r],
                                reads=[t_trsp[ss]], writes=[t_contrib])

                    ems.append(unit)
            return ems

        def pm_id(cc):
            return [(cc * 128, (cc + 1) * 128, cc * 128)]

        def pm_w3(cc):
            f0, f1 = cc * 128, (cc + 1) * 128
            out = []
            a = f0
            while a < f1:
                seg_end = min(f1, ((a // 704) + 1) * 704)
                out.append((a, seg_end, a + 64 * (a // 704)))
                a = seg_end
            return out

        mark('nf4ladder')
        deq_lazy = [
            lambda: nf4_deq("w1", 0, t_q1, w1T_contrib.ap(), t_w1C, pm_id),
            lambda: nf4_deq("w2", 1, t_q2, w2T_contrib.ap(), t_w2C, pm_id),
            lambda: nf4_deq("w3", 2, t_q3, w3T_contrib.ap(), t_w3C, pm_w3),
        ]
        deq_qs = [None, None, None]

        def emit_pads():
            _emit_pads()

        def _emit_pads():
            for cc in range(16):
                dop("d_pad1", w1T_contrib.ap()[cc * 128:(cc + 1) * 128, 704:768],
                    zpad_sb[:, :], reads=[t_zpad], writes=[t_w1C])
                dop("d_pad2", w2T_contrib.ap()[cc * 128:(cc + 1) * 128, 704:768],
                    zpad_sb[:, :], reads=[t_zpad], writes=[t_w2C])
            for k in range(8):
                for cseg in range(4):
                    dop("d_pad3", w3T_contrib.ap()[768 * k + 704: 768 * k + 768,
                                                   cseg * 64:(cseg + 1) * 64],
                        zpad_sb[0:64, 0:64], reads=[t_zpad], writes=[t_w3C])

        def emit_wags():
            _emit_wags()

        def _emit_wags():
            for nm, contrib_t, arena_t, tC, tA in (
                    ("w1", w1T_contrib, w1T_arena, t_w1C, t_w1A),
                    ("w2", w2T_contrib, w2T_arena, t_w2C, t_w2A),
                    ("w3", w3T_contrib, w3T_arena, t_w3C, t_w3A)):
                agw = all_store_waits([f"d_stw{nm}0", f"d_stw{nm}1",
                                       "d_pad1", "d_pad2", "d_pad3"])
                ident = p.op("gpsimd", (lambda g, ci=contrib_t, ai=arena_t:
                    g.collective_compute(
                        "AllGather", ALU.bypass, replica_groups=[list(range(8))],
                        ins=[ci.ap().opt()], outs=[ai.ap().opt()])),
                    waits=agw, sem=s_cc)
                tA.wid = ident
            for r in range(8):
                dop("d_w3f", w3T_flat.ap()[:, r * 256:(r + 1) * 256],
                    w3T_arena.ap()[r], reads=[t_w3A], writes=[t_w3F])

            for nm, ci, ai, tA in (("wq1", wq1_contrib, wq1_arena, t_wq1A),
                                   ("wc1", wc1_contrib, wc1_arena, t_wc1A),
                                   ("wk1", wk1_contrib, wk1_arena, t_wk1A),
                                   ("wv1", wv1_contrib, wv1_arena, t_wv1A)):
                agw = all_store_waits([f"d_c1_{nm}"])
                ident = p.op("gpsimd", (lambda g, ci=ci, ai=ai:
                    g.collective_compute(
                        "AllGather", ALU.bypass, replica_groups=[list(range(8))],
                        ins=[ci.ap().opt()], outs=[ai.ap().opt()])),
                    waits=agw, sem=s_cc)
                tA.wid = ident

        mark('nf4deq')

        t_kTh = Trk()
        t_vh = Trk()
        t_e = [Trk(), Trk()]
        t_sel = Trk()
        t_red = Trk()
        t_lngs = Trk()
        t_lng2 = Trk()
        t_lngT = [Trk()]
        t_cq = Trk()
        t_nb = Trk()

        dop("d_sel", sel_sb[:, :], sel_d.ap(), writes=[t_sel],
            extra=prep_done)

        for h in range(NH0):
            for r in range(4):
                dop(f"d_kTh{r}", kTh_sb[:, r * TOK:(r + 1) * TOK],
                    kv0_g_k(r)[h * 128:(h + 1) * 128, :],
                    reads=[t_kvG], writes=[t_kTh])
                if r == 0:
                    dop("d_vh0", vh_sb[:, :],
                        vrep_dram.ap()[h * 128:(h + 1) * 128, :],
                        reads=[t_vrep[h % 2]], writes=[t_vh])
            kv_waits = [(dma_sems[f"d_kTh{r}"], dma_sems[f"d_kTh{r}"].sched)
                        for r in range(4)]
            kv_waits += [(dma_sems["d_vh0"], dma_sems["d_vh0"].sched)]

            for qt in range(8):
                for quar in range(4):
                    pr = quar % 2
                    t_p1 = (t_pP1 if pr == 0 else t_pAB[0])
                    for kc in range(2):
                        kk = quar * 2 + kc
                        dst = (pP1[:, kc * 512:(kc + 1) * 512] if pr == 0
                               else pAB[kc][:, :])
                        eop("tensor", (lambda te, h=h, qt=qt, kk=kk, dst=dst:
                            te.matmul(
                                dst,
                                qT_sb[:, h * TOK + qt * 128:
                                      h * TOK + (qt + 1) * 128],
                                kTh_sb[:, kk * 512:(kk + 1) * 512],
                                start=True, stop=True)),
                            reads=[t_qT, t_kTh],
                            writes=[t_p1 if pr == 0 or kc == 0 else t_pAB[1]],
                            extra=(kv_waits if (qt == 0 and quar == 0 and kc == 0)
                                   else []))
                    for kc in range(2):
                        pv = (pP1[:, kc * 512:(kc + 1) * 512] if pr == 0
                              else pAB[kc][:, :]).rearrange(
                            "p (j k) -> p j k", j=4)
                        eop("vector", (lambda v, qt=qt, q=quar, kc=kc, pv=pv:
                            v.tensor_reduce(
                                red_sb[:, qt * 64 + 32 + q * 8 + kc * 4:
                                       qt * 64 + 36 + q * 8 + kc * 4],
                                pv, AX.X, ALU.max)),
                            reads=[t_p1 if pr == 0 or kc == 0 else t_pAB[1]],
                            writes=[t_red])
                    eop("vector", (lambda v, qt=qt, q=quar: v.tensor_reduce(
                        cq_sb[:, qt * 4 + q: qt * 4 + q + 1],
                        red_sb[:, qt * 64 + 32 + q * 8: qt * 64 + 40 + q * 8],
                        AX.X, ALU.max)),
                        reads=[t_red], writes=[t_cq])
                    eop("vector", (lambda v, qt=qt, q=quar: v.tensor_scalar(
                        nb_sb[:, qt * 4 + q: qt * 4 + q + 1],
                        cq_sb[:, qt * 4 + q: qt * 4 + q + 1],
                        -SCALE0, None, ALU.mult)),
                        reads=[t_cq], writes=[t_nb])
                    es = quar % 2
                    for kc in range(2):
                        pv2 = (pP1[:, kc * 512:(kc + 1) * 512] if pr == 0
                               else pAB[kc][:, :])
                        eop("scalar", (lambda s, qt=qt, q=quar, es=es, kc=kc,
                                       pv2=pv2: s.activation(
                            e_sb[:, es * 1024 + kc * 512:
                                 es * 1024 + (kc + 1) * 512],
                            pv2, AF.Exp, scale=SCALE0,
                            bias=nb_sb[:, qt * 4 + q: qt * 4 + q + 1])),
                            reads=[(t_p1 if pr == 0 or kc == 0 else t_pAB[1]),
                                   t_nb],
                            writes=[t_e[es]])
                    e3 = e_sb[:, es * 1024:(es + 1) * 1024].rearrange(
                        "p (j k) -> p j k", j=8)
                    eop("vector", (lambda v, qt=qt, q=quar, e3=e3: v.reduce_sum(
                        red_sb[:, qt * 64 + q * 8: qt * 64 + q * 8 + 8],
                        e3, AX.X)),
                        reads=[t_e[es]], writes=[t_red])
            r3 = lambda o: red_sb[:, :].rearrange("p (qt c) -> p qt c", qt=8)[
                :, :, o:o + 32]
            l3 = lambda a, b: lngs_sb[:, a:b].rearrange("p (qt c) -> p qt c", qt=8)
            cq4 = cq_sb[:, :].rearrange("p (qt q) -> p qt q", qt=8)
            cqb = cq_sb[:, :].rearrange("p (qt q o) -> p qt q o", qt=8, o=1
                                        ).broadcast_to([128, 8, 4, 8])
            eop("vector", lambda v: v.tensor_scalar(
                l3(0, 256), r3(0), 1e-38, None, ALU.max),
                reads=[t_red], writes=[t_lngs])
            eop("scalar", lambda s: s.activation(
                lngs_sb[:, 0:256], lngs_sb[:, 0:256], AF.Ln),
                reads=[t_lngs], writes=[t_lngs])
            eop("vector", lambda v: v.scalar_tensor_tensor(
                l3(256, 512), r3(32), -SCALE0, l3(0, 256),
                ALU.mult, ALU.add),
                reads=[t_red, t_lngs], writes=[t_lngs])
            l4 = lambda a, b: lngs_sb[:, a:b].rearrange(
                "p (qt q j) -> p qt q j", qt=8, q=4)
            eop("vector", lambda v: v.scalar_tensor_tensor(
                l4(256, 512), cqb, SCALE0, l4(256, 512),
                ALU.mult, ALU.add),
                reads=[t_cq, t_lngs], writes=[t_lngs])
            eop("vector", lambda v: v.tensor_reduce(
                stat_sb[:, 0:8], cq4, AX.X, ALU.max),
                reads=[t_cq], writes=[t_stat])
            eop("vector", lambda v: v.tensor_scalar(
                stat_sb[:, 0:8], stat_sb[:, 0:8], SCALE0, None, ALU.mult),
                writes=[t_stat])
            cmax_b = stat_sb[:, 0:8].rearrange("p (qt o) -> p qt o", o=1
                                               ).broadcast_to([128, 8, 4])
            eop("vector", lambda v: v.scalar_tensor_tensor(
                wf2_sb[:, :].rearrange("p (qt q) -> p qt q", qt=8),
                cq4, SCALE0, cmax_b, ALU.mult, ALU.subtract),
                reads=[t_cq, t_stat], writes=[t_nb])
            eop("scalar", lambda s: s.activation(
                wf2_sb[:, :], wf2_sb[:, :], AF.Exp),
                reads=[t_nb], writes=[t_nb])
            wfb = wf2_sb[:, :].rearrange("p (qt q o) -> p qt q o", qt=8, o=1
                                         ).broadcast_to([128, 8, 4, 8])
            r4m = red_sb[:, :].rearrange("p (qt c) -> p qt c", qt=8)[
                :, :, 0:32].rearrange("p qt (q j) -> p qt q j", q=4)
            sc4 = lambda a, b: lngs_sb[:, a:b].rearrange(
                "p (qt q j) -> p qt q j", qt=8, q=4)
            eop("vector", lambda v: v.tensor_tensor(
                sc4(0, 256), r4m, wfb, ALU.mult),
                reads=[t_red, t_nb], writes=[t_lngs])
            eop("vector", lambda v: v.reduce_sum(
                stat_sb[:, 8:16], l3(0, 256), AX.X),
                reads=[t_lngs], writes=[t_stat])
            eop("scalar", lambda s: s.activation(
                stat_sb[:, 8:16], stat_sb[:, 8:16], AF.Ln),
                reads=[t_stat], writes=[t_stat])
            eop("vector", lambda v: v.tensor_tensor(
                stat_sb[:, 8:16], stat_sb[:, 8:16], stat_sb[:, 0:8], ALU.add),
                writes=[t_stat])
            lnL_b = stat_sb[:, 8:16].rearrange("p (qt o) -> p qt o", o=1
                                               ).broadcast_to([128, 8, 32])
            eop("vector", lambda v: v.tensor_tensor(
                l3(256, 512), l3(256, 512), lnL_b, ALU.subtract),
                reads=[t_stat], writes=[t_lngs])
            hi_v = lng2_sb[:, :].rearrange("p (qt j two) -> p qt j two",
                                           qt=8, two=2)
            eop("vector", lambda v: v.tensor_scalar(
                hi_v[:, :, :, 0:1].squeeze(3), l3(256, 512),
                1.0 / SCALE0, None, ALU.mult),
                reads=[t_lngs], writes=[t_lng2])
            eop("vector", lambda v: v.tensor_scalar(
                lngs_sb[:, 0:256], lngs_sb[:, 256:512], 1.0 / SCALE0, None,
                ALU.mult), reads=[t_lngs], writes=[t_lngs])
            eop("vector", lambda v: v.tensor_tensor(
                hi_v[:, :, :, 1:2].squeeze(3), l3(0, 256),
                hi_v[:, :, :, 0:1].squeeze(3), ALU.subtract),
                reads=[t_lngs, t_lng2], writes=[t_lng2])

            ls = 0
            for qt in range(8):
                ts = qt % 2
                eop("tensor", (lambda te, qt=qt, ts=ts: te.transpose(
                    ptr[ts][0:64, 0:128],
                    lng2_sb[:, qt * 64:(qt + 1) * 64], eyeb[:, :])),
                    reads=[t_lng2, t_eye], writes=[t_ptr[ts]])
                eop("scalar", (lambda s, qt=qt, ts=ts, ls=ls: s.copy(
                    lngT_sb[0:64, qt * 128: (qt + 1) * 128],
                    ptr[ts][0:64, 0:128])),
                    reads=[t_ptr[ts]], writes=[t_lngT[ls]])

            for qc in range(2):
                def pv0(j):
                    ph = j % 2
                    eop("tensor", (lambda te, j=j, ph=ph: te.matmul(
                        pattn[:, :],
                        vh_sb[:, j * 128:(j + 1) * 128],
                        phat_sb[:, ph * 512:(ph + 1) * 512],
                        start=(j == 0), stop=(j == 31))),
                        reads=[t_vh, t_phat[ph]],
                        writes=([t_pattn] if j in (0, 31) else []))
                for j in range(32):
                    ps = j % 2
                    pb = pAB[ps]
                    eop("tensor", (lambda te, h=h, j=j, qc=qc, pb=pb: te.matmul(
                        pb[:, :],
                        kTh_sb[:, j * 128:(j + 1) * 128],
                        qT_sb[:, h * TOK + qc * 512: h * TOK + (qc + 1) * 512],
                        start=True, stop=False)),
                        reads=[t_kTh, t_qT], writes=[t_pAB[ps]])
                    eop("tensor", (lambda te, j=j, qc=qc, ls=ls, pb=pb: te.matmul(
                        pb[:, :], sel_sb[:, j * 128:(j + 1) * 128],
                        lngT_sb[0:64, qc * 512: (qc + 1) * 512],
                        start=False, stop=True)),
                        reads=[t_sel, t_lngT[ls]], writes=[t_pAB[ps]])
                    ph = j % 2
                    eop("scalar", (lambda s, ph=ph, pb=pb: s.activation(
                        phat_sb[:, ph * 512:(ph + 1) * 512], pb[:, :],
                        AF.Exp, scale=SCALE0)),
                        reads=[t_pAB[ps]], writes=[t_phat[ph]])
                    if j > 0:
                        pv0(j - 1)
                pv0(31)
                eop("scalar", (lambda s, h=h, qc=qc: s.copy(
                    lnT_sb[:, h * TOK + qc * 512: h * TOK + (qc + 1) * 512],
                    pattn[:, :])), reads=[t_pattn], writes=[t_lnT])
            mark(f'h{h}')
            if h < 12:
                n_per = (len(ladder_q) + 11) // 12
                for em in ladder_q[h * n_per:(h + 1) * n_per]:
                    em()
            if 6 <= h <= 10:
                if deq_qs[0] is None:
                    deq_qs[0] = deq_lazy[0]()
                q = deq_qs[0]
                n_per = (len(q) + 4) // 5
                for em in q[(h - 6) * n_per:(h - 5) * n_per]:
                    em()
            if 11 <= h <= 12:
                if deq_qs[1] is None:
                    deq_qs[1] = deq_lazy[1]()
                q = deq_qs[1]
                n_per = (len(q) + 1) // 2
                for em in q[(h - 11) * n_per:(h - 10) * n_per]:
                    em()
            if 13 <= h <= 14:
                if deq_qs[2] is None:
                    deq_qs[2] = deq_lazy[2]()
                q = deq_qs[2]
                n_per = (len(q) + 1) // 2
                for em in q[(h - 13) * n_per:(h - 12) * n_per]:
                    em()
            if h == 14:
                emit_pads()
                emit_wags()
        mark('l0attn')
        att_done = [(s_mm, s_mm.sched), (s_ev, s_ev.sched),
                    (s_ve, s_ve.sched)]
        att.close()

        for oc in range(8):
            slot = oc % 2
            rank = (oc * 256) // WOUT_SH
            a = woutT_arena.ap()[rank].rearrange("(k p) c -> p k c", p=128)
            dd = wq_sb[:, slot * 4096: slot * 4096 + 4096].rearrange(
                "p (k c) -> p k c", k=16)
            dop(f"d_wq{slot}", dd, a, reads=[t_woutA], writes=[t_wq[slot]])
            for qt in range(8):
                ps = qt % 2
                pb = pAB[ps]
                for k in range(16):
                    eop("tensor", (lambda te, s=slot, k=k, qt=qt, pb=pb: te.matmul(
                        pb[:, 0:256],
                        lnT_sb[:, k * TOK + qt * 128: k * TOK + (qt + 1) * 128],
                        wq_sb[:, s * 4096 + k * 256: s * 4096 + (k + 1) * 256],
                        start=(k == 0), stop=False)),
                        reads=([t_wq[slot], t_lnT] if k == 0 else []),
                        writes=([t_pAB[ps]] if k == 0 else []), need_id=(k == 0))
                eop("tensor", (lambda te, oc=oc, pb=pb: te.matmul(
                    pb[:, 0:256], onesq_sb[:, :],
                    bout_sb[0:1, oc * 256:(oc + 1) * 256],
                    start=False, stop=True)),
                    reads=[t_ones, t_bout, t_lnT, t_wq[slot]],
                    writes=[t_pAB[ps]])
                eop("vector", (lambda v, qt=qt, oc=oc, pb=pb: v.tensor_tensor(
                    x_sb[:, qt * DM + oc * 256: qt * DM + (oc + 1) * 256],
                    x_sb[:, qt * DM + oc * 256: qt * DM + (oc + 1) * 256],
                    pb[:, 0:256], ALU.add)),
                    reads=[t_pAB[ps]], writes=[t_x[qt]])

        mark('l0outproj')
        ffn = ExitStack()
        h_sb = sb(ffn, "h", [128, 48 * 512], BF16)
        lnscr2_sb = sb(ffn, "lnscr2", [128, 2 * DM], BF16)
        w3r_sb = sb(ffn, "w3r", [128, 2 * 512], BF16)
        t_lnscr2 = [Trk(), Trk()]
        t_h = [Trk() for _ in range(48)]
        t_w3r = [Trk(), Trk()]

        emit_layernorm(t_x, x_tile, lnT_sb, t_lnT, lnscr2_sb, t_lnscr2,
                       att_done)

        def w1_colblock_ap(arena, fc):
            rank, c0 = fc // 6, (fc % 6) * 128
            return arena.ap()[rank].rearrange("(k p) c -> p k c", p=128)[
                :, :, c0:c0 + 128]

        accs = [(pA[:, :], t_pAB[0]), (pB[:, :], t_pAB[1]),
                (pP1[:, 0:512], t_pP1), (pP1[:, 512:1024], t_pP1)]

        for th in range(2):
            up_ps = [(pA[:, :], t_pAB[0]), (pB[:, :], t_pAB[1]),
                     (pP1[:, 0:512], t_pP1), (pP1[:, 512:1024], t_pP1)]

            def evac_fc(fc):
                pa, ta = up_ps[(fc % 2) * 2]
                pb_, tb = up_ps[(fc % 2) * 2 + 1]
                ph = fc % 2
                eop("scalar", (lambda s, ph=ph, pa=pa: s.activation(
                    phat_sb[:, ph * 512:(ph + 1) * 512], pa, AF.Silu)),
                    reads=[ta], writes=[t_phat[ph]])
                eop("vector", (lambda v, fc=fc, ph=ph, pb_=pb_: v.tensor_tensor(
                    h_sb[:, fc * 512:(fc + 1) * 512],
                    phat_sb[:, ph * 512:(ph + 1) * 512], pb_, ALU.mult)),
                    reads=[t_phat[ph], tb], writes=[t_h[fc]])
            for fc in range(48):
                slot = fc % 2
                d1 = wq_sb[:, slot * 4096: slot * 4096 + 2048].rearrange(
                    "p (k c) -> p k c", k=16)
                d2 = wq_sb[:, slot * 4096 + 2048: slot * 4096 + 4096].rearrange(
                    "p (k c) -> p k c", k=16)
                dop(f"d_wq{slot}", d1, w1_colblock_ap(w1T_arena, fc),
                    reads=[t_w1A], writes=[t_wq[slot]])
                dop(f"d_wq{slot}", d2, w1_colblock_ap(w2T_arena, fc),
                    reads=[t_w2A], writes=[t_wq[slot]])
                for ab in range(2):
                    pb, tp = up_ps[(fc % 2) * 2 + ab]
                    co = ab * 2048
                    for k in range(16):
                        eop("tensor", (lambda te, s=slot, k=k, th=th, pb=pb,
                                       co=co: te.matmul(
                            pb,
                            wq_sb[:, s * 4096 + co + k * 128:
                                  s * 4096 + co + (k + 1) * 128],
                            lnT_sb[:, k * TOK + th * 512:
                                   k * TOK + th * 512 + 512],
                            start=(k == 0), stop=(k == 15))),
                            reads=([t_wq[slot], t_lnT] if k in (0, 15) else []),
                            writes=([tp] if k in (0, 15) else []),
                            need_id=(k in (0, 15)))
                if fc > 0:
                    evac_fc(fc - 1)
            evac_fc(47)
            for dc in range(4):
                for fc in range(48):
                    ws = fc % 2
                    dop(f"d_w3r{ws}", w3r_sb[:, ws * 512:(ws + 1) * 512],
                        w3T_flat.ap()[fc * 128:(fc + 1) * 128,
                                      dc * 512:(dc + 1) * 512],
                        reads=[t_w3F], writes=[t_w3r[ws]], extra=att_done)
                    for tb in range(4):
                        pb, tp = accs[tb]
                        eop("tensor", (lambda te, fc=fc, tb=tb, ws=ws, pb=pb:
                            te.matmul(
                                pb,
                                h_sb[:, fc * 512 + tb * 128:
                                     fc * 512 + (tb + 1) * 128],
                                w3r_sb[:, ws * 512:(ws + 1) * 512],
                                start=(fc == 0), stop=(fc == 47))),
                            reads=[t_h[fc], t_w3r[ws]],
                            writes=([tp] if fc in (0, 47) else []))
                for tb in range(4):
                    pb, tp = accs[tb]
                    tg = th * 4 + tb
                    eop("vector", (lambda v, tg=tg, dc=dc, pb=pb: v.tensor_tensor(
                        x_sb[:, tg * DM + dc * 512: tg * DM + (dc + 1) * 512],
                        x_sb[:, tg * DM + dc * 512: tg * DM + (dc + 1) * 512],
                        pb, ALU.add)),
                        reads=[tp], writes=[t_x[tg]])
        ffn.close()



        mark('l0ffn')
        l1w_pre = [(s_mm, s_mm.sched), (s_ev, s_ev.sched), (s_ve, s_ve.sched)]
        l1w = ExitStack()
        wfq_sb = sb(l1w, "wfq", [128, 2 * 2048], F32)
        qz_sb = sb(l1w, "qz", [128, 2 * 2048], BF16)
        boxw_sb = sb(l1w, "boxw", [128, 2048], I32)
        scol_sb = sb(l1w, "scol", [128, 4], F32)
        t_wfq = [Trk(), Trk()]
        t_qz = [Trk(), Trk()]
        t_boxw = Trk()
        t_scol = Trk()
        t_w1eC = Trk()
        t_w2eC = Trk()
        t_w1eA = Trk()
        t_w2eA = Trk()

        for mi in range(2):
            eop("vector", (lambda v, mi=mi: v.tensor_scalar(
                arst_sb[0:1, 5 + 2 * mi:6 + 2 * mi],
                arst_sb[0:1, 3 + mi:4 + mi],
                1.0 / 16777216.0, 1e-5, ALU.mult, ALU.max)),
                reads=[t_arst], writes=[t_arst], extra=l1w_pre)
            eop("vector", (lambda v, mi=mi: v.reciprocal(
                arst_sb[0:1, 6 + 2 * mi:7 + 2 * mi],
                arst_sb[0:1, 5 + 2 * mi:6 + 2 * mi])), writes=[t_arst])
            hv = arst_sb[0:1, 9 + mi:10 + mi].bitcast(BF16)
            eop("vector", (lambda v, mi=mi, hv=hv: v.tensor_copy(
                hv[0:1, 0:1], arst_sb[0:1, 6 + 2 * mi:7 + 2 * mi])),
                writes=[t_arst])
            eop("vector", (lambda v, mi=mi, hv=hv: v.tensor_tensor(
                hv[0:1, 1:2], arst_sb[0:1, 6 + 2 * mi:7 + 2 * mi],
                hv[0:1, 0:1], ALU.subtract)), writes=[t_arst])
            eop("tensor", (lambda te, hv=hv: te.matmul(
                pA[:, 0:1], onesq_sb[:, :], hv[0:1, 0:1],
                start=True, stop=False)),
                reads=[t_arst, t_ones], writes=[t_pAB[0]], extra=l1w_pre)
            eop("tensor", (lambda te, hv=hv: te.matmul(
                pA[:, 0:1], onesq_sb[:, :], hv[0:1, 1:2],
                start=False, stop=True)),
                reads=[t_arst, t_ones], writes=[t_pAB[0]])
            eop("vector", (lambda v, mi=mi: v.tensor_copy(
                scol_sb[:, mi:mi + 1], pA[:, 0:1])),
                reads=[t_pAB[0]], writes=[t_scol], extra=l1w_pre)

        boxw_f = boxw_sb[:, :].bitcast(F32)

        def tern_quant(srcb, nt, cols, contrib, t_contrib, scx):
            for t in range(nt):
                f = t % 2
                dop(f"d_qq", wfq_sb[:, f * 2048: f * 2048 + cols],
                    srcb[t * 128:(t + 1) * 128, :], writes=[t_wfq[f]],
                    extra=l1w_pre)
                bx = boxw_f[:, 0:cols]
                eop("vector", (lambda v, f=f, bx=bx, cols=cols: v.tensor_scalar(
                    bx, wfq_sb[:, f * 2048: f * 2048 + cols],
                    scol_sb[:, scx:scx + 1], MAGIC, ALU.mult, ALU.add)),
                    reads=[t_wfq[f], t_scol], writes=[t_boxw], extra=l1w_pre)
                eop("vector", (lambda v, bx=bx: v.tensor_scalar(
                    bx, bx, MAGIC, 1.0, ALU.subtract, ALU.min)),
                    writes=[t_boxw])
                eop("vector", (lambda v, f=f, bx=bx, cols=cols: v.tensor_scalar(
                    qz_sb[:, f * 2048: f * 2048 + cols], bx,
                    -1.0, None, ALU.max)),
                    reads=[t_boxw], writes=[t_qz[f]])
                for cc in range(cols // 128):
                    ts_ = cc % 2
                    eop("tensor", (lambda te, f=f, cc=cc, ts_=ts_: te.transpose(
                        ptr[ts_][:, :],
                        qz_sb[:, f * 2048 + cc * 128: f * 2048 + (cc + 1) * 128],
                        eyeb[:, :])),
                        reads=[t_qz[f], t_eye], writes=[t_ptr[ts_]])
                    ss = cc % 2
                    eop("scalar", (lambda s, ts_=ts_, ss=ss: s.copy(
                        trsp_sb[:, ss * 128:(ss + 1) * 128], ptr[ts_][:, :])),
                        reads=[t_ptr[ts_]], writes=[t_trsp[ss]])
                    dop(f"d_stww1{ss}",
                        contrib[cc * 128:(cc + 1) * 128, t * 128:(t + 1) * 128],
                        trsp_sb[:, ss * 128:(ss + 1) * 128],
                        reads=[t_trsp[ss]], writes=[t_contrib])

        tern_quant(w1b_sl, 8, 2048, w1e_contrib.ap(), t_w1eC, 0)
        tern_quant(w2b_sl, 16, 1024, w2e_contrib.ap(), t_w2eC, 1)
        l1w.close()

        for nm, ci, ai, tA in (("w1e", w1e_contrib, w1e_arena, t_w1eA),
                               ("w2e", w2e_contrib, w2e_arena, t_w2eA)):
            agw = all_store_waits(["d_sw0", "d_sw1"])
            ident = p.op("gpsimd", (lambda g, ci=ci, ai=ai:
                g.collective_compute(
                    "AllGather", ALU.bypass, replica_groups=[list(range(8))],
                    ins=[ci.ap().opt()], outs=[ai.ap().opt()])),
                waits=agw, sem=s_cc)
            tA.wid = ident

        mark('l1wprep')
        l1pre = [(s_mm, s_mm.sched), (s_ev, s_ev.sched), (s_ve, s_ve.sched)]
        att1 = ExitStack()
        qT1_sb = sb(att1, "qT1", [128, 16 * TOK], BF16)
        kT1_sb = sb(att1, "kT1", [128, SEQ], BF16)
        v1_sb = sb(att1, "v1", [128, SEQ], BF16)
        msqF_sb = sb(att1, "msqF", [128, 2 * 256], F32)
        sqtB_sb = sb(att1, "sqtB", [128, 256], F32)
        mskB_sb = sb(att1, "mskB", [128, 256], F32)
        ptT_sb = sb(att1, "ptT", [128, 256], F32)
        ptm_sb = sb(att1, "ptm", [128, 64], F32)
        eps_sb = sb(att1, "eps1", [128, 64], F32)
        sdia_sb = sb(att1, "sdia", [128, 64], F32)
        gloc_sb = sb(att1, "gloc", [8, 1024], F32)
        tot_sb = sb(att1, "tot", [8, 16], F32)
        wg1_sb = sb(att1, "wg1", [128, 128], BF16)
        tril_sb = sb(att1, "tril", [128, 128], BF16)
        cb_sb = sb(att1, "cb", [1, 256], F32)
        cbb_sb = sb(att1, "cbb", [1, 256], BF16)
        scat_sb = sb(att1, "scat", [128, 16], F32)
        scatb_sb = sb(att1, "scatb", [128, 16], BF16)
        pden = psum(att1, "pden", [1, 512])

        t_qT1 = Trk()
        t_kT1 = Trk()
        t_v1 = Trk()
        t_msqF = Trk()
        t_sqtB = Trk()
        t_mskB = Trk()
        t_ptT = Trk()
        t_ptm = Trk()
        t_eps1 = Trk()
        t_sdia = Trk()
        t_gloc = Trk()
        t_tot = Trk()
        t_den1 = [Trk(), Trk()]
        t_wg1 = Trk()
        t_tril = Trk()
        t_cb = Trk()
        t_cbb = Trk()
        t_scat = Trk()
        t_pden = Trk()
        t_pgd = Trk()
        t_dend = [Trk(), Trk()]
        t_kdia = Trk()
        t_vdia = Trk()
        t_yt = Trk()
        t_ytm = Trk()
        t_rcol = Trk()
        t_cold = [Trk(), Trk()]

        eop("vector", lambda v: v.memset(const_sb[:, 1:2], -GOFF),
            writes=[t_const])
        eop("vector", lambda v: v.memset(const_sb[:, 2:3], 1.0),
            writes=[t_const])
        eop("vector", lambda v: v.memset(const_sb[:, 3:4], LNC1),
            writes=[t_const])

        dop("d_wgl", wg1_sb[:, :], wg_dram.ap(), reads=[t_wg],
            writes=[t_wg1], extra=l1pre)
        dop("d_tril", tril_sb[:, :], tril_d.ap(), writes=[t_tril],
            extra=l1pre)
        dop("d_caus", cb_sb[:, :], caus_in, writes=[t_cb], extra=l1pre)
        eop("vector", lambda v: v.tensor_copy(cbb_sb[:, :], cb_sb[:, :]),
            reads=[t_cb], writes=[t_cbb])
        dop("d_bc1", scat_sb[:, :],
            bc_in.rearrange("o (c p) -> p (o c)", p=128),
            writes=[t_scat], extra=l1pre)
        eop("vector", lambda v: v.tensor_copy(scatb_sb[:, :], scat_sb[:, :]),
            reads=[t_scat], writes=[t_scat])
        dop("d_bc2", wg_dram.ap()[0:16, 0:128].rearrange("c p -> p c"),
            scatb_sb[:, :], reads=[t_scat], writes=[t_scat])
        dop("d_bc3", bout_sb[:, :],
            wg_dram.ap()[0:16, 0:128].rearrange("(o c) p -> o (c p)", o=1),
            reads=[t_scat], writes=[t_bout], extra=[(s_mm, s_mm.sched)])

        s1 = ExitStack()
        lnscr1_sb = sb(s1, "lnscr1", [128, 2 * DM], BF16)
        t_lnscr1 = [Trk(), Trk()]
        emit_layernorm(t_x, x_tile, lnT_sb, t_lnT, lnscr1_sb, t_lnscr1, l1pre)
        s1.close()

        def arena_colblock(arena_t, c0, width):
            return arena_t.ap().rearrange("(k p) c -> p k c", p=128)[
                :, :, c0:c0 + width]

        for ob in range(16):
            slot = ob % 2
            dd = wq_sb[:, slot * 4096: slot * 4096 + 2048].rearrange(
                "p (k c) -> p k c", k=16)
            dop(f"d_wq{slot}", dd, arena_colblock(wq1_arena, ob * 128, 128),
                reads=[t_wq1A], writes=[t_wq[slot]])
            for tc in range(2):
                ps = tc % 2
                pb = pAB[ps]
                for k in range(16):
                    eop("tensor", (lambda te, s=slot, k=k, tc=tc, pb=pb: te.matmul(
                        pb[:, :],
                        wq_sb[:, s * 4096 + k * 128: s * 4096 + (k + 1) * 128],
                        lnT_sb[:, k * TOK + tc * 512: k * TOK + (tc + 1) * 512],
                        start=(k == 0), stop=(k == 15))),
                        reads=([t_wq[slot], t_lnT] if k in (0, 15) else []),
                        writes=([t_pAB[ps]] if k in (0, 15) else []),
                        need_id=(k in (0, 15)))
                eop("scalar", (lambda s, ob=ob, tc=tc, pb=pb: s.copy(
                    qT1_sb[:, ob * TOK + tc * 512: ob * TOK + (tc + 1) * 512],
                    pb[:, :])), reads=[t_pAB[ps]], writes=[t_qT1])

        for ob in range(8):
            slot = ob % 2
            dd = wq_sb[:, slot * 4096: slot * 4096 + 2048].rearrange(
                "p (k c) -> p k c", k=16)
            dop(f"d_wq{slot}", dd, arena_colblock(wk1_arena, ob * 128, 128),
                reads=[t_wk1A], writes=[t_wq[slot]])
            for tc in range(2):
                ps = tc % 2
                pb = pAB[ps]
                for k in range(16):
                    eop("tensor", (lambda te, s=slot, k=k, tc=tc, pb=pb: te.matmul(
                        pb[:, :],
                        wq_sb[:, s * 4096 + k * 128: s * 4096 + (k + 1) * 128],
                        lnT_sb[:, k * TOK + tc * 512: k * TOK + (tc + 1) * 512],
                        start=(k == 0), stop=(k == 15))),
                        reads=([t_wq[slot], t_lnT] if k in (0, 15) else []),
                        writes=([t_pAB[ps]] if k in (0, 15) else []),
                        need_id=(k in (0, 15)))
                ss = tc % 2
                eop("scalar", (lambda s, ss=ss, pb=pb: s.copy(
                    kvst_sb[:, ss * 512:(ss + 1) * 512], pb[:, :])),
                    reads=[t_pAB[ps]], writes=[t_phat[ss]])
                dop(f"d_k1c{ss}",
                    kv1_c_k()[ob * 128:(ob + 1) * 128, tc * 512:(tc + 1) * 512],
                    kvst_sb[:, ss * 512:(ss + 1) * 512],
                    reads=[t_phat[ss]], writes=[t_kv1C])

        for oc in range(4):
            o0 = oc * 256
            slot = oc % 2
            dd = wq_sb[:, slot * 4096: slot * 4096 + 4096].rearrange(
                "p (k c) -> p k c", k=16)
            dop(f"d_wq{slot}", dd, arena_colblock(wv1_arena, o0, 256),
                reads=[t_wv1A], writes=[t_wq[slot]])
            for tt in range(8):
                ps = tt % 2
                pb = pAB[ps]
                for k in range(16):
                    eop("tensor", (lambda te, s=slot, k=k, tt=tt, pb=pb: te.matmul(
                        pb[:, 0:256],
                        lnT_sb[:, k * TOK + tt * 128: k * TOK + (tt + 1) * 128],
                        wq_sb[:, s * 4096 + k * 256: s * 4096 + (k + 1) * 256],
                        start=(k == 0), stop=(k == 15))),
                        reads=([t_wq[slot], t_lnT] if k in (0, 15) else []),
                        writes=([t_pAB[ps]] if k in (0, 15) else []),
                        need_id=(k in (0, 15)))
                ss = tt % 2
                eop("scalar", (lambda s, ss=ss, pb=pb: s.copy(
                    kvst_sb[:, ss * 512: ss * 512 + 256], pb[:, 0:256])),
                    reads=[t_pAB[ps]], writes=[t_phat[ss]])
                dop(f"d_v1c{ss}",
                    kv1_c_v()[tt * 128:(tt + 1) * 128, oc * 256:(oc + 1) * 256],
                    kvst_sb[:, ss * 512: ss * 512 + 256],
                    reads=[t_phat[ss]], writes=[t_kv1C])

        for tc in range(2):
            ps = tc % 2
            pb = pAB[ps]
            for k in range(16):
                eop("tensor", (lambda te, k=k, tc=tc, pb=pb: te.matmul(
                    pb[0:8, 0:512],
                    wg1_sb[:, k * 8:(k + 1) * 8],
                    lnT_sb[:, k * TOK + tc * 512: k * TOK + (tc + 1) * 512],
                    start=(k == 0), stop=(k == 15))),
                    reads=([t_wg1, t_lnT] if k in (0, 15) else []),
                    writes=([t_pAB[ps]] if k in (0, 15) else []),
                    need_id=(k in (0, 15)))
            eop("vector", (lambda v, tc=tc, pb=pb: v.tensor_copy(
                gloc_sb[0:8, tc * 512:(tc + 1) * 512], pb[0:8, 0:512])),
                reads=[t_pAB[ps]], writes=[t_gloc])
        dop("d_g1c", kv1_c_g(), gloc_sb[:, :].bitcast(BF16),
            reads=[t_gloc], writes=[t_kv1C])

        agw = all_store_waits(["d_k1c0", "d_k1c1", "d_v1c0", "d_v1c1", "d_g1c"])
        id_agkv1 = p.op("gpsimd", lambda g: g.collective_compute(
            "AllGather", ALU.bypass,
            replica_groups=[[0, 1, 2, 3], [4, 5, 6, 7]],
            ins=[kv1_contrib.ap().opt()], outs=[kv1_gather.ap().opt()]),
            waits=agw, sem=s_cc)
        t_kv1G.wid = id_agkv1
        t_vrep1 = [Trk(), Trk()]
        vrep1_v = vrep1_dram.ap().rearrange("(hp) (j d) -> hp j d", j=32)
        for hh in range(8):
            for rr in range(4):
                vs = kv1_g_v(rr).rearrange("(j p) d -> p j d", p=128)[
                    :, :, hh * 128:(hh + 1) * 128]
                dop(f"d_vrp{hh % 2}",
                    vrep1_v[hh * 128:(hh + 1) * 128, rr * 8:(rr + 1) * 8, :],
                    vs, reads=[t_kv1G], writes=[t_vrep1[hh % 2]])

        mark('l1proj')
        s2 = ExitStack()
        gch_sb = sb(s2, "gch", [8, 1024], F32)
        pg_sb = sb(s2, "pgs", [8, 1024], F32)
        hi_sb = pg_sb[:, :].bitcast(BF16)[0:8, 0:1024]
        lo_sb = pg_sb[:, :].bitcast(BF16)[0:8, 1024:2048]
        mr_sb = sb(s2, "mr", [8, 4], F32)
        t_gch = Trk()
        t_pg = Trk()
        t_hi = t_pg
        t_lo = t_pg
        t_mr = Trk()
        eop("vector", lambda v: v.memset(tot_sb[:, 4:5], 0.0), writes=[t_tot],
            extra=l1pre)
        dop("d_mr", mr_sb[:, :], maskR_in, writes=[t_mr], extra=l1pre)

        def gate_chunk(load_fn, dst_sb, t_dst, base_j):
            load_fn()
            eop("scalar", lambda s: s.activation(
                pg_sb[0:8, :], gch_sb[0:8, :], AF.Exp, scale=-1.0,
                bias=const_sb[0:8, 1:2]),
                reads=[t_gch, t_const], writes=[t_pg])
            eop("scalar", lambda s: s.activation(
                pg_sb[0:8, :], pg_sb[0:8, :], AF.Ln,
                bias=const_sb[0:8, 2:3]),
                reads=[t_const], writes=[t_pg])
            eop("vector", lambda v: v.tensor_tensor_scan(
                gch_sb[0:8, :], pg_sb[0:8, :], pg_sb[0:8, :], 0.0,
                ALU.add, ALU.bypass),
                reads=[t_pg], writes=[t_gch])
            eop("vector", lambda v: v.tensor_scalar(
                gch_sb[0:8, :], gch_sb[0:8, :], tot_sb[0:8, 4:5], None,
                ALU.add), reads=[t_tot], writes=[t_gch])
            eop("vector", lambda v: v.tensor_copy(hi_sb, gch_sb[0:8, :]),
                reads=[t_gch], writes=[t_hi])
            eop("vector", lambda v: v.tensor_tensor(
                lo_sb, gch_sb[0:8, :], hi_sb, ALU.subtract),
                reads=[t_gch, t_hi], writes=[t_lo])
            for jl in range(8):
                eop("tensor", (lambda te, jl=jl: te.transpose(
                    ptrA[0:128, 0:8], hi_sb[0:8, jl * 128:(jl + 1) * 128],
                    eyeb[0:8, 0:8])),
                    reads=[t_hi, t_eye], writes=[t_ptr[0]])
                eop("scalar", lambda s: s.copy(trsp_sb[:, 0:8],
                                               ptrA[0:128, 0:8]),
                    reads=[t_ptr[0]], writes=[t_trsp[0]])
                eop("tensor", (lambda te, jl=jl: te.transpose(
                    ptrB[0:128, 0:8], lo_sb[0:8, jl * 128:(jl + 1) * 128],
                    eyeb[0:8, 0:8])),
                    reads=[t_lo, t_eye], writes=[t_ptr[1]])
                eop("vector", (lambda v, jl=jl: v.tensor_tensor(
                    dst_sb[:, (base_j + jl) * 8:(base_j + jl + 1) * 8],
                    trsp_sb[:, 0:8], ptrB[0:128, 0:8], ALU.add)),
                    reads=[t_trsp[0], t_ptr[1]], writes=[t_dst])

        for rr in range(4):
            def load_rr(rr=rr):
                dop("d_g1l", gch_sb[:, :].bitcast(BF16), kv1_g_g(rr),
                    reads=[t_kv1G], writes=[t_gch])
            gate_chunk(load_rr, ptT_sb, t_ptT, rr * 8)
            eop("vector", (lambda v, rr=rr: v.tensor_copy(
                tot_sb[0:8, rr:rr + 1], gch_sb[0:8, 1023:1024])),
                reads=[t_gch], writes=[t_tot])
            eop("vector", (lambda v, rr=rr: v.tensor_copy(
                tot_sb[0:8, 4:5], tot_sb[0:8, rr:rr + 1])),
                writes=[t_tot])

        eop("vector", lambda v: v.tensor_tensor(
            tot_sb[0:8, 8:12], tot_sb[0:8, 0:4], mr_sb[0:8, :], ALU.mult),
            reads=[t_mr], writes=[t_tot])
        eop("vector", lambda v: v.reduce_sum(
            tot_sb[0:8, 5:6], tot_sb[0:8, 8:12], AX.X), writes=[t_tot])
        eop("vector", lambda v: v.tensor_copy(gch_sb[0:8, :], gloc_sb[0:8, :]),
            reads=[t_gloc], writes=[t_gch])
        eop("scalar", lambda s: s.activation(
            pg_sb[0:8, :], gch_sb[0:8, :], AF.Exp, scale=-1.0,
            bias=const_sb[0:8, 1:2]),
            reads=[t_gch, t_const], writes=[t_pg])
        eop("scalar", lambda s: s.activation(
            pg_sb[0:8, :], pg_sb[0:8, :], AF.Ln, bias=const_sb[0:8, 2:3]),
            reads=[t_const], writes=[t_pg])
        eop("vector", lambda v: v.tensor_tensor_scan(
            gch_sb[0:8, :], pg_sb[0:8, :], pg_sb[0:8, :], 0.0,
            ALU.add, ALU.bypass),
            reads=[t_pg], writes=[t_gch])
        eop("vector", lambda v: v.tensor_scalar(
            gch_sb[0:8, :], gch_sb[0:8, :], tot_sb[0:8, 5:6], None, ALU.add),
            reads=[t_tot], writes=[t_gch])
        eop("vector", lambda v: v.tensor_copy(hi_sb, gch_sb[0:8, :]),
            reads=[t_gch], writes=[t_hi])
        eop("vector", lambda v: v.tensor_tensor(
            lo_sb, gch_sb[0:8, :], hi_sb, ALU.subtract),
            reads=[t_gch, t_hi], writes=[t_lo])
        for jl in range(8):
            eop("tensor", (lambda te, jl=jl: te.transpose(
                ptrA[0:128, 0:8], hi_sb[0:8, jl * 128:(jl + 1) * 128],
                eyeb[0:8, 0:8])),
                reads=[t_hi, t_eye], writes=[t_ptr[0]])
            eop("scalar", lambda s: s.copy(trsp_sb[:, 0:8], ptrA[0:128, 0:8]),
                reads=[t_ptr[0]], writes=[t_trsp[0]])
            eop("tensor", (lambda te, jl=jl: te.transpose(
                ptrB[0:128, 0:8], lo_sb[0:8, jl * 128:(jl + 1) * 128],
                eyeb[0:8, 0:8])),
                reads=[t_lo, t_eye], writes=[t_ptr[1]])
            eop("vector", (lambda v, jl=jl: v.tensor_tensor(
                ptm_sb[:, jl * 8:(jl + 1) * 8],
                trsp_sb[:, 0:8], ptrB[0:128, 0:8], ALU.add)),
                reads=[t_trsp[0], t_ptr[1]], writes=[t_ptm])
        s2.close()

        eop("scalar", lambda s: s.activation(
            eps_sb[:, :], ptm_sb[:, :], AF.Exp, bias=const_sb[:, 3:4]),
            reads=[t_ptm, t_const], writes=[t_eps1])
        eop("scalar", lambda s: s.activation(
            sdia_sb[:, :], ptm_sb[:, :], AF.Exp, scale=0.5),
            reads=[t_ptm], writes=[t_sdia])
        eop("scalar", lambda s: s.activation(
            sqtB_sb[:, :], ptT_sb[:, :], AF.Exp, scale=0.5),
            reads=[t_ptT], writes=[t_sqtB])
        eop("tensor", lambda te: te.matmul(
            pA[:, 0:256], onesq_sb[:, :], cbb_sb[:, :], start=True, stop=True),
            reads=[t_cbb, t_ones], writes=[t_pAB[0]])
        eop("vector", lambda v: v.tensor_copy(mskB_sb[:, :], pA[:, 0:256]),
            reads=[t_pAB[0]], writes=[t_mskB])
        for qc in range(2):
            mv = mskB_sb[:, qc * 32:(qc + 1) * 32].rearrange(
                "p (j o) -> p j o", o=1).broadcast_to([128, 32, 8])
            eop("vector", (lambda v, qc=qc, mv=mv: v.tensor_tensor(
                msqF_sb[:, qc * 256:(qc + 1) * 256].rearrange(
                    "p (j h) -> p j h", j=32),
                sqtB_sb[:, :].rearrange("p (j h) -> p j h", j=32),
                mv, ALU.mult)),
                reads=[t_sqtB, t_mskB], writes=[t_msqF])

        mark('l1gate')
        den_sb = sb(att1, "den1", [1, 2 * 512], F32)
        kdia_sb = sb(att1, "kdia", [128, 1024], BF16)
        vdia_sb = sb(att1, "vdia", [128, 1024], BF16)
        yt_sb = sb(att1, "yt", [128, 512], BF16)
        ytm_sb = sb(att1, "ytm", [128, 128], BF16)
        rcol_sb = sb(att1, "rcol", [128, 8], F32)
        cold_sb = sb(att1, "cold", [128, 2 * 4], F32)
        g_done = [(s_ve, s_ve.sched), (s_ev, s_ev.sched), (s_mm, s_mm.sched)]

        for h in range(8):
            for r in range(4):
                dop(f"d_kT1{r}", kT1_sb[:, r * TOK:(r + 1) * TOK],
                    kv1_g_k(r)[h * 128:(h + 1) * 128, :],
                    reads=[t_kv1G], writes=[t_kT1],
                    extra=(g_done if h == 0 else []))
                if r == 0:
                    dop("d_v10", v1_sb[:, :],
                        vrep1_dram.ap()[h * 128:(h + 1) * 128, :],
                        reads=[t_vrep1[h % 2]], writes=[t_v1],
                        extra=(g_done if h == 0 else []))
            dop("d_kd1", kdia_sb[:, :], kv1_c_k()[h * 128:(h + 1) * 128, :],
                reads=[t_kv1C], writes=[t_kdia],
                extra=(g_done if h == 0 else []))
            vdd = vdia_sb[:, :].rearrange("q (j d) -> q j d", j=8)
            vds = kv1_c_v().rearrange("(j q) d -> q j d", q=128)[
                :, :, h * 128:(h + 1) * 128]
            dop("d_vd1", vdd, vds, reads=[t_kv1C], writes=[t_vdia],
                extra=(g_done if h == 0 else []))
            kv_waits = [(dma_sems[f"d_kT1{r}"], dma_sems[f"d_kT1{r}"].sched)
                        for r in range(4)]
            kv_waits += [(dma_sems["d_v10"], dma_sems["d_v10"].sched)]
            kv_waits += [(dma_sems["d_kd1"], dma_sems["d_kd1"].sched),
                         (dma_sems["d_vd1"], dma_sems["d_vd1"].sched)]

            for hqh in range(2):
                hq = 2 * h + hqh
                for qc in range(2):
                    def pv1(j):
                        sl = j % 2
                        eop("tensor", (lambda te, j=j, sl=sl: te.matmul(
                            pattn[:, :],
                            v1_sb[:, j * 128:(j + 1) * 128],
                            phat_sb[:, sl * 512:(sl + 1) * 512],
                            start=(j == 0), stop=False)),
                            reads=([t_v1, t_phat[sl]]),
                            writes=([t_pattn] if j == 0 else []))
                        eop("tensor", (lambda te, j=j, sl=sl: te.matmul(
                            pden[0:1, 0:512],
                            onescol_sb[:, :],
                            phat_sb[:, sl * 512:(sl + 1) * 512],
                            start=(j == 0), stop=False)),
                            reads=([t_onesc, t_phat[sl]]),
                            writes=([t_pden] if j == 0 else []))
                    for j in range(28):
                        ps = j % 2
                        pb = pAB[ps]
                        col = qc * 256 + j * 8 + h
                        eop("tensor", (lambda te, hq=hq, qc=qc, j=j, pb=pb:
                            te.matmul(
                                pb[:, :],
                                kT1_sb[:, j * 128:(j + 1) * 128],
                                qT1_sb[:, hq * TOK + qc * 512:
                                       hq * TOK + (qc + 1) * 512],
                                start=True, stop=True)),
                            reads=([t_kT1, t_qT1] if j in (0, 27) else []),
                            writes=[t_pAB[ps]],
                            extra=(kv_waits if (hqh == 0 and qc == 0 and j == 0)
                                   else []))
                        sl = j % 2
                        eop("scalar", (lambda s, sl=sl, col=col, pb=pb:
                            s.activation(
                                phat_sb[:, sl * 512:(sl + 1) * 512],
                                pb[:, :],
                                AF.Square, scale=msqF_sb[:, col:col + 1])),
                            reads=[t_pAB[ps], t_msqF], writes=[t_phat[sl]])
                        if j > 0:
                            pv1(j - 1)
                    pv1(27)
                    for qtl in range(4):
                        qt = qc * 4 + qtl
                        for w in range(qtl + 1):
                            lb = qc * 4 + w
                            ps = w % 2
                            pb = pAB[ps]
                            eop("tensor", (lambda te, hq=hq, qt=qt, lb=lb,
                                           pb=pb: te.matmul(
                                pb[:, 0:128],
                                kdia_sb[:, lb * 128:(lb + 1) * 128],
                                qT1_sb[:, hq * TOK + qt * 128:
                                       hq * TOK + (qt + 1) * 128],
                                start=True, stop=True)),
                                reads=[t_kdia, t_qT1], writes=[t_pAB[ps]])
                            sl = w % 2
                            dslice = phat_sb[:, sl * 512: sl * 512 + 128]
                            eop("scalar", (lambda s, pb=pb, lb=lb, h=h,
                                           dslice=dslice: s.activation(
                                dslice, pb[:, 0:128], AF.Square,
                                scale=sdia_sb[:, lb * 8 + h: lb * 8 + h + 1])),
                                reads=[t_pAB[ps], t_sdia], writes=[t_phat[sl]])
                            if w == qtl:
                                eop("vector", (lambda v, dslice=dslice:
                                    v.tensor_tensor(
                                        dslice, dslice, tril_sb[:, :],
                                        ALU.mult)),
                                    reads=[t_tril], writes=[t_phat[sl]])
                            eop("tensor", (lambda te, lb=lb, qtl=qtl, w=w,
                                           dslice=dslice: te.matmul(
                                pattn[:, qtl * 128:(qtl + 1) * 128],
                                vdia_sb[:, lb * 128:(lb + 1) * 128],
                                dslice, start=False, stop=(w == qtl))),
                                reads=[t_vdia, t_phat[sl]], writes=[t_pattn])
                            eop("tensor", (lambda te, qtl=qtl, w=w,
                                           dslice=dslice: te.matmul(
                                pden[0:1, qtl * 128:(qtl + 1) * 128],
                                onescol_sb[:, :], dslice,
                                start=False, stop=(w == qtl))),
                                reads=[t_onesc, t_phat[sl]], writes=[t_pden])
                    dsl = (hq * 2 + qc) % 2
                    eop("vector", (lambda v, dsl=dsl: v.tensor_copy(
                        den_sb[0:1, dsl * 512:(dsl + 1) * 512], pden[0:1, :])),
                        reads=[t_pden], writes=[t_den1[dsl]])
                    dop(f"d_den{dsl}",
                        pg_dram.ap()[dsl:dsl + 1, 0:512],
                        den_sb[0:1, dsl * 512:(dsl + 1) * 512],
                        reads=[t_den1[dsl]], writes=[t_dend[dsl]])
                    dop(f"d_cold{dsl}", cold_sb[:, dsl * 4:(dsl + 1) * 4],
                        pg_dram.ap()[dsl:dsl + 1, 0:512].rearrange(
                            "o (q p) -> p (o q)", p=128),
                        reads=[t_dend[dsl]], writes=[t_cold[dsl]])
                    for qtl in range(4):
                        qt = qc * 4 + qtl
                        eop("vector", (lambda v, qtl=qtl, qt=qt, h=h, dsl=dsl:
                            v.tensor_tensor(
                                rcol_sb[:, qtl:qtl + 1],
                                cold_sb[:, dsl * 4 + qtl:dsl * 4 + qtl + 1],
                                eps_sb[:, qt * 8 + h: qt * 8 + h + 1],
                                ALU.add)),
                            reads=[t_cold[dsl], t_eps1], writes=[t_rcol])
                        eop("vector", (lambda v, qtl=qtl: v.reciprocal(
                            rcol_sb[:, qtl:qtl + 1],
                            rcol_sb[:, qtl:qtl + 1])),
                            writes=[t_rcol])
                    eop("scalar", lambda s: s.copy(yt_sb[:, :], pattn[:, :]),
                        reads=[t_pattn], writes=[t_yt])
                    for qtl in range(4):
                        ts = qtl % 2
                        eop("tensor", (lambda te, qtl=qtl, ts=ts: te.transpose(
                            ptr[ts][:, :], yt_sb[:, qtl * 128:(qtl + 1) * 128],
                            eyeb[:, :])),
                            reads=[t_yt, t_eye], writes=[t_ptr[ts]])
                        eop("scalar", (lambda s, qtl=qtl, ts=ts: s.activation(
                            ytm_sb[:, :], ptr[ts][:, :], AF.Copy,
                            scale=rcol_sb[:, qtl:qtl + 1])),
                            reads=[t_ptr[ts], t_rcol], writes=[t_ytm])
                        eop("tensor", (lambda te, ts=ts: te.transpose(
                            ptr[ts][:, :], ytm_sb[:, :], eyeb[:, :])),
                            reads=[t_ytm, t_eye], writes=[t_ptr[ts]])
                        eop("scalar", (lambda s, hq=hq, qc=qc, qtl=qtl, ts=ts:
                            s.copy(
                                lnT_sb[:, hq * TOK + qc * 512 + qtl * 128:
                                       hq * TOK + qc * 512 + (qtl + 1) * 128],
                                ptr[ts][:, :])),
                            reads=[t_ptr[ts]], writes=[t_lnT])
        mark('l1attn')
        att1.close()

        for oc in range(8):
            slot = oc % 2
            dd = wq_sb[:, slot * 4096: slot * 4096 + 4096].rearrange(
                "p (k c) -> p k c", k=16)
            dop(f"d_wq{slot}", dd, arena_colblock(wc1_arena, oc * 256, 256),
                reads=[t_wc1A], writes=[t_wq[slot]])
            for qt in range(8):
                ps = qt % 2
                pb = pAB[ps]
                for k in range(16):
                    eop("tensor", (lambda te, s=slot, k=k, qt=qt, pb=pb: te.matmul(
                        pb[:, 0:256],
                        lnT_sb[:, k * TOK + qt * 128: k * TOK + (qt + 1) * 128],
                        wq_sb[:, s * 4096 + k * 256: s * 4096 + (k + 1) * 256],
                        start=(k == 0), stop=False)),
                        reads=([t_wq[slot], t_lnT] if k == 0 else []),
                        writes=([t_pAB[ps]] if k == 0 else []), need_id=(k == 0))
                eop("tensor", (lambda te, oc=oc, pb=pb: te.matmul(
                    pb[:, 0:256], onesq_sb[:, :],
                    bout_sb[0:1, oc * 256:(oc + 1) * 256],
                    start=False, stop=True)),
                    reads=[t_ones, t_bout, t_lnT, t_wq[slot]],
                    writes=[t_pAB[ps]])
                eop("vector", (lambda v, qt=qt, oc=oc, pb=pb: v.tensor_tensor(
                    x_sb[:, qt * DM + oc * 256: qt * DM + (oc + 1) * 256],
                    x_sb[:, qt * DM + oc * 256: qt * DM + (oc + 1) * 256],
                    pb[:, 0:256], ALU.add)),
                    reads=[t_pAB[ps]], writes=[t_x[qt]])


        mark('l1outproj')
        l1f_pre = [(s_mm, s_mm.sched), (s_ev, s_ev.sched), (s_ve, s_ve.sched)]
        l1f = ExitStack()
        c12_sb = sb(l1f, "c12", [128, 24], F32)
        mcol_sb = sb(l1f, "mcol", [128, 2], F32)
        w2r_sb = sb(l1f, "w2r", [128, 2 * 1024], BF16)
        evt_sb = sb(l1f, "evt", [128, 512], F32)
        t_c12 = Trk()
        t_mcol = Trk()
        t_w2r = [Trk(), Trk()]
        t_evt = [Trk(), Trk()]

        for mi in range(2):
            hv = arst_sb[0:1, 11 + mi:12 + mi].bitcast(BF16)
            eop("vector", (lambda v, mi=mi, hv=hv: v.tensor_copy(
                hv[0:1, 0:1], arst_sb[0:1, 5 + 2 * mi:6 + 2 * mi])),
                reads=[t_arst], writes=[t_arst], extra=l1f_pre)
            eop("vector", (lambda v, mi=mi, hv=hv: v.tensor_tensor(
                hv[0:1, 1:2], arst_sb[0:1, 5 + 2 * mi:6 + 2 * mi],
                hv[0:1, 0:1], ALU.subtract)), writes=[t_arst])
            eop("tensor", (lambda te, hv=hv: te.matmul(
                pA[:, 0:1], onesq_sb[:, :], hv[0:1, 0:1],
                start=True, stop=False)),
                reads=[t_arst, t_ones], writes=[t_pAB[0]], extra=l1f_pre)
            eop("tensor", (lambda te, hv=hv: te.matmul(
                pA[:, 0:1], onesq_sb[:, :], hv[0:1, 1:2],
                start=False, stop=True)),
                reads=[t_arst, t_ones], writes=[t_pAB[0]])
            eop("vector", (lambda v, mi=mi: v.tensor_copy(
                mcol_sb[:, mi:mi + 1], pA[:, 0:1])),
                reads=[t_pAB[0]], writes=[t_mcol])

        ph1 = ExitStack()
        ln3_sb = sb(ph1, "ln3", [128, 2 * 2048], F32)
        boxq_sb = sb(ph1, "boxq", [128, 2048], I32)
        k1b_sb = sb(ph1, "k1b", [128, 2048], BF16)
        t_ln3 = [Trk(), Trk()]
        t_boxq = Trk()
        t_k1b = Trk()
        boxq_f = boxq_sb[:, :].bitcast(F32)

        for t in range(8):
            eop("vector", (lambda v, t=t: v.reduce_sum(
                stat_sb[:, t:t + 1], x_tile(t), AX.X)),
                reads=[t_x[t]], writes=[t_stat], extra=l1f_pre)
            eop("scalar", (lambda s, t=t: s.activation(
                pP1[:, :], x_tile(t)[:, 0:1024], AF.Square,
                accum_out=stat_sb[:, 8 + t:9 + t])),
                reads=[t_x[t]], writes=[t_pP1, t_stat], extra=l1f_pre)
            eop("scalar", (lambda s, t=t: s.activation(
                pP1[:, :], x_tile(t)[:, 1024:2048], AF.Square,
                accum_out=stat_sb[:, 16 + t:17 + t])),
                reads=[t_x[t]], writes=[t_pP1, t_stat])
        eop("vector", lambda v: v.tensor_scalar(
            stat_sb[:, 24:32], stat_sb[:, 0:8], 1.0 / DM, None, ALU.mult),
            reads=[t_stat], writes=[t_stat])
        eop("vector", lambda v: v.tensor_tensor(
            stat_sb[:, 32:40], stat_sb[:, 24:32], stat_sb[:, 24:32],
            ALU.mult), writes=[t_stat])
        eop("vector", lambda v: v.tensor_tensor(
            stat_sb[:, 40:48], stat_sb[:, 8:16], stat_sb[:, 16:24],
            ALU.add), writes=[t_stat])
        eop("vector", lambda v: v.tensor_scalar(
            stat_sb[:, 48:56], stat_sb[:, 40:48], 1.0 / DM, None, ALU.mult),
            writes=[t_stat])
        eop("vector", lambda v: v.tensor_tensor(
            stat_sb[:, 40:48], stat_sb[:, 48:56], stat_sb[:, 32:40],
            ALU.subtract), writes=[t_stat])
        eop("scalar", lambda s: s.activation(
            stat_sb[:, 48:56], stat_sb[:, 40:48], AF.Sqrt,
            bias=const_sb[:, 0:1]),
            reads=[t_const], writes=[t_stat])
        eop("vector", lambda v: v.reciprocal(
            stat_sb[:, 56:64], stat_sb[:, 48:56]), writes=[t_stat])
        for t in range(8):
            l = t % 2
            eop("vector", (lambda v, t=t, l=l: v.tensor_scalar(
                ln3_sb[:, l * DM:(l + 1) * DM], x_tile(t),
                stat_sb[:, 24 + t:25 + t], stat_sb[:, 56 + t:57 + t],
                ALU.subtract, ALU.mult)),
                reads=[t_x[t], t_stat], writes=[t_ln3[l]], extra=l1f_pre)
            eop("vector", (lambda v, t=t, l=l: v.tensor_reduce(
                c12_sb[:, 16:17], ln3_sb[:, l * DM:(l + 1) * DM], AX.X,
                ALU.max, apply_absolute_value=True)),
                reads=[t_ln3[l]], writes=[t_c12])
            eop("vector", lambda v: v.tensor_scalar(
                c12_sb[:, 16:17], c12_sb[:, 16:17], 1e-5, None, ALU.max),
                writes=[t_c12])
            eop("vector", lambda v: v.reciprocal(
                c12_sb[:, 17:18], c12_sb[:, 16:17]), writes=[t_c12])
            eop("vector", (lambda v, t=t: v.tensor_scalar(
                c12_sb[:, 18:19], c12_sb[:, 17:18], 127.0, None, ALU.mult)),
                writes=[t_c12])
            eop("vector", (lambda v, t=t: v.tensor_scalar(
                c12_sb[:, t:t + 1], c12_sb[:, 16:17], mcol_sb[:, 0:1],
                1.0 / 127.0, ALU.mult, ALU.mult)),
                reads=[t_mcol], writes=[t_c12])
            eop("vector", (lambda v, l=l: v.tensor_scalar(
                boxq_f, ln3_sb[:, l * DM:(l + 1) * DM], c12_sb[:, 18:19],
                MAGIC, ALU.mult, ALU.add)),
                reads=[t_ln3[l], t_c12], writes=[t_boxq])
            eop("vector", lambda v: v.tensor_scalar(
                k1b_sb[:, :], boxq_f, MAGIC, None, ALU.subtract),
                reads=[t_boxq], writes=[t_k1b])
            for c in range(16):
                ts_ = c % 2
                eop("tensor", (lambda te, c=c, ts_=ts_: te.transpose(
                    ptr[ts_][:, :], k1b_sb[:, c * 128:(c + 1) * 128],
                    eyeb[:, :])),
                    reads=[t_k1b, t_eye], writes=[t_ptr[ts_]])
                eop("scalar", (lambda s, t=t, c=c, ts_=ts_: s.copy(
                    lnT_sb[:, c * TOK + t * 128: c * TOK + (t + 1) * 128],
                    ptr[ts_][:, :])),
                    reads=[t_ptr[ts_]], writes=[t_lnT],
                    extra=(l1f_pre if (t == 0 and c == 0) else []))
        ph1.close()

        ph2 = ExitStack()
        h1_sb = sb(ph2, "h1", [128, 2 * 8192], BF16)
        k2T_sb = sb(ph2, "k2T", [128, 2 * 8192], BF16)
        k2b_sb = sb(ph2, "k2b", [128, 512], BF16)
        box2_sb = sb(ph2, "box2", [128, 512], I32)
        t_h1 = [Trk(), Trk()]
        t_k2T = [Trk(), Trk()]
        t_k2b = [Trk(), Trk()]
        t_box2 = Trk()
        box2_f = box2_sb[:, :].bitcast(F32)

        def w1e_colblock(q):
            rank, c0 = q // 4, (q % 4) * 256
            return w1e_arena.ap()[rank].rearrange("(k p) c -> p k c", p=128)[
                :, :, c0:c0 + 256]

        accs2 = [(pA[:, :], t_pAB[0]), (pB[:, :], t_pAB[1]),
                 (pP1[:, 0:512], t_pP1), (pP1[:, 512:1024], t_pP1)]

        for g in range(4):
            up_ps1 = [(pA[:, 0:256], t_pAB[0]), (pB[:, 0:256], t_pAB[1]),
                      (pP1[:, 0:256], t_pP1), (pP1[:, 512:768], t_pP1)]

            def evac_q(q, g=g):
                for tl in range(2):
                    t = 2 * g + tl
                    pb, tp = up_ps1[(q % 2) * 2 + tl]
                    eop("scalar", (lambda s, q=q, t=t, tl=tl, pb=pb:
                        s.activation(
                            h1_sb[:, tl * 8192 + q * 256:
                                  tl * 8192 + (q + 1) * 256], pb,
                            AF.Silu, scale=c12_sb[:, t:t + 1])),
                        reads=[tp, t_c12], writes=[t_h1[tl]],
                        extra=(l1f_pre if (g == 0 and tl == 0 and q == 0)
                               else []))
            for q in range(32):
                slot = q % 2
                dd = wq_sb[:, slot * 4096: slot * 4096 + 4096].rearrange(
                    "p (k c) -> p k c", k=16)
                dop(f"d_wq{slot}", dd, w1e_colblock(q),
                    reads=[t_w1eA], writes=[t_wq[slot]])
                for tl in range(2):
                    t = 2 * g + tl
                    pb, tp = up_ps1[(q % 2) * 2 + tl]
                    for k in range(16):
                        eop("tensor", (lambda te, s=slot, k=k, t=t, pb=pb:
                            te.matmul(
                                pb,
                                lnT_sb[:, k * TOK + t * 128:
                                       k * TOK + (t + 1) * 128],
                                wq_sb[:, s * 4096 + k * 256:
                                      s * 4096 + (k + 1) * 256],
                                start=(k == 0), stop=(k == 15))),
                            reads=([t_wq[slot], t_lnT] if k in (0, 15) else []),
                            writes=([tp] if k in (0, 15) else []),
                            need_id=(k in (0, 15)))
                if q > 0:
                    evac_q(q - 1)
            evac_q(31)
            for tl in range(2):
                t = 2 * g + tl
                eop("vector", (lambda v, tl=tl: v.tensor_reduce(
                    c12_sb[:, 19:20], h1_sb[:, tl * 8192:(tl + 1) * 8192],
                    AX.X, ALU.max,
                    apply_absolute_value=True)),
                    reads=[t_h1[tl]], writes=[t_c12])
                eop("vector", lambda v: v.tensor_scalar(
                    c12_sb[:, 19:20], c12_sb[:, 19:20], 1e-5, None, ALU.max),
                    writes=[t_c12])
                eop("vector", lambda v: v.reciprocal(
                    c12_sb[:, 20:21], c12_sb[:, 19:20]), writes=[t_c12])
                eop("vector", lambda v: v.tensor_scalar(
                    c12_sb[:, 21:22], c12_sb[:, 20:21], 127.0, None, ALU.mult),
                    writes=[t_c12])
                eop("vector", (lambda v, t=t: v.tensor_scalar(
                    c12_sb[:, 8 + t:9 + t], c12_sb[:, 19:20], mcol_sb[:, 1:2],
                    1.0 / 127.0, ALU.mult, ALU.mult)),
                    reads=[t_mcol], writes=[t_c12])
                for ch in range(16):
                    eop("vector", (lambda v, ch=ch, tl=tl: v.tensor_scalar(
                        box2_f, h1_sb[:, tl * 8192 + ch * 512:
                                      tl * 8192 + (ch + 1) * 512],
                        c12_sb[:, 21:22], MAGIC, ALU.mult, ALU.add)),
                        reads=[t_h1[tl], t_c12], writes=[t_box2])
                    kslot = 0
                    eop("vector", lambda v: v.tensor_scalar(
                        k2b_sb[:, 0:512], box2_f,
                        MAGIC, None, ALU.subtract),
                        reads=[t_box2], writes=[t_k2b[0]])
                    for c4 in range(4):
                        fcc = ch * 4 + c4
                        ts_ = c4 % 2
                        eop("tensor", (lambda te, c4=c4, ts_=ts_:
                            te.transpose(
                                ptr[ts_][:, :],
                                k2b_sb[:, c4 * 128:(c4 + 1) * 128],
                                eyeb[:, :])),
                            reads=[t_k2b[0], t_eye], writes=[t_ptr[ts_]])
                        eop("scalar", (lambda s, tl=tl, fcc=fcc, ts_=ts_:
                            s.copy(
                                k2T_sb[:, tl * 8192 + fcc * 128:
                                       tl * 8192 + (fcc + 1) * 128],
                                ptr[ts_][:, :])),
                            reads=[t_ptr[ts_]], writes=[t_k2T[tl]])
            for dcp in range(2):
                for fc in range(64):
                    ws = fc % 2
                    dop(f"d_w2r{ws}", w2r_sb[:, ws * 1024:(ws + 1) * 1024],
                        w2e_arena.ap()[fc * 128:(fc + 1) * 128,
                                       dcp * 1024:(dcp + 1) * 1024],
                        reads=[t_w2eA], writes=[t_w2r[ws]], extra=l1f_pre)
                    for tl in range(2):
                        for dc2 in range(2):
                            pb, tp = accs2[tl * 2 + dc2]
                            eop("tensor", (lambda te, fc=fc, tl=tl, dc2=dc2,
                                           ws=ws, pb=pb: te.matmul(
                                pb,
                                k2T_sb[:, tl * 8192 + fc * 128:
                                       tl * 8192 + (fc + 1) * 128],
                                w2r_sb[:, ws * 1024 + dc2 * 512:
                                       ws * 1024 + (dc2 + 1) * 512],
                                start=(fc == 0), stop=(fc == 63))),
                                reads=[t_k2T[tl], t_w2r[ws]],
                                writes=([tp] if fc in (0, 63) else []))
                for tl in range(2):
                    t = 2 * g + tl
                    for dc2 in range(2):
                        pb, tp = accs2[tl * 2 + dc2]
                        eop("scalar", (lambda s, pb=pb, t=t:
                            s.activation(
                                evt_sb[:, :], pb,
                                AF.Copy, scale=c12_sb[:, 8 + t:9 + t])),
                            reads=[tp, t_c12], writes=[t_evt[0]])
                        dcg = dcp * 1024 + dc2 * 512
                        eop("vector", (lambda v, t=t, dcg=dcg:
                            v.tensor_tensor(
                                x_sb[:, t * DM + dcg: t * DM + dcg + 512],
                                x_sb[:, t * DM + dcg: t * DM + dcg + 512],
                                evt_sb[:, :], ALU.add)),
                            reads=[t_evt[0]], writes=[t_x[t]])
        ph2.close()
        l1f.close()

        mark('l1ffn')
        fin_ids = []
        for t in range(8):
            fin_ids.append(dop("d_out", out_x[t * 128:(t + 1) * 128, :],
                               x_sb[:, t * DM:(t + 1) * DM], reads=[t_x[t]]))
        p.op("sync", lambda e: e.nop(),
             waits=[(dma_sems["d_out"], dma_sems["d_out"].sched)])

        p.emit(nc, ctx)
    return nc


def shard_inputs(inputs):
    x = np.ascontiguousarray(np.asarray(inputs["L0_attn_x"]))
    wqkv = np.ascontiguousarray(np.asarray(inputs["L0_attn_w_qkv"]))
    wout = np.ascontiguousarray(np.asarray(inputs["L0_attn_w_out"]))
    bout = np.ascontiguousarray(np.asarray(inputs["L0_attn_b_out"]).reshape(1, DM))
    w1 = np.asarray(inputs["L0_ffn_weight1"])
    w2 = np.asarray(inputs["L0_ffn_weight2"])
    w3 = np.asarray(inputs["L0_ffn_weight3"])
    wq1 = np.ascontiguousarray(np.asarray(inputs["L1_attn_Wq"]))
    wc1 = np.ascontiguousarray(np.asarray(inputs["L1_attn_Wc"]))
    wk1 = np.ascontiguousarray(np.asarray(inputs["L1_attn_Wk"]))
    wv1 = np.ascontiguousarray(np.asarray(inputs["L1_attn_Wv"]))
    wg1 = np.ascontiguousarray(np.asarray(inputs["L1_attn_Wg"]))
    bc1 = np.ascontiguousarray(np.asarray(inputs["L1_attn_bc"]).reshape(1, DM))
    w1b = np.ascontiguousarray(np.asarray(inputs["L1_ffn_W1"]))
    w2b = np.ascontiguousarray(np.asarray(inputs["L1_ffn_W2"]))
    maps = []
    for c in range(NCORE):
        s, r = c // 4, c % 4
        maps.append({
            "x_shard": np.ascontiguousarray(x[s, r * TOK:(r + 1) * TOK, :]),
            "wqkv_sl": np.ascontiguousarray(wqkv[c * QKV_SH:(c + 1) * QKV_SH, :]),
            "wout_sl": np.ascontiguousarray(wout[c * WOUT_SH:(c + 1) * WOUT_SH, :]),
            "b_out": bout,
            "w1_sl": np.ascontiguousarray(w1[c * W1_SH:(c + 1) * W1_SH, :]),
            "w2_sl": np.ascontiguousarray(w2[c * W1_SH:(c + 1) * W1_SH, :]),
            "w3_sl": np.ascontiguousarray(w3[c * W3_SH:(c + 1) * W3_SH, :]),
            "wq1_sl": np.ascontiguousarray(wq1[c * 256:(c + 1) * 256, :]),
            "wc1_sl": np.ascontiguousarray(wc1[c * 256:(c + 1) * 256, :]),
            "wk1_sl": np.ascontiguousarray(wk1[c * 256:(c + 1) * 256, :]),
            "wv1_sl": np.ascontiguousarray(wv1[c * 256:(c + 1) * 256, :]),
            "wg_full": wg1,
            "bc_in": bc1,
            "caus_in": _caus_mask(r),
            "maskR_in": _maskR(r),
            "w1b_sl": np.ascontiguousarray(w1b[c * 1024:(c + 1) * 1024, :]),
            "w2b_sl": np.ascontiguousarray(w2b[:, c * 1024:(c + 1) * 1024]),
        })
    return maps


def _caus_mask(r):
    m = np.zeros((1, 256), dtype=np.float32)
    for qc in range(2):
        for j in range(32):
            if j < 8 * r + 4 * qc:
                m[0, qc * 32 + j] = 1.0
    return m


def _maskR(r):
    m = np.zeros((8, 4), dtype=np.float32)
    if r > 0:
        m[:, r - 1] = 1.0
    return m


_NC_CACHE = {}


def kernel(**inputs):
    maps = shard_inputs(inputs)
    if "nc" not in _NC_CACHE:
        _NC_CACHE["nc"] = build_nc()
    nc = _NC_CACHE["nc"]
    res = run_bass_kernel_spmd(nc, maps, core_ids=list(range(NCORE)))
    out = np.zeros((2, SEQ, DM), dtype=np.float32)
    for c in range(NCORE):
        s, r = c // 4, c % 4
        out[s, r * TOK:(r + 1) * TOK, :] = res.results[c]["out"]
    return out


# version 13
# speedup vs baseline: 1.0589x; 1.0589x over previous
import numpy as np
import ml_dtypes
from contextlib import ExitStack

import concourse.bass as bass
import concourse.mybir as mybir
from concourse.bass_utils import run_bass_kernel_spmd

F32 = mybir.dt.float32
BF16 = mybir.dt.bfloat16
I32 = mybir.dt.int32
ALU = mybir.AluOpType
AF = mybir.ActivationFunctionType
AX = mybir.AxisListType
ds = bass.ds
BF = ml_dtypes.bfloat16

NCORE = 8
TOK = 1024
DM = 2048
SEQ = 4096
NH0 = 16
SCALE0 = float(np.float32(1.0) / np.sqrt(np.float32(128.0)))
MAGIC = 12582912.0
LNEPS = 1e-5
QKV_SH = 768
WOUT_SH = 256
DFF0 = 5632
DFF0P = 6144
W1_SH = 704
W3_SH = 256
NF4 = [-1.0, -0.6962, -0.5251, -0.3949, -0.2844, -0.1848, -0.0911, 0.0,
       0.0796, 0.1609, 0.2461, 0.3379, 0.4407, 0.5626, 0.723, 1.0]
NF4_MID = [(NF4[i] + NF4[i + 1]) / 2.0 for i in range(15)]
NF4_ASC = {i: float(np.float32(np.float32(NF4[i + 1]) - np.float32(NF4[i])))
           for i in range(7, 15)}
NF4_DSC = {i: float(np.float32(np.float32(NF4[i]) - np.float32(NF4[i + 1])))
           for i in range(7)}
NF4_NBLK = 180224
GOFF = 6.906768
LNC1 = float(np.log(128e-6))
KV1_K = 1024 * 1024
KV1_V = 1024 * 1024
KV1_G = 8 * 2048
KV1SZ = KV1_K + KV1_V + KV1_G


class Sem:
    __slots__ = ("name", "sched", "handle")

    def __init__(self, name):
        self.name, self.sched, self.handle = name, 0, None


class Trk:
    __slots__ = ("wid", "pending")

    def __init__(self):
        self.wid = None
        self.pending = []

    def wwaits(self):
        w = list(self.pending)
        if self.wid is not None:
            w.append(self.wid)
        self.pending = []
        return w

    def rwaits(self):
        return [self.wid] if self.wid is not None else []


class Prog:
    def __init__(self):
        self.ops = []
        self.sems = []

    def sem(self, name):
        s = Sem(name)
        self.sems.append(s)
        return s

    def op(self, engine, fn, waits=(), sem=None, n_inc=1):
        seen = {}
        for s, v in waits:
            if v > 0 and seen.get(s, 0) < v:
                seen[s] = v
        w = list(seen.items())
        ident = None
        if sem is not None:
            sem.sched += n_inc
            ident = (sem, sem.sched)
        self.ops.append((engine, fn, w, (sem, n_inc) if sem else None))
        return ident

    def dma(self, engine, out_ap, in_ap, waits=(), sem=None, slow=False):
        def fn(eng, o=out_ap, i=in_ap):
            if slow:
                return eng.dma_start(out=o, in_=i,
                                     allow_slow_non_contiguous=True)
            return eng.dma_start(out=o, in_=i)
        return self.op(engine, fn, waits=waits, sem=sem, n_inc=16)

    def emit(self, nc, ctx):
        for s in self.sems:
            s.handle = ctx.enter_context(nc.semaphore(s.name))
        block = ctx.enter_context(nc.Block())
        by_eng = {}
        for rec in self.ops:
            by_eng.setdefault(rec[0], []).append(rec)

        def run_engine(eng, recs):
            last_wait = {}
            for _, fn, waits, inc in recs:
                for s, v in waits:
                    if last_wait.get(s, 0) < v:
                        eng.wait_ge(s.handle, v)
                        last_wait[s] = v
                inst = fn(eng)
                if inc is not None:
                    inst.then_inc(inc[0].handle, inc[1])

        for name in ("sync", "tensor", "scalar", "vector", "gpsimd"):
            if name in by_eng:
                recs = by_eng[name]
                getattr(block, name)(lambda eng, recs=recs: run_engine(eng, recs))


PHASES = []


def build_nc():
    nc = bass.Bass("TRN2", target_bir_lowering=False)
    p = Prog()

    s_mm = p.sem("s_mm")
    s_ev = p.sem("s_ev")
    s_ve = p.sem("s_ve")
    s_cc = p.sem("s_cc")
    s_gp = p.sem("s_gp")
    s_fin = p.sem("s_fin")

    dma_sems = {}

    def dsem(name):
        if name not in dma_sems:
            dma_sems[name] = p.sem(name)
        return dma_sems[name]

    ESEM = {"tensor": s_mm, "scalar": s_ev, "vector": s_ve, "gpsimd": s_gp}
    PHASES.clear()

    def mark(name):
        PHASES.append((name, {s.name: s.sched for s in
                              (s_mm, s_ev, s_ve, s_cc, s_gp)}))

    def eop(engine, fn, reads=(), writes=(), extra=(), need_id=True):
        waits = list(extra)
        for t in reads:
            waits += t.rwaits()
        wl = [t.wwaits() for t in writes]
        for x in wl:
            waits += x
        ident = p.op(engine, fn, waits=waits,
                     sem=(ESEM[engine] if need_id else None))
        if ident is not None:
            for t in reads:
                t.pending.append(ident)
            for t in writes:
                t.wid = ident
        return ident

    POOL = {}
    for _n in ("d_eye", "d_ones", "d_onesc", "d_sel", "d_bout", "d_wg",
               "d_wg2", "d_wgl", "d_tril", "d_caus", "d_mr", "d_bc1",
               "d_bc2", "d_bc3"):
        POOL[_n] = "d_m1"
    for _n in ("d_arin", "d_arout", "d_pgd", "d_ptT", "d_ptm", "d_g1l",
               "d_g1c"):
        POOL[_n] = "d_m2"
    for _n in ("d_pad1", "d_pad2", "d_pad3"):
        POOL[_n] = "d_pad"
    for _w in ("w1", "w2", "w3"):
        POOL[f"d_scl{_w}"] = "d_scl"
        POOL[f"d_sbm{_w}"] = "d_sbm"
        POOL[f"d_deq{_w}"] = "d_deq"
        POOL[f"d_deqA{_w}"] = "d_dqA"
        POOL[f"d_q{_w}"] = "d_qq"
        POOL[f"d_stw{_w}0"] = "d_sw0"
        POOL[f"d_stw{_w}1"] = "d_sw1"
    for _n in ("d_c1_wq1", "d_c1_wc1", "d_c1_wk1", "d_c1_wv1"):
        POOL[_n] = "d_c1"
    for _t in range(8):
        POOL[f"d_x{_t}"] = "d_x"
    pool_chain = {}

    def dop(sem_name, out_ap, in_ap, reads=(), writes=(), extra=(), engine="sync"):
        name = POOL.get(sem_name, sem_name)
        writes = list(writes)
        if name != sem_name:
            writes.append(pool_chain.setdefault(name, Trk()))
        waits = list(extra)
        for t in reads:
            waits += t.rwaits()
        for t in writes:
            waits += t.wwaits()
        ident = p.dma(engine, out_ap, in_ap, waits=waits, sem=dsem(name),
                      slow=sem_name in ("d_bc1", "d_bc2", "d_bc3", "d_cold0", "d_cold1"))
        for t in reads:
            t.pending.append(ident)
        for t in writes:
            t.wid = ident
        return ident

    def inp(name, shape, dt=F32):
        return nc.dram_tensor(name, list(shape), dt, kind="ExternalInput").ap()

    x_in = inp("x_shard", [TOK, DM])
    wqkv_sl = inp("wqkv_sl", [QKV_SH, DM])
    wout_sl = inp("wout_sl", [WOUT_SH, DM])
    b_out_in = inp("b_out", [1, DM])
    w1_sl = inp("w1_sl", [W1_SH, DM], BF16)
    w2_sl = inp("w2_sl", [W1_SH, DM], BF16)
    w3_sl = inp("w3_sl", [W3_SH, DFF0], BF16)
    wq1_sl = inp("wq1_sl", [256, DM])
    wc1_sl = inp("wc1_sl", [256, DM])
    wk1_sl = inp("wk1_sl", [256, 1024])
    wv1_sl = inp("wv1_sl", [256, 1024])
    wg_full = inp("wg_full", [DM, 8])
    bc_in = inp("bc_in", [1, DM])
    w1b_sl = inp("w1b_sl", [1024, DM])
    w2b_sl = inp("w2b_sl", [DM, 1024])
    caus_in = inp("caus_in", [1, 256])
    maskR_in = inp("maskR_in", [8, 4])

    out_x = nc.dram_tensor("out", [TOK, DM], F32, kind="ExternalOutput").ap()

    qkvT_contrib = nc.dram_tensor("qkvT_contrib", [DM, QKV_SH], BF16)
    qkvT_arena = nc.dram_tensor("qkvT_arena", [NCORE, DM, QKV_SH], BF16)
    woutT_contrib = nc.dram_tensor("woutT_contrib", [DM, WOUT_SH], BF16)
    woutT_arena = nc.dram_tensor("woutT_arena", [NCORE, DM, WOUT_SH], BF16)

    w1q_dram = nc.dram_tensor("w1q_dram", [W1_SH, DM], BF16)
    w2q_dram = nc.dram_tensor("w2q_dram", [W1_SH, DM], BF16)
    w3q_dram = nc.dram_tensor("w3q_dram", [W3_SH, DFF0], BF16)
    scl_dram = nc.dram_tensor("scl_dram", [3 * 22528], BF16)
    w1T_contrib = nc.dram_tensor("w1T_contrib", [DM, 768], BF16)
    w2T_contrib = nc.dram_tensor("w2T_contrib", [DM, 768], BF16)
    w3T_contrib = nc.dram_tensor("w3T_contrib", [DFF0P, 256], BF16)
    w1T_arena = nc.dram_tensor("w1T_arena", [NCORE, DM, 768], BF16)
    w2T_arena = nc.dram_tensor("w2T_arena", [NCORE, DM, 768], BF16)
    w3T_arena = nc.dram_tensor("w3T_arena", [NCORE, DFF0P, 256], BF16)
    w3T_flat = nc.dram_tensor("w3T_flat", [DFF0P, DM], BF16)
    wq1_contrib = nc.dram_tensor("wq1_contrib", [256, DM], BF16)
    wq1_arena = nc.dram_tensor("wq1_arena", [DM, DM], BF16)
    wc1_contrib = nc.dram_tensor("wc1_contrib", [256, DM], BF16)
    wc1_arena = nc.dram_tensor("wc1_arena", [DM, DM], BF16)
    wk1_contrib = nc.dram_tensor("wk1_contrib", [256, 1024], BF16)
    wk1_arena = nc.dram_tensor("wk1_arena", [DM, 1024], BF16)
    wv1_contrib = nc.dram_tensor("wv1_contrib", [256, 1024], BF16)
    wv1_arena = nc.dram_tensor("wv1_arena", [DM, 1024], BF16)
    kv1_contrib = nc.dram_tensor("kv1_contrib", [KV1SZ], BF16)
    kv1_gather = nc.dram_tensor("kv1_gather", [4, KV1SZ], BF16)
    w1e_contrib = nc.dram_tensor("w1e_contrib", [DM, 1024], BF16)
    w1e_arena = nc.dram_tensor("w1e_arena", [8, DM, 1024], BF16)
    w2e_contrib = nc.dram_tensor("w2e_contrib", [1024, DM], BF16)
    w2e_arena = nc.dram_tensor("w2e_arena", [8 * 1024, DM], BF16)
    vrep_dram = nc.dram_tensor("vrep_dram", [16 * 128, 32 * 128], BF16)
    vrep1_dram = nc.dram_tensor("vrep1_dram", [8 * 128, 32 * 128], BF16)
    pg_dram = nc.dram_tensor("pg_dram", [8, 1024], F32)
    wg_dram = nc.dram_tensor("wg_dram", [128, 128], BF16)
    ar_in = nc.dram_tensor("ar_in", [1, 16], F32)
    ar_out = nc.dram_tensor("ar_out", [1, 16], F32)
    kv0_contrib = nc.dram_tensor("kv0_contrib", [2 * DM * TOK], BF16)
    kv0_gather = nc.dram_tensor("kv0_gather", [4, 2 * DM * TOK], BF16)

    kv1_c_k = lambda: kv1_contrib.ap()[0:KV1_K].rearrange("(d t) -> d t", d=1024)
    kv1_c_v = lambda: kv1_contrib.ap()[KV1_K:KV1_K + KV1_V].rearrange(
        "(t d) -> t d", t=TOK)
    kv1_c_g = lambda: kv1_contrib.ap()[KV1_K + KV1_V:].rearrange(
        "(h t) -> h t", h=8)
    kv1_g_k = lambda r: kv1_gather.ap()[r, 0:KV1_K].rearrange(
        "(d t) -> d t", d=1024)
    kv1_g_v = lambda r: kv1_gather.ap()[r, KV1_K:KV1_K + KV1_V].rearrange(
        "(t d) -> t d", t=TOK)
    kv1_g_g = lambda r: kv1_gather.ap()[r, KV1_K + KV1_V:].rearrange(
        "(h t) -> h t", h=8)
    kv0_c_k = lambda: kv0_contrib.ap()[0:DM * TOK].rearrange("(d t) -> d t", d=DM)
    kv0_c_v = lambda: kv0_contrib.ap()[DM * TOK:].rearrange("(t d) -> t d", t=TOK)
    kv0_g_k = lambda r: kv0_gather.ap()[r, 0:DM * TOK].rearrange("(d t) -> d t", d=DM)
    kv0_g_v = lambda r: kv0_gather.ap()[r, DM * TOK:].rearrange("(t d) -> t d", t=TOK)

    eye_d = nc.inline_tensor(np.eye(128, dtype=BF), name="eye128")
    onesq_d = nc.inline_tensor(np.ones((1, 128), dtype=BF), name="ones1x128")
    onescol_d = nc.inline_tensor(np.ones((128, 1), dtype=BF), name="ones128x1")
    sel_np = np.zeros((64, 32 * 128), dtype=BF)
    for j in range(32):
        sel_np[2 * j, j * 128:(j + 1) * 128] = 1
        sel_np[2 * j + 1, j * 128:(j + 1) * 128] = 1
    sel_d = nc.inline_tensor(sel_np, name="sel64")
    tril_np = np.zeros((128, 128), dtype=BF)
    for kk in range(128):
        tril_np[kk, kk:] = 1
    tril_d = nc.inline_tensor(tril_np, name="tril128")

    with ExitStack() as ctx:
        def sb(scope, name, shape, dt=F32):
            return scope.enter_context(nc.sbuf_tensor(name, list(shape), dt))

        def psum(scope, name, shape, dt=F32):
            return scope.enter_context(nc.psum_tensor(name, list(shape), dt))

        x_sb = sb(ctx, "x", [128, 8 * DM])
        lnT_sb = sb(ctx, "lnT", [128, 16 * TOK], BF16)
        wq_sb = sb(ctx, "wq", [128, 2 * 4096], BF16)
        ul_sb = sb(ctx, "ul", [128, 2 * 512], BF16)
        acc2_sb = sb(ctx, "acc2", [128, 2 * 512], F32)
        tl2_sb = sb(ctx, "tl2", [128, 2 * 512], F32)
        phat_sb = sb(ctx, "phat", [128, 2 * 512], BF16)
        kvst_sb = phat_sb
        stat_sb = sb(ctx, "stat", [128, 64], F32)
        bout_sb = sb(ctx, "bout", [1, DM], BF16)
        eyeb = sb(ctx, "eyeb", [128, 128], BF16)
        onesq_sb = sb(ctx, "onesq", [1, 128], BF16)
        onescol_sb = sb(ctx, "onescol", [128, 1], BF16)
        trsp_sb = sb(ctx, "trsp", [128, 2 * 128], BF16)
        const_sb = sb(ctx, "const", [128, 8], F32)
        deqA_sb = sb(ctx, "deqA", [128, 192], BF16)
        sbm_sb = sb(ctx, "sbm", [88, 256], BF16)
        cen_sb = sb(ctx, "cen", [88, 256], BF16)
        box88 = sb(ctx, "box88", [88, 256], I32)
        sabs_sb = sb(ctx, "sabs", [88, 4], F32)
        meanbf_sb = sb(ctx, "meanbf", [1, 8], BF16)
        mean_sb = sb(ctx, "meanp", [88, 4], F32)
        arst_sb = sb(ctx, "arst", [1, 16], F32)
        zpad_sb = sb(ctx, "zpad", [128, 64], BF16)
        qtl_sb = sb(ctx, "qtl", [128, 2 * 128], BF16)

        pP1 = psum(ctx, "pP1", [128, 1024])
        pA = psum(ctx, "pA", [128, 512])
        pB = psum(ctx, "pB", [128, 512])
        pattn = psum(ctx, "pattn", [128, 512])
        ptrA = psum(ctx, "ptrA", [128, 128], BF16)
        ptrB = psum(ctx, "ptrB", [128, 128], BF16)
        ptr = [ptrA, ptrB]
        pAB = [pA, pB]

        T = lambda n=1: [Trk() for _ in range(n)] if n > 1 else Trk()
        t_x = [Trk() for _ in range(8)]
        t_lnT = Trk()
        t_qT = Trk()
        t_wq = [Trk(), Trk()]
        t_lnscr = [Trk(), Trk()]
        t_phat = [Trk(), Trk()]
        t_stat = Trk()
        t_bout = Trk()
        t_const = Trk()
        t_eye = Trk()
        t_ones = Trk()
        t_onesc = Trk()
        t_trsp = [Trk(), Trk()]
        t_pP1 = Trk()
        t_pAB = [Trk(), Trk()]
        t_pattn = Trk()
        t_ptr = [Trk(), Trk()]
        t_qkvC = Trk()
        t_woutC = Trk()
        t_qkvA = Trk()
        t_woutA = Trk()
        t_kvC = Trk()
        t_kvG = Trk()

        eop("vector", lambda v: v.memset(const_sb[:, 0:1], LNEPS),
            writes=[t_const])
        eop("vector", lambda v: v.memset(const_sb[:, 4:5], -192.0),
            writes=[t_const])
        dop("d_eye", eyeb[:, :], eye_d.ap(), writes=[t_eye])
        dop("d_ones", onesq_sb[:, :], onesq_d.ap(), writes=[t_ones])
        dop("d_onesc", onescol_sb[:, :], onescol_d.ap(), writes=[t_onesc])
        for t in range(8):
            dop(f"d_x{t}", x_sb[:, t * DM:(t + 1) * DM],
                x_in[t * 128:(t + 1) * 128, :], writes=[t_x[t]])

        prep_scope = ExitStack()
        wf_sb = sb(prep_scope, "wf", [128, 2 * 2048], F32)
        wc_sb = sb(prep_scope, "wc", [128, 2 * 2048], BF16)
        bout_f = sb(prep_scope, "bout_f", [1, DM], F32)
        slc_sb = sb(prep_scope, "slc", [128, 5632], BF16)
        sclA_sb = sb(prep_scope, "sclA", [128, 192], BF16)
        rcp_sb = sb(prep_scope, "rcp", [128, 192], F32)
        t_slc = Trk()
        t_sclA = Trk()
        t_rcp = Trk()
        t_acc = Trk()
        t_tl = Trk()
        t_deqA = Trk()
        t_sbm = Trk()
        t_cen = Trk()
        t_qsc = Trk()
        t_sabs = Trk()
        t_mean = Trk()
        t_arst = Trk()
        t_zpad = Trk()
        t_scld = Trk()
        t_ar = Trk()
        t_q1 = Trk()
        t_q2 = Trk()
        t_q3 = Trk()
        t_w1C = Trk()
        t_w2C = Trk()
        t_w3C = Trk()
        t_w1A = Trk()
        t_w2A = Trk()
        t_w3A = Trk()
        t_w3F = Trk()
        t_wf = [Trk(), Trk()]
        t_wc = [Trk(), Trk()]
        t_boutf = Trk()
        t_wq1C, t_wc1C, t_wk1C, t_wv1C = Trk(), Trk(), Trk(), Trk()
        t_wq1A, t_wc1A, t_wk1A, t_wv1A = Trk(), Trk(), Trk(), Trk()
        t_wg = Trk()
        t_tril = Trk()
        t_kv1C = Trk()
        t_kv1G = Trk()

        NF4_TILES = {
            "w1": (w1_sl, w1q_dram, 6, 2048, 32, [128] * 5 + [64]),
            "w2": (w2_sl, w2q_dram, 6, 2048, 32, [128] * 5 + [64]),
            "w3": (w3_sl, w3q_dram, 2, 5632, 88, [128, 128]),
        }

        dop("d_bout", bout_f[:, :], b_out_in, writes=[t_boutf])
        eop("vector", lambda v: v.tensor_copy(bout_sb[:, :], bout_f[:, :]),
            reads=[t_boutf], writes=[t_bout])

        def emit_transpose_cast(src_ap, nrows, contrib, t_contrib, tag):
            for rc in range(nrows // 128):
                f = rc % 2
                dop(f"d_wf{f}", wf_sb[:, f * 2048:(f + 1) * 2048],
                    src_ap[rc * 128:(rc + 1) * 128, :], writes=[t_wf[f]])
                c = rc % 2
                eop("vector", (lambda v, f=f, c=c: v.tensor_copy(
                    wc_sb[:, c * 2048:(c + 1) * 2048],
                    wf_sb[:, f * 2048:(f + 1) * 2048])),
                    reads=[t_wf[f]], writes=[t_wc[c]])
                for cc in range(16):
                    ts = cc % 2
                    last = (cc == 15)
                    eop("tensor", (lambda t, c=c, cc=cc, ts=ts: t.transpose(
                        ptr[ts][:, :],
                        wc_sb[:, c * 2048 + cc * 128: c * 2048 + (cc + 1) * 128],
                        eyeb[:, :])),
                        reads=[t_wc[c], t_eye], writes=[t_ptr[ts]])
                    ss = cc % 2
                    eop("scalar", (lambda s, ts=ts, ss=ss: s.copy(
                        trsp_sb[:, ss * 128:(ss + 1) * 128], ptr[ts][:, :])),
                        reads=[t_ptr[ts]], writes=[t_trsp[ss]])
                    dop(f"d_st_{tag}{ss}",
                        contrib[cc * 128:(cc + 1) * 128, rc * 128:(rc + 1) * 128],
                        trsp_sb[:, ss * 128:(ss + 1) * 128],
                        reads=[t_trsp[ss]], writes=[t_contrib])

        emit_transpose_cast(wqkv_sl, QKV_SH, qkvT_contrib.ap(), t_qkvC, "q")
        emit_transpose_cast(wout_sl, WOUT_SH, woutT_contrib.ap(), t_woutC, "o")

        def emit_cast(src_ap, contrib, cols, t_contrib, tag):
            for rc in range(2):
                f = rc % 2
                dop(f"d_wf{f}", wf_sb[:, f * 2048: f * 2048 + cols],
                    src_ap[rc * 128:(rc + 1) * 128, :], writes=[t_wf[f]])
                eop("vector", (lambda v, f=f, cols=cols: v.tensor_copy(
                    wc_sb[:, f * 2048: f * 2048 + cols],
                    wf_sb[:, f * 2048: f * 2048 + cols])),
                    reads=[t_wf[f]], writes=[t_wc[f]])
                dop(f"d_c1_{tag}", contrib[rc * 128:(rc + 1) * 128, :],
                    wc_sb[:, f * 2048: f * 2048 + cols],
                    reads=[t_wc[f]], writes=[t_contrib])

        emit_cast(wq1_sl, wq1_contrib.ap(), 2048, t_wq1C, "wq1")
        emit_cast(wc1_sl, wc1_contrib.ap(), 2048, t_wc1C, "wc1")
        emit_cast(wk1_sl, wk1_contrib.ap(), 1024, t_wk1C, "wk1")
        emit_cast(wv1_sl, wv1_contrib.ap(), 1024, t_wv1C, "wv1")
        dop("d_wg", wf_sb[:, 0:128].rearrange("p (k c) -> p k c", k=16),
            wg_full.rearrange("(k p) c -> p k c", p=128),
            reads=[t_wf[0]], writes=[t_wf[0]])
        eop("vector", lambda v: v.tensor_copy(wc_sb[:, 0:128], wf_sb[:, 0:128]),
            reads=[t_wf[0]], writes=[t_wc[0]])
        dop("d_wg2", wg_dram.ap(), wc_sb[:, 0:128],
            reads=[t_wc[0]], writes=[t_wg])

        eop("vector", lambda v: v.memset(zpad_sb[:, :], 0.0), writes=[t_zpad])
        eop("vector", lambda v: v.memset(arst_sb[:, :], 0.0), writes=[t_arst])

        absw_sb = sb(prep_scope, "absw", [128, 4], F32)
        t_absw = Trk()
        eop("vector", lambda v: v.memset(absw_sb[:, 0:2], 0.0),
            writes=[t_absw])
        for mi, (srcb, nt, cols) in enumerate(
                ((w1b_sl, 8, 2048), (w2b_sl, 16, 1024))):
            for t in range(nt):
                f = t % 2
                dop(f"d_wf{f}", wf_sb[:, f * 2048: f * 2048 + cols],
                    srcb[t * 128:(t + 1) * 128, :], writes=[t_wf[f]])
                eop("vector", (lambda v, f=f, cols=cols: v.tensor_reduce(
                    absw_sb[:, 2:3], wf_sb[:, f * 2048: f * 2048 + cols],
                    AX.X, ALU.add, apply_absolute_value=True)),
                    reads=[t_wf[f]], writes=[t_absw])
                eop("vector", (lambda v, mi=mi: v.tensor_tensor(
                    absw_sb[:, mi:mi + 1], absw_sb[:, mi:mi + 1],
                    absw_sb[:, 2:3], ALU.add)), writes=[t_absw])
        eop("gpsimd", lambda g: g.tensor_reduce(
            arst_sb[0:1, 3:5], absw_sb[0:128, 0:2], AX.C, ALU.add),
            reads=[t_absw], writes=[t_arst])


        t_slc_l = Trk()

        def nf4_passA(name, midx, t_qd):
            srcd, qd, nt, wdt, nblk, rows = NF4_TILES[name]
            eop("vector", lambda v: v.memset(sclA_sb[:, :], 0.0),
                writes=[t_sclA])
            for t in range(nt):
                r = rows[t]
                r0 = sum(rows[:t])
                dop(f"d_slc", slc_sb[0:r, 0:wdt], srcd[r0:r0 + r, :],
                    writes=[t_slc_l])
                w3v = slc_sb[0:r, 0:wdt].rearrange("p (b k) -> p b k", k=64)
                eop("vector", (lambda v, t=t, r=r, w3v=w3v, nblk=nblk:
                    v.tensor_reduce(
                        sclA_sb[0:r, t * nblk:(t + 1) * nblk], w3v, AX.X,
                        ALU.max, apply_absolute_value=True)),
                    reads=[t_slc_l], writes=[t_sclA])
                eop("vector", (lambda v, t=t, r=r, nblk=nblk: v.reciprocal(
                    rcp_sb[0:r, 0:nblk],
                    sclA_sb[0:r, t * nblk:(t + 1) * nblk])),
                    reads=[t_sclA], writes=[t_rcp])
                rb = rcp_sb[0:r, 0:nblk].rearrange(
                    "p (b o) -> p b o", o=1).broadcast_to([r, nblk, 64])
                eop("vector", (lambda v, w3v=w3v, rb=rb: v.tensor_tensor(
                    w3v, w3v, rb, ALU.mult)),
                    reads=[t_rcp], writes=[t_slc_l])
                dop(f"d_q{name}", qd[r0:r0 + r, :], slc_sb[0:r, 0:wdt],
                    reads=[t_slc_l], writes=[t_qd])
            eop("tensor", lambda te: te.matmul(
                pA[0:1, 0:192], onescol_sb[:, :], sclA_sb[:, 0:192],
                start=True, stop=True),
                reads=[t_sclA, t_onesc], writes=[t_pAB[0]])
            eop("vector", (lambda v, m=midx: v.reduce_sum(
                arst_sb[0:1, m:m + 1], pA[0:1, 0:192], AX.X)),
                reads=[t_pAB[0]], writes=[t_arst])
            for t in range(nt):
                r = rows[t]
                off = midx * 22528 + sum(rows[:t]) * nblk
                dop(f"d_scl{name}", scl_dram.ap()[off:off + r * nblk].rearrange(
                    "(p b) -> p b", p=r),
                    sclA_sb[0:r, t * nblk:(t + 1) * nblk],
                    reads=[t_sclA], writes=[t_scld])

        nf4_passA("w1", 0, t_q1)
        nf4_passA("w2", 1, t_q2)
        nf4_passA("w3", 2, t_q3)

        dop("d_arin", ar_in.ap(), arst_sb[:, :], reads=[t_arst])
        p.op("gpsimd", lambda g: g.collective_compute(
            "AllReduce", ALU.add, replica_groups=[list(range(8))],
            ins=[ar_in.ap().opt()], outs=[ar_out.ap().opt()]),
            waits=[(dma_sems["d_m2"], dma_sems["d_m2"].sched)], sem=s_cc)
        t_ar.wid = (s_cc, s_cc.sched)
        dop("d_arout", arst_sb[:, :], ar_out.ap(), reads=[t_ar], writes=[t_arst])
        prep_done = ([(s, s.sched) for s in dma_sems.values()]
                     + [(s_ve, s_ve.sched), (s_ev, s_ev.sched),
                        (s_mm, s_mm.sched)])
        mark('prep')
        prep_scope.close()

        att = ExitStack()
        qT_sb = sb(att, "qT", [128, NH0 * TOK], BF16)
        lnscr_sb = sb(att, "lnscr", [128, 2 * DM], BF16)
        kTh_sb = sb(att, "kTh", [128, SEQ], BF16)
        vh_sb = sb(att, "vh", [128, SEQ], BF16)
        e_sb = sb(att, "eqm", [128, 2 * 1024], BF16)
        sel_sb = sb(att, "sel", [64, 32 * 128], BF16)
        red_sb = sb(att, "red", [128, 8 * 64], F32)
        lngs_sb = sb(att, "lngs", [128, 512], F32)
        lng2_sb = sb(att, "lng2", [128, 8 * 64], BF16)
        lngT_sb = sb(att, "lngT", [64, 1024], BF16)
        cq_sb = sb(att, "cq", [128, 32], F32)
        nb_sb = sb(att, "nb", [128, 32], F32)
        wf2_sb = nb_sb

        def all_store_waits(tags):
            w = []
            for tg in tags:
                tg = POOL.get(tg, tg)
                if tg in dma_sems:
                    w.append((dma_sems[tg], dma_sems[tg].sched))
            return w

        agw = all_store_waits(["d_st_q0", "d_st_q1"])
        id_ag1 = p.op("gpsimd", lambda g: g.collective_compute(
            "AllGather", ALU.bypass, replica_groups=[list(range(8))],
            ins=[qkvT_contrib.ap().opt()], outs=[qkvT_arena.ap().opt()]),
            waits=agw, sem=s_cc)
        t_qkvA.wid = id_ag1
        agw = all_store_waits(["d_st_o0", "d_st_o1"])
        id_ag2 = p.op("gpsimd", lambda g: g.collective_compute(
            "AllGather", ALU.bypass, replica_groups=[list(range(8))],
            ins=[woutT_contrib.ap().opt()], outs=[woutT_arena.ap().opt()]),
            waits=agw, sem=s_cc)
        t_woutA.wid = id_ag2

        def emit_layernorm(t_x_tiles, x_tile, dst_T, t_dst, lnscr, t_lnscr,
                           first_extra):
            for t in range(8):
                eop("vector", (lambda v, t=t: v.reduce_sum(
                    stat_sb[:, t:t + 1], x_tile(t), AX.X)),
                    reads=[t_x_tiles[t]], writes=[t_stat])
                eop("scalar", (lambda s, t=t: s.activation(
                    pP1[:, :], x_tile(t)[:, 0:1024], AF.Square,
                    accum_out=stat_sb[:, 8 + t:9 + t])),
                    reads=[t_x_tiles[t]], writes=[t_pP1, t_stat])
                eop("scalar", (lambda s, t=t: s.activation(
                    pP1[:, :], x_tile(t)[:, 1024:2048], AF.Square,
                    accum_out=stat_sb[:, 16 + t:17 + t])),
                    reads=[t_x_tiles[t]], writes=[t_pP1, t_stat])
            eop("vector", lambda v: v.tensor_scalar(
                stat_sb[:, 24:32], stat_sb[:, 0:8], 1.0 / DM, None, ALU.mult),
                reads=[t_stat], writes=[t_stat])
            eop("vector", lambda v: v.tensor_tensor(
                stat_sb[:, 32:40], stat_sb[:, 24:32], stat_sb[:, 24:32],
                ALU.mult), writes=[t_stat])
            eop("vector", lambda v: v.tensor_tensor(
                stat_sb[:, 40:48], stat_sb[:, 8:16], stat_sb[:, 16:24],
                ALU.add), writes=[t_stat])
            eop("vector", lambda v: v.tensor_scalar(
                stat_sb[:, 48:56], stat_sb[:, 40:48], 1.0 / DM, None, ALU.mult),
                writes=[t_stat])
            eop("vector", lambda v: v.tensor_tensor(
                stat_sb[:, 40:48], stat_sb[:, 48:56], stat_sb[:, 32:40],
                ALU.subtract), writes=[t_stat])
            eop("scalar", lambda s: s.activation(
                stat_sb[:, 48:56], stat_sb[:, 40:48], AF.Sqrt,
                bias=const_sb[:, 0:1]),
                reads=[t_const], writes=[t_stat])
            eop("vector", lambda v: v.reciprocal(
                stat_sb[:, 56:64], stat_sb[:, 48:56]), writes=[t_stat])
            for t in range(8):
                l = t % 2
                eop("vector", (lambda v, t=t, l=l: v.tensor_scalar(
                    lnscr[:, l * DM:(l + 1) * DM], x_tile(t),
                    stat_sb[:, 24 + t:25 + t], stat_sb[:, 56 + t:57 + t],
                    ALU.subtract, ALU.mult)),
                    reads=[t_x_tiles[t], t_stat], writes=[t_lnscr[l]],
                    extra=first_extra)
                for c in range(16):
                    ts = c % 2
                    eop("tensor", (lambda te, l=l, c=c, ts=ts: te.transpose(
                        ptr[ts][:, :],
                        lnscr[:, l * DM + c * 128: l * DM + (c + 1) * 128],
                        eyeb[:, :])),
                        reads=[t_lnscr[l], t_eye], writes=[t_ptr[ts]])
                    eop("scalar", (lambda s, t=t, c=c, ts=ts: s.copy(
                        dst_T[:, c * TOK + t * 128: c * TOK + (t + 1) * 128],
                        ptr[ts][:, :])),
                        reads=[t_ptr[ts]], writes=[t_dst])

        x_tile = lambda t: x_sb[:, t * DM:(t + 1) * DM]
        emit_layernorm(t_x, x_tile, lnT_sb, t_lnT, lnscr_sb, t_lnscr,
                       prep_done)
        mark('ln0')

        def qkv_colblock_ap(c0, width):
            rank = c0 // QKV_SH
            return qkvT_arena.ap()[rank].rearrange("(k p) c -> p k c", p=128)[
                :, :, c0 - rank * QKV_SH: c0 - rank * QKV_SH + width]

        for ob in range(32):
            slot = ob % 2
            dd = wq_sb[:, slot * 4096: slot * 4096 + 2048].rearrange(
                "p (k c) -> p k c", k=16)
            dop(f"d_wq{slot}", dd, qkv_colblock_ap(ob * 128, 128),
                reads=[t_qkvA], writes=[t_wq[slot]])
            for tc in range(2):
                ps = (ob * 2 + tc) % 2
                pb = pAB[ps]
                for k in range(16):
                    eop("tensor", (lambda te, s=slot, k=k, tc=tc, pb=pb: te.matmul(
                        pb[:, :],
                        wq_sb[:, s * 4096 + k * 128: s * 4096 + (k + 1) * 128],
                        lnT_sb[:, k * TOK + tc * 512: k * TOK + (tc + 1) * 512],
                        start=(k == 0), stop=(k == 15))),
                        reads=([t_wq[slot], t_lnT] if k in (0, 15) else []),
                        writes=([t_pAB[ps]] if k in (0, 15) else []),
                        need_id=(k in (0, 15)))
                if ob < 16:
                    eop("scalar", (lambda s, ob=ob, tc=tc, pb=pb: s.copy(
                        qT_sb[:, ob * TOK + tc * 512: ob * TOK + (tc + 1) * 512],
                        pb[:, :])), reads=[t_pAB[ps]], writes=[t_qT])
                else:
                    ss = (ob * 2 + tc) % 2
                    eop("scalar", (lambda s, ss=ss, pb=pb: s.copy(
                        kvst_sb[:, ss * 512:(ss + 1) * 512], pb[:, :])),
                        reads=[t_pAB[ps]], writes=[t_phat[ss]])
                    ko = ob - 16
                    dop(f"d_kv{ss}",
                        kv0_c_k()[ko * 128:(ko + 1) * 128, tc * 512:(tc + 1) * 512],
                        kvst_sb[:, ss * 512:(ss + 1) * 512],
                        reads=[t_phat[ss]], writes=[t_kvC])

        for oc in range(8):
            o0 = 4096 + oc * 256
            slot = oc % 2
            dd = wq_sb[:, slot * 4096: slot * 4096 + 4096].rearrange(
                "p (k c) -> p k c", k=16)
            dop(f"d_wq{slot}", dd, qkv_colblock_ap(o0, 256),
                reads=[t_qkvA], writes=[t_wq[slot]])
            for tt in range(8):
                ps = tt % 2
                pb = pAB[ps]
                for k in range(16):
                    eop("tensor", (lambda te, s=slot, k=k, tt=tt, pb=pb: te.matmul(
                        pb[:, 0:256],
                        lnT_sb[:, k * TOK + tt * 128: k * TOK + (tt + 1) * 128],
                        wq_sb[:, s * 4096 + k * 256: s * 4096 + (k + 1) * 256],
                        start=(k == 0), stop=(k == 15))),
                        reads=([t_wq[slot], t_lnT] if k in (0, 15) else []),
                        writes=([t_pAB[ps]] if k in (0, 15) else []),
                        need_id=(k in (0, 15)))
                ss = tt % 2
                eop("scalar", (lambda s, ss=ss, pb=pb: s.copy(
                    kvst_sb[:, ss * 512: ss * 512 + 256], pb[:, 0:256])),
                    reads=[t_pAB[ps]], writes=[t_phat[ss]])
                dop(f"d_kv{ss}",
                    kv0_c_v()[tt * 128:(tt + 1) * 128, oc * 256:(oc + 1) * 256],
                    kvst_sb[:, ss * 512: ss * 512 + 256],
                    reads=[t_phat[ss]], writes=[t_kvC])

        mark('l0qkv')
        agw = all_store_waits(["d_kv0", "d_kv1"])
        id_agkv = p.op("gpsimd", lambda g: g.collective_compute(
            "AllGather", ALU.bypass,
            replica_groups=[[0, 1, 2, 3], [4, 5, 6, 7]],
            ins=[kv0_contrib.ap().opt()], outs=[kv0_gather.ap().opt()]),
            waits=agw, sem=s_cc)
        t_kvG.wid = id_agkv
        t_vrep = [Trk(), Trk()]
        vrep_v = vrep_dram.ap().rearrange("(hp) (j d) -> hp j d", j=32)
        for hh in range(16):
            for rr in range(4):
                vs = kv0_g_v(rr).rearrange("(j p) d -> p j d", p=128)[
                    :, :, hh * 128:(hh + 1) * 128]
                dop(f"d_vrp{hh % 2}",
                    vrep_v[hh * 128:(hh + 1) * 128, rr * 8:(rr + 1) * 8, :],
                    vs, reads=[t_kvG], writes=[t_vrep[hh % 2]])

        def nf4_ladder(name, t_qd):
            srcd, qd, nt, wdt, nblk, rows = NF4_TILES[name]
            nrows = sum(rows)
            total = nrows * wdt
            CH = 512
            nch_per_row128 = wdt // CH
            for t in range(nt):
                r = rows[t]
                r0 = sum(rows[:t])
                for ch in range(nch_per_row128):
                    idx = len(ladder_q)
                    eng = "vector" if (idx % 3) else "gpsimd"
                    slot = idx % 2

                    def chunk_emit(qd=qd, t_qd=t_qd, r=r, r0=r0, ch=ch,
                                   eng=eng, slot=slot):
                        seg = ul_sb[0:r, slot * 512:(slot + 1) * 512]
                        accv = acc2_sb[0:r, slot * 512:(slot + 1) * 512]
                        tlv = tl2_sb[0:r, slot * 512:(slot + 1) * 512]
                        dop(f"d_ul{slot}", seg,
                            qd[r0:r0 + r, ch * CH:(ch + 1) * CH],
                            reads=[t_qd], writes=[t_ul[slot]],
                            engine="gpsimd")
                        eop(eng, (lambda e, accv=accv: e.memset(accv, 0.0)),
                            writes=[t_acc2[slot]])
                        for (i, cmp_op, dl) in (
                                [(i, ALU.is_gt, NF4_ASC[i]) for i in range(7, 15)]
                                + [(i, ALU.is_le, NF4_DSC[i]) for i in range(7)]):
                            eop(eng, (lambda e, seg=seg, tlv=tlv, i=i, c=cmp_op,
                                      dl=dl: e.tensor_scalar(
                                          tlv, seg, NF4_MID[i], dl, c, ALU.mult)),
                                reads=[t_ul[slot]], writes=[t_tl2[slot]])
                            eop(eng, (lambda e, accv=accv, tlv=tlv:
                                e.tensor_tensor(accv, accv, tlv, ALU.add)),
                                reads=[t_tl2[slot]], writes=[t_acc2[slot]])
                        eop(eng, (lambda e, seg=seg, accv=accv:
                            e.tensor_copy(seg, accv)),
                            reads=[t_acc2[slot]], writes=[t_ul[slot]])
                        dop(f"d_ulo{slot}", qd[r0:r0 + r, ch * CH:(ch + 1) * CH],
                            seg, reads=[t_ul[slot]], writes=[t_qd],
                            engine="gpsimd")

                    ladder_q.append(chunk_emit)

        t_ul = [Trk(), Trk()]
        t_acc2 = [Trk(), Trk()]
        t_tl2 = [Trk(), Trk()]
        mark('kv0ag')
        ladder_q = []
        nf4_ladder("w1", t_q1)
        nf4_ladder("w2", t_q2)
        nf4_ladder("w3", t_q3)

        t_meanbf = Trk()
        t_qtl = [Trk(), Trk()]

        def nf4_deq(name, midx, t_qd, contrib, t_contrib, padmap):
            ems = []
            srcd, qd, nt, wdt, nblk, rows = NF4_TILES[name]
            def prelude():
                eop("vector", (lambda v, m=midx: v.tensor_scalar(
                    arst_sb[0:1, 8 + m:9 + m], arst_sb[0:1, m:m + 1],
                    1.0 / NF4_NBLK, None, ALU.mult)),
                    reads=[t_arst], writes=[t_arst])
                eop("vector", (lambda v, m=midx: v.tensor_copy(
                    meanbf_sb[0:1, m:m + 1], arst_sb[0:1, 8 + m:9 + m])),
                    reads=[t_arst], writes=[t_meanbf])
                eop("tensor", (lambda te, m=midx: te.matmul(
                    pA[0:88, 0:1], onesq_sb[0:1, 0:88], meanbf_sb[0:1, m:m + 1],
                    start=True, stop=True)),
                    reads=[t_meanbf, t_ones], writes=[t_pAB[0]])
                eop("vector", lambda v: v.tensor_copy(mean_sb[0:88, 0:1],
                                                      pA[0:88, 0:1]),
                    reads=[t_pAB[0]], writes=[t_mean])
                dop(f"d_sbm{name}", sbm_sb[:, :],
                    scl_dram.ap()[midx * 22528:(midx + 1) * 22528].rearrange(
                        "(p b) -> p b", p=88),
                    reads=[t_scld], writes=[t_sbm])
                eop("vector", lambda v: v.tensor_scalar(
                    cen_sb[:, :], sbm_sb[:, :], mean_sb[0:88, 0:1], None,
                    ALU.subtract), reads=[t_sbm, t_mean], writes=[t_cen])
                sab_bf = sabs_sb[0:88, 3:4].bitcast(BF16)[0:88, 0:1]
                eop("vector", (lambda v, sab_bf=sab_bf: v.tensor_reduce(
                    sab_bf, cen_sb[:, :], AX.X, ALU.max,
                    apply_absolute_value=True)),
                    reads=[t_cen], writes=[t_sabs])
                two_bf = sabs_sb[0:88, 2:3].bitcast(BF16)[0:88, 0:1]
                eop("vector", (lambda v, sab_bf=sab_bf, two_bf=two_bf:
                    v.tensor_scalar(two_bf, sab_bf, 2.0, None, ALU.mult)),
                    writes=[t_sabs])
                qf_bf = sabs_sb[0:88, 2:3].bitcast(BF16)[0:88, 1:2]
                eop("vector", (lambda v, two_bf=two_bf: v.reciprocal(
                    sabs_sb[0:88, 1:2], two_bf)), writes=[t_sabs])
                eop("vector", (lambda v, qf_bf=qf_bf: v.tensor_scalar(
                    qf_bf, sabs_sb[0:88, 1:2], 256.0, None, ALU.mult)),
                    writes=[t_sabs])
                eop("vector", (lambda v, qf_bf=qf_bf: v.tensor_copy(
                    sabs_sb[0:88, 1:2], qf_bf)), writes=[t_sabs])
                boxf = box88[:, :].bitcast(F32)
                eop("vector", lambda v: v.tensor_scalar(
                    boxf, cen_sb[:, :], sabs_sb[0:88, 1:2], MAGIC,
                    ALU.mult, ALU.add), reads=[t_cen, t_sabs], writes=[t_qsc])
                eop("vector", lambda v: v.tensor_scalar(
                    boxf, boxf, MAGIC, 127.0, ALU.subtract, ALU.min),
                    reads=[t_qsc], writes=[t_qsc])
                eop("vector", lambda v: v.tensor_scalar(
                    boxf, boxf, -128.0, None, ALU.max),
                    writes=[t_qsc])
                eop("vector", lambda v: v.reciprocal(
                    sabs_sb[0:88, 0:1], sabs_sb[0:88, 1:2]), writes=[t_sabs])
                eop("vector", lambda v: v.tensor_scalar(
                    cen_sb[:, :], boxf, sabs_sb[0:88, 0:1], None,
                    ALU.mult), reads=[t_qsc, t_sabs], writes=[t_cen])
                eop("vector", lambda v: v.tensor_scalar(
                    cen_sb[:, :], cen_sb[:, :], mean_sb[0:88, 0:1], None,
                    ALU.add), reads=[t_mean], writes=[t_cen])
                dop(f"d_deq{name}", scl_dram.ap()[midx * 22528:(midx + 1) * 22528
                                                  ].rearrange(
                    "(p b) -> p b", p=88), cen_sb[:, :],
                    reads=[t_cen], writes=[t_scld])
                for t in range(nt):
                    r = rows[t]
                    off = midx * 22528 + sum(rows[:t]) * nblk
                    dop(f"d_deqA{name}",
                        deqA_sb[0:r, t * nblk:(t + 1) * nblk],
                        scl_dram.ap()[off:off + r * nblk].rearrange(
                            "(p b) -> p b", p=r),
                        reads=[t_scld], writes=[t_deqA])
            ems.append(prelude)
            ncc = wdt // 128
            for rc in range(nt):
                r = rows[rc]
                r0 = sum(rows[:rc])
                for cc in range(ncc):
                    qs = (rc * ncc + cc) % 2

                    def unitA(r=r, r0=r0, cc=cc, qs=qs, rc=rc):
                        dop(f"d_qt{qs}", qtl_sb[0:r, qs * 128:(qs + 1) * 128],
                            qd[r0:r0 + r, cc * 128:(cc + 1) * 128],
                            reads=[t_qd], writes=[t_qtl[qs]])
                        db = deqA_sb[0:r, rc * nblk + cc * 2:
                                     rc * nblk + cc * 2 + 2
                                     ].rearrange("p (b o) -> p b o", o=1
                                                 ).broadcast_to([r, 2, 64])
                        wv = qtl_sb[0:r, qs * 128:(qs + 1) * 128].rearrange(
                            "p (b k) -> p b k", k=64)
                        eop("vector", (lambda v, wv=wv, db=db: v.tensor_tensor(
                            wv, wv, db, ALU.mult)),
                            reads=[t_deqA], writes=[t_qtl[qs]])

                    def unitB(r=r, r0=r0, cc=cc, qs=qs, rc=rc):
                        ts_ = cc % 2
                        eop("tensor", (lambda te, qs=qs, ts_=ts_, r=r:
                            te.transpose(
                                ptr[ts_][:, 0:r],
                                qtl_sb[0:r, qs * 128:(qs + 1) * 128],
                                eyeb[0:r, 0:r])),
                            reads=[t_qtl[qs], t_eye], writes=[t_ptr[ts_]])
                        ss = cc % 2
                        eop("scalar", (lambda s, ts_=ts_, ss=ss, r=r: s.copy(
                            trsp_sb[:, ss * 128: ss * 128 + r],
                            ptr[ts_][:, 0:r])),
                            reads=[t_ptr[ts_]], writes=[t_trsp[ss]])
                        for (fr0, fr1, pr0) in padmap(cc):
                            dop(f"d_stw{name}{ss}",
                                contrib[pr0: pr0 + (fr1 - fr0), r0: r0 + r],
                                trsp_sb[fr0 - cc * 128: fr1 - cc * 128,
                                        ss * 128: ss * 128 + r],
                                reads=[t_trsp[ss]], writes=[t_contrib])

                    ems.append((unitA, unitB))
            return ems

        def pm_id(cc):
            return [(cc * 128, (cc + 1) * 128, cc * 128)]

        def pm_w3(cc):
            f0, f1 = cc * 128, (cc + 1) * 128
            out = []
            a = f0
            while a < f1:
                seg_end = min(f1, ((a // 704) + 1) * 704)
                out.append((a, seg_end, a + 64 * (a // 704)))
                a = seg_end
            return out

        mark('nf4ladder')
        deq_lazy = [
            lambda: nf4_deq("w1", 0, t_q1, w1T_contrib.ap(), t_w1C, pm_id),
            lambda: nf4_deq("w2", 1, t_q2, w2T_contrib.ap(), t_w2C, pm_id),
            lambda: nf4_deq("w3", 2, t_q3, w3T_contrib.ap(), t_w3C, pm_w3),
        ]
        deq_qs = [None, None, None]

        def emit_pads():
            _emit_pads()

        def _emit_pads():
            for cc in range(16):
                dop("d_pad1", w1T_contrib.ap()[cc * 128:(cc + 1) * 128, 704:768],
                    zpad_sb[:, :], reads=[t_zpad], writes=[t_w1C])
                dop("d_pad2", w2T_contrib.ap()[cc * 128:(cc + 1) * 128, 704:768],
                    zpad_sb[:, :], reads=[t_zpad], writes=[t_w2C])
            for k in range(8):
                for cseg in range(4):
                    dop("d_pad3", w3T_contrib.ap()[768 * k + 704: 768 * k + 768,
                                                   cseg * 64:(cseg + 1) * 64],
                        zpad_sb[0:64, 0:64], reads=[t_zpad], writes=[t_w3C])

        def emit_wags():
            _emit_wags()

        def _emit_wags():
            for nm, contrib_t, arena_t, tC, tA in (
                    ("w1", w1T_contrib, w1T_arena, t_w1C, t_w1A),
                    ("w2", w2T_contrib, w2T_arena, t_w2C, t_w2A),
                    ("w3", w3T_contrib, w3T_arena, t_w3C, t_w3A)):
                agw = all_store_waits([f"d_stw{nm}0", f"d_stw{nm}1",
                                       "d_pad1", "d_pad2", "d_pad3"])
                ident = p.op("gpsimd", (lambda g, ci=contrib_t, ai=arena_t:
                    g.collective_compute(
                        "AllGather", ALU.bypass, replica_groups=[list(range(8))],
                        ins=[ci.ap().opt()], outs=[ai.ap().opt()])),
                    waits=agw, sem=s_cc)
                tA.wid = ident
            for r in range(8):
                dop("d_w3f", w3T_flat.ap()[:, r * 256:(r + 1) * 256],
                    w3T_arena.ap()[r], reads=[t_w3A], writes=[t_w3F],
                    engine="gpsimd")

            for nm, ci, ai, tA in (("wq1", wq1_contrib, wq1_arena, t_wq1A),
                                   ("wc1", wc1_contrib, wc1_arena, t_wc1A),
                                   ("wk1", wk1_contrib, wk1_arena, t_wk1A),
                                   ("wv1", wv1_contrib, wv1_arena, t_wv1A)):
                agw = all_store_waits([f"d_c1_{nm}"])
                ident = p.op("gpsimd", (lambda g, ci=ci, ai=ai:
                    g.collective_compute(
                        "AllGather", ALU.bypass, replica_groups=[list(range(8))],
                        ins=[ci.ap().opt()], outs=[ai.ap().opt()])),
                    waits=agw, sem=s_cc)
                tA.wid = ident

        mark('nf4deq')

        t_kTh = Trk()
        t_vh = Trk()
        t_e = [Trk(), Trk()]
        t_sel = Trk()
        t_red = Trk()
        t_lngs = Trk()
        t_lng2 = Trk()
        t_lngT = [Trk()]
        t_cq = Trk()
        t_nb = Trk()

        dop("d_sel", sel_sb[:, :], sel_d.ap(), writes=[t_sel],
            extra=prep_done)

        def emit_units(ems):
            for i in range(0, len(ems), 2):
                pair = ems[i:i + 2]
                for em in pair:
                    if isinstance(em, tuple):
                        em[0]()
                    else:
                        em()
                for em in pair:
                    if isinstance(em, tuple):
                        em[1]()

        for h in range(NH0):
            for r in range(4):
                dop(f"d_kTh{r}", kTh_sb[:, r * TOK:(r + 1) * TOK],
                    kv0_g_k(r)[h * 128:(h + 1) * 128, :],
                    reads=[t_kvG], writes=[t_kTh])
                if r == 0:
                    dop("d_vh0", vh_sb[:, :],
                        vrep_dram.ap()[h * 128:(h + 1) * 128, :],
                        reads=[t_vrep[h % 2]], writes=[t_vh])
            kv_waits = [(dma_sems[f"d_kTh{r}"], dma_sems[f"d_kTh{r}"].sched)
                        for r in range(4)]
            kv_waits += [(dma_sems["d_vh0"], dma_sems["d_vh0"].sched)]

            for qt in range(8):
                for quar in range(4):
                    pr = quar % 2
                    t_p1 = (t_pP1 if pr == 0 else t_pAB[0])
                    for kc in range(2):
                        kk = quar * 2 + kc
                        dst = (pP1[:, kc * 512:(kc + 1) * 512] if pr == 0
                               else pAB[kc][:, :])
                        eop("tensor", (lambda te, h=h, qt=qt, kk=kk, dst=dst:
                            te.matmul(
                                dst,
                                qT_sb[:, h * TOK + qt * 128:
                                      h * TOK + (qt + 1) * 128],
                                kTh_sb[:, kk * 512:(kk + 1) * 512],
                                start=True, stop=True)),
                            reads=[t_qT, t_kTh],
                            writes=[t_p1 if pr == 0 or kc == 0 else t_pAB[1]],
                            extra=(kv_waits if (qt == 0 and quar == 0 and kc == 0)
                                   else []))
                    for kc in range(2):
                        pv = (pP1[:, kc * 512:(kc + 1) * 512] if pr == 0
                              else pAB[kc][:, :]).rearrange(
                            "p (j k) -> p j k", j=4)
                        eop("vector", (lambda v, qt=qt, q=quar, kc=kc, pv=pv:
                            v.tensor_reduce(
                                red_sb[:, qt * 64 + 32 + q * 8 + kc * 4:
                                       qt * 64 + 36 + q * 8 + kc * 4],
                                pv, AX.X, ALU.max)),
                            reads=[t_p1 if pr == 0 or kc == 0 else t_pAB[1]],
                            writes=[t_red])
                    eop("vector", (lambda v, qt=qt, q=quar: v.tensor_reduce(
                        cq_sb[:, qt * 4 + q: qt * 4 + q + 1],
                        red_sb[:, qt * 64 + 32 + q * 8: qt * 64 + 40 + q * 8],
                        AX.X, ALU.max)),
                        reads=[t_red], writes=[t_cq])
                    eop("vector", (lambda v, qt=qt, q=quar: v.tensor_scalar(
                        nb_sb[:, qt * 4 + q: qt * 4 + q + 1],
                        cq_sb[:, qt * 4 + q: qt * 4 + q + 1],
                        -SCALE0, None, ALU.mult)),
                        reads=[t_cq], writes=[t_nb])
                    es = quar % 2
                    for kc in range(2):
                        pv2 = (pP1[:, kc * 512:(kc + 1) * 512] if pr == 0
                               else pAB[kc][:, :])
                        eop("scalar", (lambda s, qt=qt, q=quar, es=es, kc=kc,
                                       pv2=pv2: s.activation(
                            e_sb[:, es * 1024 + kc * 512:
                                 es * 1024 + (kc + 1) * 512],
                            pv2, AF.Exp, scale=SCALE0,
                            bias=nb_sb[:, qt * 4 + q: qt * 4 + q + 1])),
                            reads=[(t_p1 if pr == 0 or kc == 0 else t_pAB[1]),
                                   t_nb],
                            writes=[t_e[es]])
                    e3 = e_sb[:, es * 1024:(es + 1) * 1024].rearrange(
                        "p (j k) -> p j k", j=8)
                    eop("vector", (lambda v, qt=qt, q=quar, e3=e3: v.reduce_sum(
                        red_sb[:, qt * 64 + q * 8: qt * 64 + q * 8 + 8],
                        e3, AX.X)),
                        reads=[t_e[es]], writes=[t_red])
            r3 = lambda o: red_sb[:, :].rearrange("p (qt c) -> p qt c", qt=8)[
                :, :, o:o + 32]
            l3 = lambda a, b: lngs_sb[:, a:b].rearrange("p (qt c) -> p qt c", qt=8)
            cq4 = cq_sb[:, :].rearrange("p (qt q) -> p qt q", qt=8)
            cqb = cq_sb[:, :].rearrange("p (qt q o) -> p qt q o", qt=8, o=1
                                        ).broadcast_to([128, 8, 4, 8])
            eop("vector", lambda v: v.tensor_scalar(
                l3(0, 256), r3(0), 1e-38, None, ALU.max),
                reads=[t_red], writes=[t_lngs])
            eop("scalar", lambda s: s.activation(
                lngs_sb[:, 0:256], lngs_sb[:, 0:256], AF.Ln),
                reads=[t_lngs], writes=[t_lngs])
            eop("vector", lambda v: v.scalar_tensor_tensor(
                l3(256, 512), r3(32), -SCALE0, l3(0, 256),
                ALU.mult, ALU.add),
                reads=[t_red, t_lngs], writes=[t_lngs])
            l4 = lambda a, b: lngs_sb[:, a:b].rearrange(
                "p (qt q j) -> p qt q j", qt=8, q=4)
            eop("vector", lambda v: v.scalar_tensor_tensor(
                l4(256, 512), cqb, SCALE0, l4(256, 512),
                ALU.mult, ALU.add),
                reads=[t_cq, t_lngs], writes=[t_lngs])
            eop("vector", lambda v: v.tensor_reduce(
                stat_sb[:, 0:8], cq4, AX.X, ALU.max),
                reads=[t_cq], writes=[t_stat])
            eop("vector", lambda v: v.tensor_scalar(
                stat_sb[:, 0:8], stat_sb[:, 0:8], SCALE0, None, ALU.mult),
                writes=[t_stat])
            cmax_b = stat_sb[:, 0:8].rearrange("p (qt o) -> p qt o", o=1
                                               ).broadcast_to([128, 8, 4])
            eop("vector", lambda v: v.scalar_tensor_tensor(
                wf2_sb[:, :].rearrange("p (qt q) -> p qt q", qt=8),
                cq4, SCALE0, cmax_b, ALU.mult, ALU.subtract),
                reads=[t_cq, t_stat], writes=[t_nb])
            eop("scalar", lambda s: s.activation(
                wf2_sb[:, :], wf2_sb[:, :], AF.Exp),
                reads=[t_nb], writes=[t_nb])
            wfb = wf2_sb[:, :].rearrange("p (qt q o) -> p qt q o", qt=8, o=1
                                         ).broadcast_to([128, 8, 4, 8])
            r4m = red_sb[:, :].rearrange("p (qt c) -> p qt c", qt=8)[
                :, :, 0:32].rearrange("p qt (q j) -> p qt q j", q=4)
            sc4 = lambda a, b: lngs_sb[:, a:b].rearrange(
                "p (qt q j) -> p qt q j", qt=8, q=4)
            eop("vector", lambda v: v.tensor_tensor(
                sc4(0, 256), r4m, wfb, ALU.mult),
                reads=[t_red, t_nb], writes=[t_lngs])
            eop("vector", lambda v: v.reduce_sum(
                stat_sb[:, 8:16], l3(0, 256), AX.X),
                reads=[t_lngs], writes=[t_stat])
            eop("scalar", lambda s: s.activation(
                stat_sb[:, 8:16], stat_sb[:, 8:16], AF.Ln),
                reads=[t_stat], writes=[t_stat])
            eop("vector", lambda v: v.tensor_tensor(
                stat_sb[:, 8:16], stat_sb[:, 8:16], stat_sb[:, 0:8], ALU.add),
                writes=[t_stat])
            lnL_b = stat_sb[:, 8:16].rearrange("p (qt o) -> p qt o", o=1
                                               ).broadcast_to([128, 8, 32])
            eop("vector", lambda v: v.tensor_tensor(
                l3(256, 512), l3(256, 512), lnL_b, ALU.subtract),
                reads=[t_stat], writes=[t_lngs])
            hi_v = lng2_sb[:, :].rearrange("p (qt j two) -> p qt j two",
                                           qt=8, two=2)
            eop("vector", lambda v: v.tensor_scalar(
                hi_v[:, :, :, 0:1].squeeze(3), l3(256, 512),
                1.0 / SCALE0, None, ALU.mult),
                reads=[t_lngs], writes=[t_lng2])
            eop("vector", lambda v: v.tensor_scalar(
                lngs_sb[:, 0:256], lngs_sb[:, 256:512], 1.0 / SCALE0, None,
                ALU.mult), reads=[t_lngs], writes=[t_lngs])
            eop("vector", lambda v: v.tensor_tensor(
                hi_v[:, :, :, 1:2].squeeze(3), l3(0, 256),
                hi_v[:, :, :, 0:1].squeeze(3), ALU.subtract),
                reads=[t_lngs, t_lng2], writes=[t_lng2])

            ls = 0
            for qt in range(8):
                ts = qt % 2
                eop("tensor", (lambda te, qt=qt, ts=ts: te.transpose(
                    ptr[ts][0:64, 0:128],
                    lng2_sb[:, qt * 64:(qt + 1) * 64], eyeb[:, :])),
                    reads=[t_lng2, t_eye], writes=[t_ptr[ts]])
                eop("scalar", (lambda s, qt=qt, ts=ts, ls=ls: s.copy(
                    lngT_sb[0:64, qt * 128: (qt + 1) * 128],
                    ptr[ts][0:64, 0:128])),
                    reads=[t_ptr[ts]], writes=[t_lngT[ls]])

            for qc in range(2):
                def pv0(j):
                    ph = j % 2
                    eop("tensor", (lambda te, j=j, ph=ph: te.matmul(
                        pattn[:, :],
                        vh_sb[:, j * 128:(j + 1) * 128],
                        phat_sb[:, ph * 512:(ph + 1) * 512],
                        start=(j == 0), stop=(j == 31))),
                        reads=[t_vh, t_phat[ph]],
                        writes=([t_pattn] if j in (0, 31) else []))
                for j in range(32):
                    ps = j % 2
                    pb = pAB[ps]
                    eop("tensor", (lambda te, h=h, j=j, qc=qc, pb=pb: te.matmul(
                        pb[:, :],
                        kTh_sb[:, j * 128:(j + 1) * 128],
                        qT_sb[:, h * TOK + qc * 512: h * TOK + (qc + 1) * 512],
                        start=True, stop=False)),
                        reads=[t_kTh, t_qT], writes=[t_pAB[ps]])
                    eop("tensor", (lambda te, j=j, qc=qc, ls=ls, pb=pb: te.matmul(
                        pb[:, :], sel_sb[:, j * 128:(j + 1) * 128],
                        lngT_sb[0:64, qc * 512: (qc + 1) * 512],
                        start=False, stop=True)),
                        reads=[t_sel, t_lngT[ls]], writes=[t_pAB[ps]])
                    ph = j % 2
                    eop("scalar", (lambda s, ph=ph, pb=pb: s.activation(
                        phat_sb[:, ph * 512:(ph + 1) * 512], pb[:, :],
                        AF.Exp, scale=SCALE0)),
                        reads=[t_pAB[ps]], writes=[t_phat[ph]])
                    if j > 0:
                        pv0(j - 1)
                pv0(31)
                eop("scalar", (lambda s, h=h, qc=qc: s.copy(
                    lnT_sb[:, h * TOK + qc * 512: h * TOK + (qc + 1) * 512],
                    pattn[:, :])), reads=[t_pattn], writes=[t_lnT])
            mark(f'h{h}')
            if h < 12:
                n_per = (len(ladder_q) + 11) // 12
                for em in ladder_q[h * n_per:(h + 1) * n_per]:
                    em()
            if 6 <= h <= 10:
                if deq_qs[0] is None:
                    deq_qs[0] = deq_lazy[0]()
                q = deq_qs[0]
                n_per = (len(q) + 4) // 5
                emit_units(q[(h - 6) * n_per:(h - 5) * n_per])
            if 11 <= h <= 12:
                if deq_qs[1] is None:
                    deq_qs[1] = deq_lazy[1]()
                q = deq_qs[1]
                n_per = (len(q) + 1) // 2
                emit_units(q[(h - 11) * n_per:(h - 10) * n_per])
            if 13 <= h <= 14:
                if deq_qs[2] is None:
                    deq_qs[2] = deq_lazy[2]()
                q = deq_qs[2]
                n_per = (len(q) + 1) // 2
                emit_units(q[(h - 13) * n_per:(h - 12) * n_per])
            if h == 14:
                emit_pads()
                emit_wags()
        mark('l0attn')
        att_done = [(s_mm, s_mm.sched), (s_ev, s_ev.sched),
                    (s_ve, s_ve.sched)]
        att.close()

        for oc in range(8):
            slot = oc % 2
            rank = (oc * 256) // WOUT_SH
            a = woutT_arena.ap()[rank].rearrange("(k p) c -> p k c", p=128)
            dd = wq_sb[:, slot * 4096: slot * 4096 + 4096].rearrange(
                "p (k c) -> p k c", k=16)
            dop(f"d_wq{slot}", dd, a, reads=[t_woutA], writes=[t_wq[slot]])
            for qt in range(8):
                ps = qt % 2
                pb = pAB[ps]
                for k in range(16):
                    eop("tensor", (lambda te, s=slot, k=k, qt=qt, pb=pb: te.matmul(
                        pb[:, 0:256],
                        lnT_sb[:, k * TOK + qt * 128: k * TOK + (qt + 1) * 128],
                        wq_sb[:, s * 4096 + k * 256: s * 4096 + (k + 1) * 256],
                        start=(k == 0), stop=False)),
                        reads=([t_wq[slot], t_lnT] if k == 0 else []),
                        writes=([t_pAB[ps]] if k == 0 else []), need_id=(k == 0))
                eop("tensor", (lambda te, oc=oc, pb=pb: te.matmul(
                    pb[:, 0:256], onesq_sb[:, :],
                    bout_sb[0:1, oc * 256:(oc + 1) * 256],
                    start=False, stop=True)),
                    reads=[t_ones, t_bout, t_lnT, t_wq[slot]],
                    writes=[t_pAB[ps]])
                eop("vector", (lambda v, qt=qt, oc=oc, pb=pb: v.tensor_tensor(
                    x_sb[:, qt * DM + oc * 256: qt * DM + (oc + 1) * 256],
                    x_sb[:, qt * DM + oc * 256: qt * DM + (oc + 1) * 256],
                    pb[:, 0:256], ALU.add)),
                    reads=[t_pAB[ps]], writes=[t_x[qt]])

        mark('l0outproj')
        ffn = ExitStack()
        h_sb = sb(ffn, "h", [128, 48 * 512], BF16)
        lnscr2_sb = sb(ffn, "lnscr2", [128, 2 * DM], BF16)
        w3r_sb = sb(ffn, "w3r", [128, 2 * 512], BF16)
        t_lnscr2 = [Trk(), Trk()]
        t_h = [Trk() for _ in range(48)]
        t_w3r = [Trk(), Trk()]

        emit_layernorm(t_x, x_tile, lnT_sb, t_lnT, lnscr2_sb, t_lnscr2,
                       att_done)

        def w1_colblock_ap(arena, fc):
            rank, c0 = fc // 6, (fc % 6) * 128
            return arena.ap()[rank].rearrange("(k p) c -> p k c", p=128)[
                :, :, c0:c0 + 128]

        accs = [(pA[:, :], t_pAB[0]), (pB[:, :], t_pAB[1]),
                (pP1[:, 0:512], t_pP1), (pP1[:, 512:1024], t_pP1)]

        for th in range(2):
            up_ps = [(pA[:, :], t_pAB[0]), (pB[:, :], t_pAB[1]),
                     (pP1[:, 0:512], t_pP1), (pP1[:, 512:1024], t_pP1)]

            def evac_fc(fc):
                pa, ta = up_ps[(fc % 2) * 2]
                pb_, tb = up_ps[(fc % 2) * 2 + 1]
                ph = fc % 2
                eop("scalar", (lambda s, ph=ph, pa=pa: s.activation(
                    phat_sb[:, ph * 512:(ph + 1) * 512], pa, AF.Silu)),
                    reads=[ta], writes=[t_phat[ph]])
                eop("vector", (lambda v, fc=fc, ph=ph, pb_=pb_: v.tensor_tensor(
                    h_sb[:, fc * 512:(fc + 1) * 512],
                    phat_sb[:, ph * 512:(ph + 1) * 512], pb_, ALU.mult)),
                    reads=[t_phat[ph], tb], writes=[t_h[fc]])
            for fc in range(48):
                slot = fc % 2
                d1 = wq_sb[:, slot * 4096: slot * 4096 + 2048].rearrange(
                    "p (k c) -> p k c", k=16)
                d2 = wq_sb[:, slot * 4096 + 2048: slot * 4096 + 4096].rearrange(
                    "p (k c) -> p k c", k=16)
                dop(f"d_wq{slot}", d1, w1_colblock_ap(w1T_arena, fc),
                    reads=[t_w1A], writes=[t_wq[slot]])
                dop(f"d_wq{slot}", d2, w1_colblock_ap(w2T_arena, fc),
                    reads=[t_w2A], writes=[t_wq[slot]])
                for ab in range(2):
                    pb, tp = up_ps[(fc % 2) * 2 + ab]
                    co = ab * 2048
                    for k in range(16):
                        eop("tensor", (lambda te, s=slot, k=k, th=th, pb=pb,
                                       co=co: te.matmul(
                            pb,
                            wq_sb[:, s * 4096 + co + k * 128:
                                  s * 4096 + co + (k + 1) * 128],
                            lnT_sb[:, k * TOK + th * 512:
                                   k * TOK + th * 512 + 512],
                            start=(k == 0), stop=(k == 15))),
                            reads=([t_wq[slot], t_lnT] if k in (0, 15) else []),
                            writes=([tp] if k in (0, 15) else []),
                            need_id=(k in (0, 15)))
                if fc > 0:
                    evac_fc(fc - 1)
            evac_fc(47)
            for dc in range(4):
                for fc in range(48):
                    ws = fc % 2
                    dop(f"d_w3r{ws}", w3r_sb[:, ws * 512:(ws + 1) * 512],
                        w3T_flat.ap()[fc * 128:(fc + 1) * 128,
                                      dc * 512:(dc + 1) * 512],
                        reads=[t_w3F], writes=[t_w3r[ws]], extra=att_done)
                    for tb in range(4):
                        pb, tp = accs[tb]
                        eop("tensor", (lambda te, fc=fc, tb=tb, ws=ws, pb=pb:
                            te.matmul(
                                pb,
                                h_sb[:, fc * 512 + tb * 128:
                                     fc * 512 + (tb + 1) * 128],
                                w3r_sb[:, ws * 512:(ws + 1) * 512],
                                start=(fc == 0), stop=(fc == 47))),
                            reads=[t_h[fc], t_w3r[ws]],
                            writes=([tp] if fc in (0, 47) else []))
                for tb in range(4):
                    pb, tp = accs[tb]
                    tg = th * 4 + tb
                    eop("vector", (lambda v, tg=tg, dc=dc, pb=pb: v.tensor_tensor(
                        x_sb[:, tg * DM + dc * 512: tg * DM + (dc + 1) * 512],
                        x_sb[:, tg * DM + dc * 512: tg * DM + (dc + 1) * 512],
                        pb, ALU.add)),
                        reads=[tp], writes=[t_x[tg]])
        ffn.close()



        mark('l0ffn')
        l1w_pre = [(s_mm, s_mm.sched), (s_ev, s_ev.sched), (s_ve, s_ve.sched)]
        l1w = ExitStack()
        wfq_sb = sb(l1w, "wfq", [128, 2 * 2048], F32)
        qz_sb = sb(l1w, "qz", [128, 2 * 2048], BF16)
        boxw_sb = sb(l1w, "boxw", [128, 2048], I32)
        scol_sb = sb(l1w, "scol", [128, 4], F32)
        t_wfq = [Trk(), Trk()]
        t_qz = [Trk(), Trk()]
        t_boxw = Trk()
        t_scol = Trk()
        t_w1eC = Trk()
        t_w2eC = Trk()
        t_w1eA = Trk()
        t_w2eA = Trk()

        for mi in range(2):
            eop("vector", (lambda v, mi=mi: v.tensor_scalar(
                arst_sb[0:1, 5 + 2 * mi:6 + 2 * mi],
                arst_sb[0:1, 3 + mi:4 + mi],
                1.0 / 16777216.0, 1e-5, ALU.mult, ALU.max)),
                reads=[t_arst], writes=[t_arst], extra=l1w_pre)
            eop("vector", (lambda v, mi=mi: v.reciprocal(
                arst_sb[0:1, 6 + 2 * mi:7 + 2 * mi],
                arst_sb[0:1, 5 + 2 * mi:6 + 2 * mi])), writes=[t_arst])
            hv = arst_sb[0:1, 9 + mi:10 + mi].bitcast(BF16)
            eop("vector", (lambda v, mi=mi, hv=hv: v.tensor_copy(
                hv[0:1, 0:1], arst_sb[0:1, 6 + 2 * mi:7 + 2 * mi])),
                writes=[t_arst])
            eop("vector", (lambda v, mi=mi, hv=hv: v.tensor_tensor(
                hv[0:1, 1:2], arst_sb[0:1, 6 + 2 * mi:7 + 2 * mi],
                hv[0:1, 0:1], ALU.subtract)), writes=[t_arst])
            eop("tensor", (lambda te, hv=hv: te.matmul(
                pA[:, 0:1], onesq_sb[:, :], hv[0:1, 0:1],
                start=True, stop=False)),
                reads=[t_arst, t_ones], writes=[t_pAB[0]], extra=l1w_pre)
            eop("tensor", (lambda te, hv=hv: te.matmul(
                pA[:, 0:1], onesq_sb[:, :], hv[0:1, 1:2],
                start=False, stop=True)),
                reads=[t_arst, t_ones], writes=[t_pAB[0]])
            eop("vector", (lambda v, mi=mi: v.tensor_copy(
                scol_sb[:, mi:mi + 1], pA[:, 0:1])),
                reads=[t_pAB[0]], writes=[t_scol], extra=l1w_pre)

        boxw_f = boxw_sb[:, :].bitcast(F32)

        def tern_quant(srcb, nt, cols, contrib, t_contrib, scx):
            for t in range(nt):
                f = t % 2
                dop(f"d_qq", wfq_sb[:, f * 2048: f * 2048 + cols],
                    srcb[t * 128:(t + 1) * 128, :], writes=[t_wfq[f]],
                    extra=l1w_pre)
                bx = boxw_f[:, 0:cols]
                eop("vector", (lambda v, f=f, bx=bx, cols=cols: v.tensor_scalar(
                    bx, wfq_sb[:, f * 2048: f * 2048 + cols],
                    scol_sb[:, scx:scx + 1], MAGIC, ALU.mult, ALU.add)),
                    reads=[t_wfq[f], t_scol], writes=[t_boxw], extra=l1w_pre)
                eop("vector", (lambda v, bx=bx: v.tensor_scalar(
                    bx, bx, MAGIC, 1.0, ALU.subtract, ALU.min)),
                    writes=[t_boxw])
                eop("vector", (lambda v, f=f, bx=bx, cols=cols: v.tensor_scalar(
                    qz_sb[:, f * 2048: f * 2048 + cols], bx,
                    -1.0, None, ALU.max)),
                    reads=[t_boxw], writes=[t_qz[f]])
                for cc in range(cols // 128):
                    ts_ = cc % 2
                    eop("tensor", (lambda te, f=f, cc=cc, ts_=ts_: te.transpose(
                        ptr[ts_][:, :],
                        qz_sb[:, f * 2048 + cc * 128: f * 2048 + (cc + 1) * 128],
                        eyeb[:, :])),
                        reads=[t_qz[f], t_eye], writes=[t_ptr[ts_]])
                    ss = cc % 2
                    eop("scalar", (lambda s, ts_=ts_, ss=ss: s.copy(
                        trsp_sb[:, ss * 128:(ss + 1) * 128], ptr[ts_][:, :])),
                        reads=[t_ptr[ts_]], writes=[t_trsp[ss]])
                    dop(f"d_stww1{ss}",
                        contrib[cc * 128:(cc + 1) * 128, t * 128:(t + 1) * 128],
                        trsp_sb[:, ss * 128:(ss + 1) * 128],
                        reads=[t_trsp[ss]], writes=[t_contrib])

        tern_quant(w1b_sl, 8, 2048, w1e_contrib.ap(), t_w1eC, 0)
        tern_quant(w2b_sl, 16, 1024, w2e_contrib.ap(), t_w2eC, 1)
        l1w.close()

        for nm, ci, ai, tA in (("w1e", w1e_contrib, w1e_arena, t_w1eA),
                               ("w2e", w2e_contrib, w2e_arena, t_w2eA)):
            agw = all_store_waits(["d_sw0", "d_sw1"])
            ident = p.op("gpsimd", (lambda g, ci=ci, ai=ai:
                g.collective_compute(
                    "AllGather", ALU.bypass, replica_groups=[list(range(8))],
                    ins=[ci.ap().opt()], outs=[ai.ap().opt()])),
                waits=agw, sem=s_cc)
            tA.wid = ident

        mark('l1wprep')
        l1pre = [(s_mm, s_mm.sched), (s_ev, s_ev.sched), (s_ve, s_ve.sched)]
        att1 = ExitStack()
        qT1_sb = sb(att1, "qT1", [128, 16 * TOK], BF16)
        kT1_sb = sb(att1, "kT1", [128, SEQ], BF16)
        v1_sb = sb(att1, "v1", [128, SEQ], BF16)
        msqF_sb = sb(att1, "msqF", [128, 2 * 256], F32)
        sqtB_sb = sb(att1, "sqtB", [128, 256], F32)
        mskB_sb = sb(att1, "mskB", [128, 256], F32)
        ptT_sb = sb(att1, "ptT", [128, 256], F32)
        ptm_sb = sb(att1, "ptm", [128, 64], F32)
        eps_sb = sb(att1, "eps1", [128, 64], F32)
        sdia_sb = sb(att1, "sdia", [128, 64], F32)
        gloc_sb = sb(att1, "gloc", [8, 1024], F32)
        tot_sb = sb(att1, "tot", [8, 16], F32)
        wg1_sb = sb(att1, "wg1", [128, 128], BF16)
        tril_sb = sb(att1, "tril", [128, 128], BF16)
        cb_sb = sb(att1, "cb", [1, 256], F32)
        cbb_sb = sb(att1, "cbb", [1, 256], BF16)
        scat_sb = sb(att1, "scat", [128, 16], F32)
        scatb_sb = sb(att1, "scatb", [128, 16], BF16)
        pden = psum(att1, "pden", [1, 512])

        t_qT1 = Trk()
        t_kT1 = Trk()
        t_v1 = Trk()
        t_msqF = Trk()
        t_sqtB = Trk()
        t_mskB = Trk()
        t_ptT = Trk()
        t_ptm = Trk()
        t_eps1 = Trk()
        t_sdia = Trk()
        t_gloc = Trk()
        t_tot = Trk()
        t_den1 = [Trk(), Trk()]
        t_wg1 = Trk()
        t_tril = Trk()
        t_cb = Trk()
        t_cbb = Trk()
        t_scat = Trk()
        t_pden = Trk()
        t_pgd = Trk()
        t_dend = [Trk(), Trk()]
        t_kdia = Trk()
        t_vdia = Trk()
        t_yt = Trk()
        t_ytm = Trk()
        t_rcol = Trk()
        t_cold = [Trk(), Trk()]

        eop("vector", lambda v: v.memset(const_sb[:, 1:2], -GOFF),
            writes=[t_const])
        eop("vector", lambda v: v.memset(const_sb[:, 2:3], 1.0),
            writes=[t_const])
        eop("vector", lambda v: v.memset(const_sb[:, 3:4], LNC1),
            writes=[t_const])

        dop("d_wgl", wg1_sb[:, :], wg_dram.ap(), reads=[t_wg],
            writes=[t_wg1], extra=l1pre)
        dop("d_tril", tril_sb[:, :], tril_d.ap(), writes=[t_tril],
            extra=l1pre)
        dop("d_caus", cb_sb[:, :], caus_in, writes=[t_cb], extra=l1pre)
        eop("vector", lambda v: v.tensor_copy(cbb_sb[:, :], cb_sb[:, :]),
            reads=[t_cb], writes=[t_cbb])
        dop("d_bc1", scat_sb[:, :],
            bc_in.rearrange("o (c p) -> p (o c)", p=128),
            writes=[t_scat], extra=l1pre)
        eop("vector", lambda v: v.tensor_copy(scatb_sb[:, :], scat_sb[:, :]),
            reads=[t_scat], writes=[t_scat])
        dop("d_bc2", wg_dram.ap()[0:16, 0:128].rearrange("c p -> p c"),
            scatb_sb[:, :], reads=[t_scat], writes=[t_scat])
        dop("d_bc3", bout_sb[:, :],
            wg_dram.ap()[0:16, 0:128].rearrange("(o c) p -> o (c p)", o=1),
            reads=[t_scat], writes=[t_bout], extra=[(s_mm, s_mm.sched)])

        s1 = ExitStack()
        lnscr1_sb = sb(s1, "lnscr1", [128, 2 * DM], BF16)
        t_lnscr1 = [Trk(), Trk()]
        emit_layernorm(t_x, x_tile, lnT_sb, t_lnT, lnscr1_sb, t_lnscr1, l1pre)
        s1.close()

        def arena_colblock(arena_t, c0, width):
            return arena_t.ap().rearrange("(k p) c -> p k c", p=128)[
                :, :, c0:c0 + width]

        for ob in range(16):
            slot = ob % 2
            dd = wq_sb[:, slot * 4096: slot * 4096 + 2048].rearrange(
                "p (k c) -> p k c", k=16)
            dop(f"d_wq{slot}", dd, arena_colblock(wq1_arena, ob * 128, 128),
                reads=[t_wq1A], writes=[t_wq[slot]])
            for tc in range(2):
                ps = tc % 2
                pb = pAB[ps]
                for k in range(16):
                    eop("tensor", (lambda te, s=slot, k=k, tc=tc, pb=pb: te.matmul(
                        pb[:, :],
                        wq_sb[:, s * 4096 + k * 128: s * 4096 + (k + 1) * 128],
                        lnT_sb[:, k * TOK + tc * 512: k * TOK + (tc + 1) * 512],
                        start=(k == 0), stop=(k == 15))),
                        reads=([t_wq[slot], t_lnT] if k in (0, 15) else []),
                        writes=([t_pAB[ps]] if k in (0, 15) else []),
                        need_id=(k in (0, 15)))
                eop("scalar", (lambda s, ob=ob, tc=tc, pb=pb: s.copy(
                    qT1_sb[:, ob * TOK + tc * 512: ob * TOK + (tc + 1) * 512],
                    pb[:, :])), reads=[t_pAB[ps]], writes=[t_qT1])

        for ob in range(8):
            slot = ob % 2
            dd = wq_sb[:, slot * 4096: slot * 4096 + 2048].rearrange(
                "p (k c) -> p k c", k=16)
            dop(f"d_wq{slot}", dd, arena_colblock(wk1_arena, ob * 128, 128),
                reads=[t_wk1A], writes=[t_wq[slot]])
            for tc in range(2):
                ps = tc % 2
                pb = pAB[ps]
                for k in range(16):
                    eop("tensor", (lambda te, s=slot, k=k, tc=tc, pb=pb: te.matmul(
                        pb[:, :],
                        wq_sb[:, s * 4096 + k * 128: s * 4096 + (k + 1) * 128],
                        lnT_sb[:, k * TOK + tc * 512: k * TOK + (tc + 1) * 512],
                        start=(k == 0), stop=(k == 15))),
                        reads=([t_wq[slot], t_lnT] if k in (0, 15) else []),
                        writes=([t_pAB[ps]] if k in (0, 15) else []),
                        need_id=(k in (0, 15)))
                ss = tc % 2
                eop("scalar", (lambda s, ss=ss, pb=pb: s.copy(
                    kvst_sb[:, ss * 512:(ss + 1) * 512], pb[:, :])),
                    reads=[t_pAB[ps]], writes=[t_phat[ss]])
                dop(f"d_k1c{ss}",
                    kv1_c_k()[ob * 128:(ob + 1) * 128, tc * 512:(tc + 1) * 512],
                    kvst_sb[:, ss * 512:(ss + 1) * 512],
                    reads=[t_phat[ss]], writes=[t_kv1C])

        for oc in range(4):
            o0 = oc * 256
            slot = oc % 2
            dd = wq_sb[:, slot * 4096: slot * 4096 + 4096].rearrange(
                "p (k c) -> p k c", k=16)
            dop(f"d_wq{slot}", dd, arena_colblock(wv1_arena, o0, 256),
                reads=[t_wv1A], writes=[t_wq[slot]])
            for tt in range(8):
                ps = tt % 2
                pb = pAB[ps]
                for k in range(16):
                    eop("tensor", (lambda te, s=slot, k=k, tt=tt, pb=pb: te.matmul(
                        pb[:, 0:256],
                        lnT_sb[:, k * TOK + tt * 128: k * TOK + (tt + 1) * 128],
                        wq_sb[:, s * 4096 + k * 256: s * 4096 + (k + 1) * 256],
                        start=(k == 0), stop=(k == 15))),
                        reads=([t_wq[slot], t_lnT] if k in (0, 15) else []),
                        writes=([t_pAB[ps]] if k in (0, 15) else []),
                        need_id=(k in (0, 15)))
                ss = tt % 2
                eop("scalar", (lambda s, ss=ss, pb=pb: s.copy(
                    kvst_sb[:, ss * 512: ss * 512 + 256], pb[:, 0:256])),
                    reads=[t_pAB[ps]], writes=[t_phat[ss]])
                dop(f"d_v1c{ss}",
                    kv1_c_v()[tt * 128:(tt + 1) * 128, oc * 256:(oc + 1) * 256],
                    kvst_sb[:, ss * 512: ss * 512 + 256],
                    reads=[t_phat[ss]], writes=[t_kv1C])

        for tc in range(2):
            ps = tc % 2
            pb = pAB[ps]
            for k in range(16):
                eop("tensor", (lambda te, k=k, tc=tc, pb=pb: te.matmul(
                    pb[0:8, 0:512],
                    wg1_sb[:, k * 8:(k + 1) * 8],
                    lnT_sb[:, k * TOK + tc * 512: k * TOK + (tc + 1) * 512],
                    start=(k == 0), stop=(k == 15))),
                    reads=([t_wg1, t_lnT] if k in (0, 15) else []),
                    writes=([t_pAB[ps]] if k in (0, 15) else []),
                    need_id=(k in (0, 15)))
            eop("vector", (lambda v, tc=tc, pb=pb: v.tensor_copy(
                gloc_sb[0:8, tc * 512:(tc + 1) * 512], pb[0:8, 0:512])),
                reads=[t_pAB[ps]], writes=[t_gloc])
        dop("d_g1c", kv1_c_g(), gloc_sb[:, :].bitcast(BF16),
            reads=[t_gloc], writes=[t_kv1C])

        agw = all_store_waits(["d_k1c0", "d_k1c1", "d_v1c0", "d_v1c1", "d_g1c"])
        id_agkv1 = p.op("gpsimd", lambda g: g.collective_compute(
            "AllGather", ALU.bypass,
            replica_groups=[[0, 1, 2, 3], [4, 5, 6, 7]],
            ins=[kv1_contrib.ap().opt()], outs=[kv1_gather.ap().opt()]),
            waits=agw, sem=s_cc)
        t_kv1G.wid = id_agkv1
        t_vrep1 = [Trk(), Trk()]
        vrep1_v = vrep1_dram.ap().rearrange("(hp) (j d) -> hp j d", j=32)
        for hh in range(8):
            for rr in range(4):
                vs = kv1_g_v(rr).rearrange("(j p) d -> p j d", p=128)[
                    :, :, hh * 128:(hh + 1) * 128]
                dop(f"d_vrp{hh % 2}",
                    vrep1_v[hh * 128:(hh + 1) * 128, rr * 8:(rr + 1) * 8, :],
                    vs, reads=[t_kv1G], writes=[t_vrep1[hh % 2]])

        mark('l1proj')
        s2 = ExitStack()
        gch_sb = sb(s2, "gch", [8, 1024], F32)
        pg_sb = sb(s2, "pgs", [8, 1024], F32)
        hi_sb = pg_sb[:, :].bitcast(BF16)[0:8, 0:1024]
        lo_sb = pg_sb[:, :].bitcast(BF16)[0:8, 1024:2048]
        mr_sb = sb(s2, "mr", [8, 4], F32)
        t_gch = Trk()
        t_pg = Trk()
        t_hi = t_pg
        t_lo = t_pg
        t_mr = Trk()
        eop("vector", lambda v: v.memset(tot_sb[:, 4:5], 0.0), writes=[t_tot],
            extra=l1pre)
        dop("d_mr", mr_sb[:, :], maskR_in, writes=[t_mr], extra=l1pre)

        def gate_chunk(load_fn, dst_sb, t_dst, base_j):
            load_fn()
            eop("scalar", lambda s: s.activation(
                pg_sb[0:8, :], gch_sb[0:8, :], AF.Exp, scale=-1.0,
                bias=const_sb[0:8, 1:2]),
                reads=[t_gch, t_const], writes=[t_pg])
            eop("scalar", lambda s: s.activation(
                pg_sb[0:8, :], pg_sb[0:8, :], AF.Ln,
                bias=const_sb[0:8, 2:3]),
                reads=[t_const], writes=[t_pg])
            eop("vector", lambda v: v.tensor_tensor_scan(
                gch_sb[0:8, :], pg_sb[0:8, :], pg_sb[0:8, :], 0.0,
                ALU.add, ALU.bypass),
                reads=[t_pg], writes=[t_gch])
            eop("vector", lambda v: v.tensor_scalar(
                gch_sb[0:8, :], gch_sb[0:8, :], tot_sb[0:8, 4:5], None,
                ALU.add), reads=[t_tot], writes=[t_gch])
            eop("vector", lambda v: v.tensor_copy(hi_sb, gch_sb[0:8, :]),
                reads=[t_gch], writes=[t_hi])
            eop("vector", lambda v: v.tensor_tensor(
                lo_sb, gch_sb[0:8, :], hi_sb, ALU.subtract),
                reads=[t_gch, t_hi], writes=[t_lo])
            for jl in range(8):
                eop("tensor", (lambda te, jl=jl: te.transpose(
                    ptrA[0:128, 0:8], hi_sb[0:8, jl * 128:(jl + 1) * 128],
                    eyeb[0:8, 0:8])),
                    reads=[t_hi, t_eye], writes=[t_ptr[0]])
                eop("scalar", lambda s: s.copy(trsp_sb[:, 0:8],
                                               ptrA[0:128, 0:8]),
                    reads=[t_ptr[0]], writes=[t_trsp[0]])
                eop("tensor", (lambda te, jl=jl: te.transpose(
                    ptrB[0:128, 0:8], lo_sb[0:8, jl * 128:(jl + 1) * 128],
                    eyeb[0:8, 0:8])),
                    reads=[t_lo, t_eye], writes=[t_ptr[1]])
                eop("vector", (lambda v, jl=jl: v.tensor_tensor(
                    dst_sb[:, (base_j + jl) * 8:(base_j + jl + 1) * 8],
                    trsp_sb[:, 0:8], ptrB[0:128, 0:8], ALU.add)),
                    reads=[t_trsp[0], t_ptr[1]], writes=[t_dst])

        for rr in range(4):
            def load_rr(rr=rr):
                dop("d_g1l", gch_sb[:, :].bitcast(BF16), kv1_g_g(rr),
                    reads=[t_kv1G], writes=[t_gch])
            gate_chunk(load_rr, ptT_sb, t_ptT, rr * 8)
            eop("vector", (lambda v, rr=rr: v.tensor_copy(
                tot_sb[0:8, rr:rr + 1], gch_sb[0:8, 1023:1024])),
                reads=[t_gch], writes=[t_tot])
            eop("vector", (lambda v, rr=rr: v.tensor_copy(
                tot_sb[0:8, 4:5], tot_sb[0:8, rr:rr + 1])),
                writes=[t_tot])

        eop("vector", lambda v: v.tensor_tensor(
            tot_sb[0:8, 8:12], tot_sb[0:8, 0:4], mr_sb[0:8, :], ALU.mult),
            reads=[t_mr], writes=[t_tot])
        eop("vector", lambda v: v.reduce_sum(
            tot_sb[0:8, 5:6], tot_sb[0:8, 8:12], AX.X), writes=[t_tot])
        eop("vector", lambda v: v.tensor_copy(gch_sb[0:8, :], gloc_sb[0:8, :]),
            reads=[t_gloc], writes=[t_gch])
        eop("scalar", lambda s: s.activation(
            pg_sb[0:8, :], gch_sb[0:8, :], AF.Exp, scale=-1.0,
            bias=const_sb[0:8, 1:2]),
            reads=[t_gch, t_const], writes=[t_pg])
        eop("scalar", lambda s: s.activation(
            pg_sb[0:8, :], pg_sb[0:8, :], AF.Ln, bias=const_sb[0:8, 2:3]),
            reads=[t_const], writes=[t_pg])
        eop("vector", lambda v: v.tensor_tensor_scan(
            gch_sb[0:8, :], pg_sb[0:8, :], pg_sb[0:8, :], 0.0,
            ALU.add, ALU.bypass),
            reads=[t_pg], writes=[t_gch])
        eop("vector", lambda v: v.tensor_scalar(
            gch_sb[0:8, :], gch_sb[0:8, :], tot_sb[0:8, 5:6], None, ALU.add),
            reads=[t_tot], writes=[t_gch])
        eop("vector", lambda v: v.tensor_copy(hi_sb, gch_sb[0:8, :]),
            reads=[t_gch], writes=[t_hi])
        eop("vector", lambda v: v.tensor_tensor(
            lo_sb, gch_sb[0:8, :], hi_sb, ALU.subtract),
            reads=[t_gch, t_hi], writes=[t_lo])
        for jl in range(8):
            eop("tensor", (lambda te, jl=jl: te.transpose(
                ptrA[0:128, 0:8], hi_sb[0:8, jl * 128:(jl + 1) * 128],
                eyeb[0:8, 0:8])),
                reads=[t_hi, t_eye], writes=[t_ptr[0]])
            eop("scalar", lambda s: s.copy(trsp_sb[:, 0:8], ptrA[0:128, 0:8]),
                reads=[t_ptr[0]], writes=[t_trsp[0]])
            eop("tensor", (lambda te, jl=jl: te.transpose(
                ptrB[0:128, 0:8], lo_sb[0:8, jl * 128:(jl + 1) * 128],
                eyeb[0:8, 0:8])),
                reads=[t_lo, t_eye], writes=[t_ptr[1]])
            eop("vector", (lambda v, jl=jl: v.tensor_tensor(
                ptm_sb[:, jl * 8:(jl + 1) * 8],
                trsp_sb[:, 0:8], ptrB[0:128, 0:8], ALU.add)),
                reads=[t_trsp[0], t_ptr[1]], writes=[t_ptm])
        s2.close()

        eop("scalar", lambda s: s.activation(
            eps_sb[:, :], ptm_sb[:, :], AF.Exp, bias=const_sb[:, 3:4]),
            reads=[t_ptm, t_const], writes=[t_eps1])
        eop("scalar", lambda s: s.activation(
            sdia_sb[:, :], ptm_sb[:, :], AF.Exp, scale=0.5),
            reads=[t_ptm], writes=[t_sdia])
        eop("scalar", lambda s: s.activation(
            sqtB_sb[:, :], ptT_sb[:, :], AF.Exp, scale=0.5),
            reads=[t_ptT], writes=[t_sqtB])
        eop("tensor", lambda te: te.matmul(
            pA[:, 0:256], onesq_sb[:, :], cbb_sb[:, :], start=True, stop=True),
            reads=[t_cbb, t_ones], writes=[t_pAB[0]])
        eop("vector", lambda v: v.tensor_copy(mskB_sb[:, :], pA[:, 0:256]),
            reads=[t_pAB[0]], writes=[t_mskB])
        for qc in range(2):
            mv = mskB_sb[:, qc * 32:(qc + 1) * 32].rearrange(
                "p (j o) -> p j o", o=1).broadcast_to([128, 32, 8])
            eop("vector", (lambda v, qc=qc, mv=mv: v.tensor_tensor(
                msqF_sb[:, qc * 256:(qc + 1) * 256].rearrange(
                    "p (j h) -> p j h", j=32),
                sqtB_sb[:, :].rearrange("p (j h) -> p j h", j=32),
                mv, ALU.mult)),
                reads=[t_sqtB, t_mskB], writes=[t_msqF])

        mark('l1gate')
        den_sb = sb(att1, "den1", [1, 2 * 512], F32)
        kdia_sb = sb(att1, "kdia", [128, 1024], BF16)
        vdia_sb = sb(att1, "vdia", [128, 1024], BF16)
        yt_sb = sb(att1, "yt", [128, 512], BF16)
        ytm_sb = sb(att1, "ytm", [128, 128], BF16)
        rcol_sb = sb(att1, "rcol", [128, 8], F32)
        cold_sb = sb(att1, "cold", [128, 2 * 4], F32)
        g_done = [(s_ve, s_ve.sched), (s_ev, s_ev.sched), (s_mm, s_mm.sched)]

        for h in range(8):
            for r in range(4):
                dop(f"d_kT1{r}", kT1_sb[:, r * TOK:(r + 1) * TOK],
                    kv1_g_k(r)[h * 128:(h + 1) * 128, :],
                    reads=[t_kv1G], writes=[t_kT1],
                    extra=(g_done if h == 0 else []))
                if r == 0:
                    dop("d_v10", v1_sb[:, :],
                        vrep1_dram.ap()[h * 128:(h + 1) * 128, :],
                        reads=[t_vrep1[h % 2]], writes=[t_v1],
                        extra=(g_done if h == 0 else []))
            dop("d_kd1", kdia_sb[:, :], kv1_c_k()[h * 128:(h + 1) * 128, :],
                reads=[t_kv1C], writes=[t_kdia],
                extra=(g_done if h == 0 else []))
            vdd = vdia_sb[:, :].rearrange("q (j d) -> q j d", j=8)
            vds = kv1_c_v().rearrange("(j q) d -> q j d", q=128)[
                :, :, h * 128:(h + 1) * 128]
            dop("d_vd1", vdd, vds, reads=[t_kv1C], writes=[t_vdia],
                extra=(g_done if h == 0 else []))
            kv_waits = [(dma_sems[f"d_kT1{r}"], dma_sems[f"d_kT1{r}"].sched)
                        for r in range(4)]
            kv_waits += [(dma_sems["d_v10"], dma_sems["d_v10"].sched)]
            kv_waits += [(dma_sems["d_kd1"], dma_sems["d_kd1"].sched),
                         (dma_sems["d_vd1"], dma_sems["d_vd1"].sched)]

            for hqh in range(2):
                hq = 2 * h + hqh
                for qc in range(2):
                    def pv1(j):
                        sl = j % 2
                        eop("tensor", (lambda te, j=j, sl=sl: te.matmul(
                            pattn[:, :],
                            v1_sb[:, j * 128:(j + 1) * 128],
                            phat_sb[:, sl * 512:(sl + 1) * 512],
                            start=(j == 0), stop=False)),
                            reads=([t_v1, t_phat[sl]]),
                            writes=([t_pattn] if j == 0 else []))
                        eop("tensor", (lambda te, j=j, sl=sl: te.matmul(
                            pden[0:1, 0:512],
                            onescol_sb[:, :],
                            phat_sb[:, sl * 512:(sl + 1) * 512],
                            start=(j == 0), stop=False)),
                            reads=([t_onesc, t_phat[sl]]),
                            writes=([t_pden] if j == 0 else []))
                    for j in range(28):
                        ps = j % 2
                        pb = pAB[ps]
                        col = qc * 256 + j * 8 + h
                        eop("tensor", (lambda te, hq=hq, qc=qc, j=j, pb=pb:
                            te.matmul(
                                pb[:, :],
                                kT1_sb[:, j * 128:(j + 1) * 128],
                                qT1_sb[:, hq * TOK + qc * 512:
                                       hq * TOK + (qc + 1) * 512],
                                start=True, stop=True)),
                            reads=([t_kT1, t_qT1] if j in (0, 27) else []),
                            writes=[t_pAB[ps]],
                            extra=(kv_waits if (hqh == 0 and qc == 0 and j == 0)
                                   else []))
                        sl = j % 2
                        eop("scalar", (lambda s, sl=sl, col=col, pb=pb:
                            s.activation(
                                phat_sb[:, sl * 512:(sl + 1) * 512],
                                pb[:, :],
                                AF.Square, scale=msqF_sb[:, col:col + 1])),
                            reads=[t_pAB[ps], t_msqF], writes=[t_phat[sl]])
                        if j > 0:
                            pv1(j - 1)
                    pv1(27)
                    for qtl in range(4):
                        qt = qc * 4 + qtl
                        for w in range(qtl + 1):
                            lb = qc * 4 + w
                            ps = w % 2
                            pb = pAB[ps]
                            eop("tensor", (lambda te, hq=hq, qt=qt, lb=lb,
                                           pb=pb: te.matmul(
                                pb[:, 0:128],
                                kdia_sb[:, lb * 128:(lb + 1) * 128],
                                qT1_sb[:, hq * TOK + qt * 128:
                                       hq * TOK + (qt + 1) * 128],
                                start=True, stop=True)),
                                reads=[t_kdia, t_qT1], writes=[t_pAB[ps]])
                            sl = w % 2
                            dslice = phat_sb[:, sl * 512: sl * 512 + 128]
                            eop("scalar", (lambda s, pb=pb, lb=lb, h=h,
                                           dslice=dslice: s.activation(
                                dslice, pb[:, 0:128], AF.Square,
                                scale=sdia_sb[:, lb * 8 + h: lb * 8 + h + 1])),
                                reads=[t_pAB[ps], t_sdia], writes=[t_phat[sl]])
                            if w == qtl:
                                eop("vector", (lambda v, dslice=dslice:
                                    v.tensor_tensor(
                                        dslice, dslice, tril_sb[:, :],
                                        ALU.mult)),
                                    reads=[t_tril], writes=[t_phat[sl]])
                            eop("tensor", (lambda te, lb=lb, qtl=qtl, w=w,
                                           dslice=dslice: te.matmul(
                                pattn[:, qtl * 128:(qtl + 1) * 128],
                                vdia_sb[:, lb * 128:(lb + 1) * 128],
                                dslice, start=False, stop=(w == qtl))),
                                reads=[t_vdia, t_phat[sl]], writes=[t_pattn])
                            eop("tensor", (lambda te, qtl=qtl, w=w,
                                           dslice=dslice: te.matmul(
                                pden[0:1, qtl * 128:(qtl + 1) * 128],
                                onescol_sb[:, :], dslice,
                                start=False, stop=(w == qtl))),
                                reads=[t_onesc, t_phat[sl]], writes=[t_pden])
                    dsl = (hq * 2 + qc) % 2
                    eop("vector", (lambda v, dsl=dsl: v.tensor_copy(
                        den_sb[0:1, dsl * 512:(dsl + 1) * 512], pden[0:1, :])),
                        reads=[t_pden], writes=[t_den1[dsl]])
                    dop(f"d_den{dsl}",
                        pg_dram.ap()[dsl:dsl + 1, 0:512],
                        den_sb[0:1, dsl * 512:(dsl + 1) * 512],
                        reads=[t_den1[dsl]], writes=[t_dend[dsl]])
                    dop(f"d_cold{dsl}", cold_sb[:, dsl * 4:(dsl + 1) * 4],
                        pg_dram.ap()[dsl:dsl + 1, 0:512].rearrange(
                            "o (q p) -> p (o q)", p=128),
                        reads=[t_dend[dsl]], writes=[t_cold[dsl]])
                    for qtl in range(4):
                        qt = qc * 4 + qtl
                        eop("vector", (lambda v, qtl=qtl, qt=qt, h=h, dsl=dsl:
                            v.tensor_tensor(
                                rcol_sb[:, qtl:qtl + 1],
                                cold_sb[:, dsl * 4 + qtl:dsl * 4 + qtl + 1],
                                eps_sb[:, qt * 8 + h: qt * 8 + h + 1],
                                ALU.add)),
                            reads=[t_cold[dsl], t_eps1], writes=[t_rcol])
                        eop("vector", (lambda v, qtl=qtl: v.reciprocal(
                            rcol_sb[:, qtl:qtl + 1],
                            rcol_sb[:, qtl:qtl + 1])),
                            writes=[t_rcol])
                    eop("scalar", lambda s: s.copy(yt_sb[:, :], pattn[:, :]),
                        reads=[t_pattn], writes=[t_yt])
                    for qtl in range(4):
                        ts = qtl % 2
                        eop("tensor", (lambda te, qtl=qtl, ts=ts: te.transpose(
                            ptr[ts][:, :], yt_sb[:, qtl * 128:(qtl + 1) * 128],
                            eyeb[:, :])),
                            reads=[t_yt, t_eye], writes=[t_ptr[ts]])
                        eop("scalar", (lambda s, qtl=qtl, ts=ts: s.activation(
                            ytm_sb[:, :], ptr[ts][:, :], AF.Copy,
                            scale=rcol_sb[:, qtl:qtl + 1])),
                            reads=[t_ptr[ts], t_rcol], writes=[t_ytm])
                        eop("tensor", (lambda te, ts=ts: te.transpose(
                            ptr[ts][:, :], ytm_sb[:, :], eyeb[:, :])),
                            reads=[t_ytm, t_eye], writes=[t_ptr[ts]])
                        eop("scalar", (lambda s, hq=hq, qc=qc, qtl=qtl, ts=ts:
                            s.copy(
                                lnT_sb[:, hq * TOK + qc * 512 + qtl * 128:
                                       hq * TOK + qc * 512 + (qtl + 1) * 128],
                                ptr[ts][:, :])),
                            reads=[t_ptr[ts]], writes=[t_lnT])
        mark('l1attn')
        att1.close()

        for oc in range(8):
            slot = oc % 2
            dd = wq_sb[:, slot * 4096: slot * 4096 + 4096].rearrange(
                "p (k c) -> p k c", k=16)
            dop(f"d_wq{slot}", dd, arena_colblock(wc1_arena, oc * 256, 256),
                reads=[t_wc1A], writes=[t_wq[slot]])
            for qt in range(8):
                ps = qt % 2
                pb = pAB[ps]
                for k in range(16):
                    eop("tensor", (lambda te, s=slot, k=k, qt=qt, pb=pb: te.matmul(
                        pb[:, 0:256],
                        lnT_sb[:, k * TOK + qt * 128: k * TOK + (qt + 1) * 128],
                        wq_sb[:, s * 4096 + k * 256: s * 4096 + (k + 1) * 256],
                        start=(k == 0), stop=False)),
                        reads=([t_wq[slot], t_lnT] if k == 0 else []),
                        writes=([t_pAB[ps]] if k == 0 else []), need_id=(k == 0))
                eop("tensor", (lambda te, oc=oc, pb=pb: te.matmul(
                    pb[:, 0:256], onesq_sb[:, :],
                    bout_sb[0:1, oc * 256:(oc + 1) * 256],
                    start=False, stop=True)),
                    reads=[t_ones, t_bout, t_lnT, t_wq[slot]],
                    writes=[t_pAB[ps]])
                eop("vector", (lambda v, qt=qt, oc=oc, pb=pb: v.tensor_tensor(
                    x_sb[:, qt * DM + oc * 256: qt * DM + (oc + 1) * 256],
                    x_sb[:, qt * DM + oc * 256: qt * DM + (oc + 1) * 256],
                    pb[:, 0:256], ALU.add)),
                    reads=[t_pAB[ps]], writes=[t_x[qt]])


        mark('l1outproj')
        l1f_pre = [(s_mm, s_mm.sched), (s_ev, s_ev.sched), (s_ve, s_ve.sched)]
        l1f = ExitStack()
        c12_sb = sb(l1f, "c12", [128, 24], F32)
        mcol_sb = sb(l1f, "mcol", [128, 2], F32)
        w2r_sb = sb(l1f, "w2r", [128, 2 * 1024], BF16)
        evt_sb = sb(l1f, "evt", [128, 512], F32)
        t_c12 = Trk()
        t_mcol = Trk()
        t_w2r = [Trk(), Trk()]
        t_evt = [Trk(), Trk()]

        for mi in range(2):
            hv = arst_sb[0:1, 11 + mi:12 + mi].bitcast(BF16)
            eop("vector", (lambda v, mi=mi, hv=hv: v.tensor_copy(
                hv[0:1, 0:1], arst_sb[0:1, 5 + 2 * mi:6 + 2 * mi])),
                reads=[t_arst], writes=[t_arst], extra=l1f_pre)
            eop("vector", (lambda v, mi=mi, hv=hv: v.tensor_tensor(
                hv[0:1, 1:2], arst_sb[0:1, 5 + 2 * mi:6 + 2 * mi],
                hv[0:1, 0:1], ALU.subtract)), writes=[t_arst])
            eop("tensor", (lambda te, hv=hv: te.matmul(
                pA[:, 0:1], onesq_sb[:, :], hv[0:1, 0:1],
                start=True, stop=False)),
                reads=[t_arst, t_ones], writes=[t_pAB[0]], extra=l1f_pre)
            eop("tensor", (lambda te, hv=hv: te.matmul(
                pA[:, 0:1], onesq_sb[:, :], hv[0:1, 1:2],
                start=False, stop=True)),
                reads=[t_arst, t_ones], writes=[t_pAB[0]])
            eop("vector", (lambda v, mi=mi: v.tensor_copy(
                mcol_sb[:, mi:mi + 1], pA[:, 0:1])),
                reads=[t_pAB[0]], writes=[t_mcol])

        ph1 = ExitStack()
        ln3_sb = sb(ph1, "ln3", [128, 2 * 2048], F32)
        boxq_sb = sb(ph1, "boxq", [128, 2048], I32)
        k1b_sb = sb(ph1, "k1b", [128, 2048], BF16)
        t_ln3 = [Trk(), Trk()]
        t_boxq = Trk()
        t_k1b = Trk()
        boxq_f = boxq_sb[:, :].bitcast(F32)

        for t in range(8):
            eop("vector", (lambda v, t=t: v.reduce_sum(
                stat_sb[:, t:t + 1], x_tile(t), AX.X)),
                reads=[t_x[t]], writes=[t_stat], extra=l1f_pre)
            eop("scalar", (lambda s, t=t: s.activation(
                pP1[:, :], x_tile(t)[:, 0:1024], AF.Square,
                accum_out=stat_sb[:, 8 + t:9 + t])),
                reads=[t_x[t]], writes=[t_pP1, t_stat], extra=l1f_pre)
            eop("scalar", (lambda s, t=t: s.activation(
                pP1[:, :], x_tile(t)[:, 1024:2048], AF.Square,
                accum_out=stat_sb[:, 16 + t:17 + t])),
                reads=[t_x[t]], writes=[t_pP1, t_stat])
        eop("vector", lambda v: v.tensor_scalar(
            stat_sb[:, 24:32], stat_sb[:, 0:8], 1.0 / DM, None, ALU.mult),
            reads=[t_stat], writes=[t_stat])
        eop("vector", lambda v: v.tensor_tensor(
            stat_sb[:, 32:40], stat_sb[:, 24:32], stat_sb[:, 24:32],
            ALU.mult), writes=[t_stat])
        eop("vector", lambda v: v.tensor_tensor(
            stat_sb[:, 40:48], stat_sb[:, 8:16], stat_sb[:, 16:24],
            ALU.add), writes=[t_stat])
        eop("vector", lambda v: v.tensor_scalar(
            stat_sb[:, 48:56], stat_sb[:, 40:48], 1.0 / DM, None, ALU.mult),
            writes=[t_stat])
        eop("vector", lambda v: v.tensor_tensor(
            stat_sb[:, 40:48], stat_sb[:, 48:56], stat_sb[:, 32:40],
            ALU.subtract), writes=[t_stat])
        eop("scalar", lambda s: s.activation(
            stat_sb[:, 48:56], stat_sb[:, 40:48], AF.Sqrt,
            bias=const_sb[:, 0:1]),
            reads=[t_const], writes=[t_stat])
        eop("vector", lambda v: v.reciprocal(
            stat_sb[:, 56:64], stat_sb[:, 48:56]), writes=[t_stat])
        for t in range(8):
            l = t % 2
            eop("vector", (lambda v, t=t, l=l: v.tensor_scalar(
                ln3_sb[:, l * DM:(l + 1) * DM], x_tile(t),
                stat_sb[:, 24 + t:25 + t], stat_sb[:, 56 + t:57 + t],
                ALU.subtract, ALU.mult)),
                reads=[t_x[t], t_stat], writes=[t_ln3[l]], extra=l1f_pre)
            eop("vector", (lambda v, t=t, l=l: v.tensor_reduce(
                c12_sb[:, 16:17], ln3_sb[:, l * DM:(l + 1) * DM], AX.X,
                ALU.max, apply_absolute_value=True)),
                reads=[t_ln3[l]], writes=[t_c12])
            eop("vector", lambda v: v.tensor_scalar(
                c12_sb[:, 16:17], c12_sb[:, 16:17], 1e-5, None, ALU.max),
                writes=[t_c12])
            eop("vector", lambda v: v.reciprocal(
                c12_sb[:, 17:18], c12_sb[:, 16:17]), writes=[t_c12])
            eop("vector", (lambda v, t=t: v.tensor_scalar(
                c12_sb[:, 18:19], c12_sb[:, 17:18], 127.0, None, ALU.mult)),
                writes=[t_c12])
            eop("vector", (lambda v, t=t: v.tensor_scalar(
                c12_sb[:, t:t + 1], c12_sb[:, 16:17], mcol_sb[:, 0:1],
                1.0 / 127.0, ALU.mult, ALU.mult)),
                reads=[t_mcol], writes=[t_c12])
            eop("vector", (lambda v, l=l: v.tensor_scalar(
                boxq_f, ln3_sb[:, l * DM:(l + 1) * DM], c12_sb[:, 18:19],
                MAGIC, ALU.mult, ALU.add)),
                reads=[t_ln3[l], t_c12], writes=[t_boxq])
            eop("vector", lambda v: v.tensor_scalar(
                k1b_sb[:, :], boxq_f, MAGIC, None, ALU.subtract),
                reads=[t_boxq], writes=[t_k1b])
            for c in range(16):
                ts_ = c % 2
                eop("tensor", (lambda te, c=c, ts_=ts_: te.transpose(
                    ptr[ts_][:, :], k1b_sb[:, c * 128:(c + 1) * 128],
                    eyeb[:, :])),
                    reads=[t_k1b, t_eye], writes=[t_ptr[ts_]])
                eop("scalar", (lambda s, t=t, c=c, ts_=ts_: s.copy(
                    lnT_sb[:, c * TOK + t * 128: c * TOK + (t + 1) * 128],
                    ptr[ts_][:, :])),
                    reads=[t_ptr[ts_]], writes=[t_lnT],
                    extra=(l1f_pre if (t == 0 and c == 0) else []))
        ph1.close()

        ph2 = ExitStack()
        h1_sb = sb(ph2, "h1", [128, 2 * 8192], BF16)
        k2T_sb = sb(ph2, "k2T", [128, 2 * 8192], BF16)
        k2b_sb = sb(ph2, "k2b", [128, 512], BF16)
        box2_sb = sb(ph2, "box2", [128, 512], I32)
        t_h1 = [Trk(), Trk()]
        t_k2T = [Trk(), Trk()]
        t_k2b = [Trk(), Trk()]
        t_box2 = Trk()
        box2_f = box2_sb[:, :].bitcast(F32)

        def w1e_colblock(q):
            rank, c0 = q // 4, (q % 4) * 256
            return w1e_arena.ap()[rank].rearrange("(k p) c -> p k c", p=128)[
                :, :, c0:c0 + 256]

        accs2 = [(pA[:, :], t_pAB[0]), (pB[:, :], t_pAB[1]),
                 (pP1[:, 0:512], t_pP1), (pP1[:, 512:1024], t_pP1)]

        for g in range(4):
            up_ps1 = [(pA[:, 0:256], t_pAB[0]), (pB[:, 0:256], t_pAB[1]),
                      (pP1[:, 0:256], t_pP1), (pP1[:, 512:768], t_pP1)]

            def evac_q(q, g=g):
                for tl in range(2):
                    t = 2 * g + tl
                    pb, tp = up_ps1[(q % 2) * 2 + tl]
                    eop("scalar", (lambda s, q=q, t=t, tl=tl, pb=pb:
                        s.activation(
                            h1_sb[:, tl * 8192 + q * 256:
                                  tl * 8192 + (q + 1) * 256], pb,
                            AF.Silu, scale=c12_sb[:, t:t + 1])),
                        reads=[tp, t_c12], writes=[t_h1[tl]],
                        extra=(l1f_pre if (g == 0 and tl == 0 and q == 0)
                               else []))
            for q in range(32):
                slot = q % 2
                dd = wq_sb[:, slot * 4096: slot * 4096 + 4096].rearrange(
                    "p (k c) -> p k c", k=16)
                dop(f"d_wq{slot}", dd, w1e_colblock(q),
                    reads=[t_w1eA], writes=[t_wq[slot]])
                for tl in range(2):
                    t = 2 * g + tl
                    pb, tp = up_ps1[(q % 2) * 2 + tl]
                    for k in range(16):
                        eop("tensor", (lambda te, s=slot, k=k, t=t, pb=pb:
                            te.matmul(
                                pb,
                                lnT_sb[:, k * TOK + t * 128:
                                       k * TOK + (t + 1) * 128],
                                wq_sb[:, s * 4096 + k * 256:
                                      s * 4096 + (k + 1) * 256],
                                start=(k == 0), stop=(k == 15))),
                            reads=([t_wq[slot], t_lnT] if k in (0, 15) else []),
                            writes=([tp] if k in (0, 15) else []),
                            need_id=(k in (0, 15)))
                if q > 0:
                    evac_q(q - 1)
            evac_q(31)
            for tl in range(2):
                t = 2 * g + tl
                eop("vector", (lambda v, tl=tl: v.tensor_reduce(
                    c12_sb[:, 19:20], h1_sb[:, tl * 8192:(tl + 1) * 8192],
                    AX.X, ALU.max,
                    apply_absolute_value=True)),
                    reads=[t_h1[tl]], writes=[t_c12])
                eop("vector", lambda v: v.tensor_scalar(
                    c12_sb[:, 19:20], c12_sb[:, 19:20], 1e-5, None, ALU.max),
                    writes=[t_c12])
                eop("vector", lambda v: v.reciprocal(
                    c12_sb[:, 20:21], c12_sb[:, 19:20]), writes=[t_c12])
                eop("vector", lambda v: v.tensor_scalar(
                    c12_sb[:, 21:22], c12_sb[:, 20:21], 127.0, None, ALU.mult),
                    writes=[t_c12])
                eop("vector", (lambda v, t=t: v.tensor_scalar(
                    c12_sb[:, 8 + t:9 + t], c12_sb[:, 19:20], mcol_sb[:, 1:2],
                    1.0 / 127.0, ALU.mult, ALU.mult)),
                    reads=[t_mcol], writes=[t_c12])
                for ch in range(16):
                    eop("vector", (lambda v, ch=ch, tl=tl: v.tensor_scalar(
                        box2_f, h1_sb[:, tl * 8192 + ch * 512:
                                      tl * 8192 + (ch + 1) * 512],
                        c12_sb[:, 21:22], MAGIC, ALU.mult, ALU.add)),
                        reads=[t_h1[tl], t_c12], writes=[t_box2])
                    kslot = 0
                    eop("vector", lambda v: v.tensor_scalar(
                        k2b_sb[:, 0:512], box2_f,
                        MAGIC, None, ALU.subtract),
                        reads=[t_box2], writes=[t_k2b[0]])
                    for c4 in range(4):
                        fcc = ch * 4 + c4
                        ts_ = c4 % 2
                        eop("tensor", (lambda te, c4=c4, ts_=ts_:
                            te.transpose(
                                ptr[ts_][:, :],
                                k2b_sb[:, c4 * 128:(c4 + 1) * 128],
                                eyeb[:, :])),
                            reads=[t_k2b[0], t_eye], writes=[t_ptr[ts_]])
                        eop("scalar", (lambda s, tl=tl, fcc=fcc, ts_=ts_:
                            s.copy(
                                k2T_sb[:, tl * 8192 + fcc * 128:
                                       tl * 8192 + (fcc + 1) * 128],
                                ptr[ts_][:, :])),
                            reads=[t_ptr[ts_]], writes=[t_k2T[tl]])
            for dcp in range(2):
                for fc in range(64):
                    ws = fc % 2
                    dop(f"d_w2r{ws}", w2r_sb[:, ws * 1024:(ws + 1) * 1024],
                        w2e_arena.ap()[fc * 128:(fc + 1) * 128,
                                       dcp * 1024:(dcp + 1) * 1024],
                        reads=[t_w2eA], writes=[t_w2r[ws]], extra=l1f_pre)
                    for tl in range(2):
                        for dc2 in range(2):
                            pb, tp = accs2[tl * 2 + dc2]
                            eop("tensor", (lambda te, fc=fc, tl=tl, dc2=dc2,
                                           ws=ws, pb=pb: te.matmul(
                                pb,
                                k2T_sb[:, tl * 8192 + fc * 128:
                                       tl * 8192 + (fc + 1) * 128],
                                w2r_sb[:, ws * 1024 + dc2 * 512:
                                       ws * 1024 + (dc2 + 1) * 512],
                                start=(fc == 0), stop=(fc == 63))),
                                reads=[t_k2T[tl], t_w2r[ws]],
                                writes=([tp] if fc in (0, 63) else []))
                for tl in range(2):
                    t = 2 * g + tl
                    for dc2 in range(2):
                        pb, tp = accs2[tl * 2 + dc2]
                        eop("scalar", (lambda s, pb=pb, t=t:
                            s.activation(
                                evt_sb[:, :], pb,
                                AF.Copy, scale=c12_sb[:, 8 + t:9 + t])),
                            reads=[tp, t_c12], writes=[t_evt[0]])
                        dcg = dcp * 1024 + dc2 * 512
                        eop("vector", (lambda v, t=t, dcg=dcg:
                            v.tensor_tensor(
                                x_sb[:, t * DM + dcg: t * DM + dcg + 512],
                                x_sb[:, t * DM + dcg: t * DM + dcg + 512],
                                evt_sb[:, :], ALU.add)),
                            reads=[t_evt[0]], writes=[t_x[t]])
        ph2.close()
        l1f.close()

        mark('l1ffn')
        fin_ids = []
        for t in range(8):
            fin_ids.append(dop("d_out", out_x[t * 128:(t + 1) * 128, :],
                               x_sb[:, t * DM:(t + 1) * DM], reads=[t_x[t]]))
        p.op("sync", lambda e: e.nop(),
             waits=[(dma_sems["d_out"], dma_sems["d_out"].sched)])

        p.emit(nc, ctx)
    return nc


def shard_inputs(inputs):
    x = np.ascontiguousarray(np.asarray(inputs["L0_attn_x"]))
    wqkv = np.ascontiguousarray(np.asarray(inputs["L0_attn_w_qkv"]))
    wout = np.ascontiguousarray(np.asarray(inputs["L0_attn_w_out"]))
    bout = np.ascontiguousarray(np.asarray(inputs["L0_attn_b_out"]).reshape(1, DM))
    w1 = np.asarray(inputs["L0_ffn_weight1"])
    w2 = np.asarray(inputs["L0_ffn_weight2"])
    w3 = np.asarray(inputs["L0_ffn_weight3"])
    wq1 = np.ascontiguousarray(np.asarray(inputs["L1_attn_Wq"]))
    wc1 = np.ascontiguousarray(np.asarray(inputs["L1_attn_Wc"]))
    wk1 = np.ascontiguousarray(np.asarray(inputs["L1_attn_Wk"]))
    wv1 = np.ascontiguousarray(np.asarray(inputs["L1_attn_Wv"]))
    wg1 = np.ascontiguousarray(np.asarray(inputs["L1_attn_Wg"]))
    bc1 = np.ascontiguousarray(np.asarray(inputs["L1_attn_bc"]).reshape(1, DM))
    w1b = np.ascontiguousarray(np.asarray(inputs["L1_ffn_W1"]))
    w2b = np.ascontiguousarray(np.asarray(inputs["L1_ffn_W2"]))
    maps = []
    for c in range(NCORE):
        s, r = c // 4, c % 4
        maps.append({
            "x_shard": np.ascontiguousarray(x[s, r * TOK:(r + 1) * TOK, :]),
            "wqkv_sl": np.ascontiguousarray(wqkv[c * QKV_SH:(c + 1) * QKV_SH, :]),
            "wout_sl": np.ascontiguousarray(wout[c * WOUT_SH:(c + 1) * WOUT_SH, :]),
            "b_out": bout,
            "w1_sl": np.ascontiguousarray(w1[c * W1_SH:(c + 1) * W1_SH, :]),
            "w2_sl": np.ascontiguousarray(w2[c * W1_SH:(c + 1) * W1_SH, :]),
            "w3_sl": np.ascontiguousarray(w3[c * W3_SH:(c + 1) * W3_SH, :]),
            "wq1_sl": np.ascontiguousarray(wq1[c * 256:(c + 1) * 256, :]),
            "wc1_sl": np.ascontiguousarray(wc1[c * 256:(c + 1) * 256, :]),
            "wk1_sl": np.ascontiguousarray(wk1[c * 256:(c + 1) * 256, :]),
            "wv1_sl": np.ascontiguousarray(wv1[c * 256:(c + 1) * 256, :]),
            "wg_full": wg1,
            "bc_in": bc1,
            "caus_in": _caus_mask(r),
            "maskR_in": _maskR(r),
            "w1b_sl": np.ascontiguousarray(w1b[c * 1024:(c + 1) * 1024, :]),
            "w2b_sl": np.ascontiguousarray(w2b[:, c * 1024:(c + 1) * 1024]),
        })
    return maps


def _caus_mask(r):
    m = np.zeros((1, 256), dtype=np.float32)
    for qc in range(2):
        for j in range(32):
            if j < 8 * r + 4 * qc:
                m[0, qc * 32 + j] = 1.0
    return m


def _maskR(r):
    m = np.zeros((8, 4), dtype=np.float32)
    if r > 0:
        m[:, r - 1] = 1.0
    return m


_NC_CACHE = {}


def kernel(**inputs):
    maps = shard_inputs(inputs)
    if "nc" not in _NC_CACHE:
        _NC_CACHE["nc"] = build_nc()
    nc = _NC_CACHE["nc"]
    res = run_bass_kernel_spmd(nc, maps, core_ids=list(range(NCORE)))
    out = np.zeros((2, SEQ, DM), dtype=np.float32)
    for c in range(NCORE):
        s, r = c // 4, c % 4
        out[s, r * TOK:(r + 1) * TOK, :] = res.results[c]["out"]
    return out


# version 16 (final)
# speedup vs baseline: 1.1441x; 1.0805x over previous
import numpy as np
import ml_dtypes
from contextlib import ExitStack

import concourse.bass as bass
import concourse.mybir as mybir
from concourse.bass_utils import run_bass_kernel_spmd

F32 = mybir.dt.float32
BF16 = mybir.dt.bfloat16
I32 = mybir.dt.int32
ALU = mybir.AluOpType
AF = mybir.ActivationFunctionType
AX = mybir.AxisListType
ds = bass.ds
BF = ml_dtypes.bfloat16

NCORE = 8
TOK = 1024
DM = 2048
SEQ = 4096
NH0 = 16
SCALE0 = float(np.float32(1.0) / np.sqrt(np.float32(128.0)))
MAGIC = 12582912.0
LNEPS = 1e-5
QKV_SH = 768
WOUT_SH = 256
DFF0 = 5632
DFF0P = 6144
W1_SH = 704
W3_SH = 256
NF4 = [-1.0, -0.6962, -0.5251, -0.3949, -0.2844, -0.1848, -0.0911, 0.0,
       0.0796, 0.1609, 0.2461, 0.3379, 0.4407, 0.5626, 0.723, 1.0]
NF4_MID = [(NF4[i] + NF4[i + 1]) / 2.0 for i in range(15)]
NF4_ASC = {i: float(np.float32(np.float32(NF4[i + 1]) - np.float32(NF4[i])))
           for i in range(7, 15)}
NF4_DSC = {i: float(np.float32(np.float32(NF4[i]) - np.float32(NF4[i + 1])))
           for i in range(7)}
NF4_NBLK = 180224
GOFF = 6.906768
LNC1 = float(np.log(128e-6))
KV1_K = 1024 * 1024
KV1_V = 1024 * 1024
KV1_G = 8 * 2048
KV1SZ = KV1_K + KV1_V + KV1_G


class Sem:
    __slots__ = ("name", "sched", "handle")

    def __init__(self, name):
        self.name, self.sched, self.handle = name, 0, None


class Trk:
    __slots__ = ("wid", "pending")

    def __init__(self):
        self.wid = None
        self.pending = []

    def wwaits(self):
        w = list(self.pending)
        if self.wid is not None:
            w.append(self.wid)
        self.pending = []
        return w

    def rwaits(self):
        return [self.wid] if self.wid is not None else []


class Prog:
    def __init__(self):
        self.ops = []
        self.sems = []

    def sem(self, name):
        s = Sem(name)
        self.sems.append(s)
        return s

    def op(self, engine, fn, waits=(), sem=None, n_inc=1):
        seen = {}
        for s, v in waits:
            if v > 0 and seen.get(s, 0) < v:
                seen[s] = v
        w = list(seen.items())
        ident = None
        if sem is not None:
            sem.sched += n_inc
            ident = (sem, sem.sched)
            try:
                ln = fn.__code__.co_firstlineno
            except AttributeError:
                ln = 0
            OPLOG.setdefault(sem.name, []).append((sem.sched, ln))
        self.ops.append((engine, fn, w, (sem, n_inc) if sem else None))
        return ident

    def dma(self, engine, out_ap, in_ap, waits=(), sem=None, slow=False):
        def fn(eng, o=out_ap, i=in_ap):
            if slow:
                return eng.dma_start(out=o, in_=i,
                                     allow_slow_non_contiguous=True)
            return eng.dma_start(out=o, in_=i)
        return self.op(engine, fn, waits=waits, sem=sem, n_inc=16)

    def emit(self, nc, ctx):
        for s in self.sems:
            s.handle = ctx.enter_context(nc.semaphore(s.name))
        block = ctx.enter_context(nc.Block())
        by_eng = {}
        for rec in self.ops:
            by_eng.setdefault(rec[0], []).append(rec)

        def run_engine(eng, recs):
            last_wait = {}
            for _, fn, waits, inc in recs:
                for s, v in waits:
                    if last_wait.get(s, 0) < v:
                        eng.wait_ge(s.handle, v)
                        last_wait[s] = v
                inst = fn(eng)
                if inc is not None:
                    inst.then_inc(inc[0].handle, inc[1])

        for name in ("sync", "tensor", "scalar", "vector", "gpsimd"):
            if name in by_eng:
                recs = by_eng[name]
                getattr(block, name)(lambda eng, recs=recs: run_engine(eng, recs))


PHASES = []
OPLOG = {}


def build_nc():
    nc = bass.Bass("TRN2", target_bir_lowering=False)
    p = Prog()

    s_mm = p.sem("s_mm")
    s_ev = p.sem("s_ev")
    s_ve = p.sem("s_ve")
    s_cc = p.sem("s_cc")
    s_gp = p.sem("s_gp")
    s_fin = p.sem("s_fin")

    dma_sems = {}

    def dsem(name):
        if name not in dma_sems:
            dma_sems[name] = p.sem(name)
        return dma_sems[name]

    ESEM = {"tensor": s_mm, "scalar": s_ev, "vector": s_ve, "gpsimd": s_gp}
    PHASES.clear()

    def mark(name):
        PHASES.append((name, {s.name: s.sched for s in
                              (s_mm, s_ev, s_ve, s_cc, s_gp)}))

    def eop(engine, fn, reads=(), writes=(), extra=(), need_id=True):
        waits = list(extra)
        for t in reads:
            waits += t.rwaits()
        wl = [t.wwaits() for t in writes]
        for x in wl:
            waits += x
        ident = p.op(engine, fn, waits=waits,
                     sem=(ESEM[engine] if need_id else None))
        if ident is not None:
            for t in reads:
                t.pending.append(ident)
            for t in writes:
                t.wid = ident
        return ident

    POOL = {}
    for _n in ("d_eye", "d_ones", "d_onesc", "d_sel", "d_bout", "d_bo", "d_wg",
               "d_wg2", "d_wgl", "d_tril", "d_caus", "d_mr", "d_bc1",
               "d_bc2", "d_bc3"):
        POOL[_n] = "d_m1"
    for _n in ("d_arin", "d_arout", "d_pgd", "d_ptT", "d_ptm", "d_g1l",
               "d_g1c"):
        POOL[_n] = "d_m2"
    for _n in ("d_pad1", "d_pad2", "d_pad3"):
        POOL[_n] = "d_pad"
    for _w in ("w1", "w2", "w3"):
        POOL[f"d_scl{_w}"] = "d_scl"
        POOL[f"d_sbm{_w}"] = "d_sbm"
        POOL[f"d_deq{_w}"] = "d_deq"
        POOL[f"d_deqA{_w}"] = "d_dqA"
        POOL[f"d_q{_w}"] = "d_qq"
        POOL[f"d_stw{_w}0"] = "d_sw0"
        POOL[f"d_stw{_w}1"] = "d_sw1"
    for _n in ("d_c1_wq1", "d_c1_wc1", "d_c1_wk1", "d_c1_wv1"):
        POOL[_n] = "d_c1"
    for _t in range(8):
        POOL[f"d_x{_t}"] = "d_x"
    pool_chain = {}

    def dop(sem_name, out_ap, in_ap, reads=(), writes=(), extra=(), engine="sync"):
        name = POOL.get(sem_name, sem_name)
        writes = list(writes)
        if name != sem_name:
            writes.append(pool_chain.setdefault(name, Trk()))
        waits = list(extra)
        for t in reads:
            waits += t.rwaits()
        for t in writes:
            waits += t.wwaits()
        ident = p.dma(engine, out_ap, in_ap, waits=waits, sem=dsem(name),
                      slow=sem_name in ("d_bc1", "d_bc2", "d_bc3", "d_cold0", "d_cold1"))
        for t in reads:
            t.pending.append(ident)
        for t in writes:
            t.wid = ident
        return ident

    def inp(name, shape, dt=F32):
        return nc.dram_tensor(name, list(shape), dt, kind="ExternalInput").ap()

    x_in = inp("x_shard", [TOK, DM])
    wqkv_sl = inp("wqkv_sl", [QKV_SH, DM])
    wout_sl = inp("wout_sl", [WOUT_SH, DM])
    b_out_in = inp("b_out", [1, DM])
    w1_sl = inp("w1_sl", [W1_SH, DM], BF16)
    w2_sl = inp("w2_sl", [W1_SH, DM], BF16)
    w3_sl = inp("w3_sl", [W3_SH, DFF0], BF16)
    wq1_sl = inp("wq1_sl", [256, DM])
    wc1_sl = inp("wc1_sl", [256, DM])
    wk1_sl = inp("wk1_sl", [256, 1024])
    wv1_sl = inp("wv1_sl", [256, 1024])
    wg_full = inp("wg_full", [DM, 8])
    bc_in = inp("bc_in", [1, DM])
    w1b_sl = inp("w1b_sl", [1024, DM])
    w2b_sl = inp("w2b_sl", [DM, 1024])
    caus_in = inp("caus_in", [1, 256])
    maskR_in = inp("maskR_in", [8, 4])

    out_x = nc.dram_tensor("out", [TOK, DM], F32, kind="ExternalOutput").ap()

    qkvT_contrib = nc.dram_tensor("qkvT_contrib", [DM, QKV_SH], BF16)
    qkvT_arena = nc.dram_tensor("qkvT_arena", [NCORE, DM, QKV_SH], BF16)
    woutT_contrib = nc.dram_tensor("woutT_contrib", [DM, WOUT_SH], BF16)
    woutT_arena = nc.dram_tensor("woutT_arena", [NCORE, DM, WOUT_SH], BF16)

    w1q_dram = nc.dram_tensor("w1q_dram", [W1_SH, DM], BF16)
    w2q_dram = nc.dram_tensor("w2q_dram", [W1_SH, DM], BF16)
    w3q_dram = nc.dram_tensor("w3q_dram", [W3_SH, DFF0], BF16)
    scl_dram = nc.dram_tensor("scl_dram", [3 * 22528], BF16)
    w1T_contrib = nc.dram_tensor("w1T_contrib", [DM, 768], BF16)
    w2T_contrib = nc.dram_tensor("w2T_contrib", [DM, 768], BF16)
    w3T_contrib = nc.dram_tensor("w3T_contrib", [DFF0P, 256], BF16)
    w1T_arena = nc.dram_tensor("w1T_arena", [NCORE, DM, 768], BF16)
    w2T_arena = nc.dram_tensor("w2T_arena", [NCORE, DM, 768], BF16)
    w3T_arena = nc.dram_tensor("w3T_arena", [NCORE, DFF0P, 256], BF16)
    w3T_flat = nc.dram_tensor("w3T_flat", [DFF0P, DM], BF16)
    wq1_contrib = nc.dram_tensor("wq1_contrib", [256, DM], BF16)
    wq1_arena = nc.dram_tensor("wq1_arena", [DM, DM], BF16)
    wc1_contrib = nc.dram_tensor("wc1_contrib", [256, DM], BF16)
    wc1_arena = nc.dram_tensor("wc1_arena", [DM, DM], BF16)
    wk1_contrib = nc.dram_tensor("wk1_contrib", [256, 1024], BF16)
    wk1_arena = nc.dram_tensor("wk1_arena", [DM, 1024], BF16)
    wv1_contrib = nc.dram_tensor("wv1_contrib", [256, 1024], BF16)
    wv1_arena = nc.dram_tensor("wv1_arena", [DM, 1024], BF16)
    kv1_contrib = nc.dram_tensor("kv1_contrib", [KV1SZ], BF16)
    kv1_gather = nc.dram_tensor("kv1_gather", [4, KV1SZ], BF16)
    w1e_contrib = nc.dram_tensor("w1e_contrib", [DM, 1024], BF16)
    w1e_arena = nc.dram_tensor("w1e_arena", [8, DM, 1024], BF16)
    w2e_contrib = nc.dram_tensor("w2e_contrib", [1024, DM], BF16)
    w2e_arena = nc.dram_tensor("w2e_arena", [8 * 1024, DM], BF16)
    vrep_dram = nc.dram_tensor("vrep_dram", [16 * 128, 32 * 128], BF16)
    vrep1_dram = nc.dram_tensor("vrep1_dram", [8 * 128, 32 * 128], BF16)
    pg_dram = nc.dram_tensor("pg_dram", [8, 1024], F32)
    wg_dram = nc.dram_tensor("wg_dram", [128, 128], BF16)
    bias_dram = nc.dram_tensor("bias_dram", [2, DM], BF16)
    ar_in = nc.dram_tensor("ar_in", [1, 16], F32)
    ar_out = nc.dram_tensor("ar_out", [1, 16], F32)
    kv0_contrib = nc.dram_tensor("kv0_contrib", [2 * DM * TOK], BF16)
    kv0_gather = nc.dram_tensor("kv0_gather", [4, 2 * DM * TOK], BF16)

    kv1_c_k = lambda: kv1_contrib.ap()[0:KV1_K].rearrange("(d t) -> d t", d=1024)
    kv1_c_v = lambda: kv1_contrib.ap()[KV1_K:KV1_K + KV1_V].rearrange(
        "(t d) -> t d", t=TOK)
    kv1_c_g = lambda: kv1_contrib.ap()[KV1_K + KV1_V:].rearrange(
        "(h t) -> h t", h=8)
    kv1_g_k = lambda r: kv1_gather.ap()[r, 0:KV1_K].rearrange(
        "(d t) -> d t", d=1024)
    kv1_g_v = lambda r: kv1_gather.ap()[r, KV1_K:KV1_K + KV1_V].rearrange(
        "(t d) -> t d", t=TOK)
    kv1_g_g = lambda r: kv1_gather.ap()[r, KV1_K + KV1_V:].rearrange(
        "(h t) -> h t", h=8)
    kv0_c_k = lambda: kv0_contrib.ap()[0:DM * TOK].rearrange("(d t) -> d t", d=DM)
    kv0_c_v = lambda: kv0_contrib.ap()[DM * TOK:].rearrange("(t d) -> t d", t=TOK)
    kv0_g_k = lambda r: kv0_gather.ap()[r, 0:DM * TOK].rearrange("(d t) -> d t", d=DM)
    kv0_g_v = lambda r: kv0_gather.ap()[r, DM * TOK:].rearrange("(t d) -> t d", t=TOK)

    eye_d = nc.inline_tensor(np.eye(128, dtype=BF), name="eye128")
    onesq_d = nc.inline_tensor(np.ones((1, 128), dtype=BF), name="ones1x128")
    onescol_d = nc.inline_tensor(np.ones((128, 1), dtype=BF), name="ones128x1")
    sel_np = np.zeros((64, 32 * 128), dtype=BF)
    for j in range(32):
        sel_np[2 * j, j * 128:(j + 1) * 128] = 1
        sel_np[2 * j + 1, j * 128:(j + 1) * 128] = 1
    sel_d = nc.inline_tensor(sel_np, name="sel64")
    tril_np = np.zeros((128, 128), dtype=BF)
    for kk in range(128):
        tril_np[kk, kk:] = 1
    tril_d = nc.inline_tensor(tril_np, name="tril128")

    with ExitStack() as ctx:
        def sb(scope, name, shape, dt=F32):
            return scope.enter_context(nc.sbuf_tensor(name, list(shape), dt))

        def psum(scope, name, shape, dt=F32):
            return scope.enter_context(nc.psum_tensor(name, list(shape), dt))

        x_sb = sb(ctx, "x", [128, 8 * DM])
        lnT_sb = sb(ctx, "lnT", [128, 16 * TOK], BF16)
        wq_sb = sb(ctx, "wq", [128, 2 * 4096], BF16)
        ul_sb = sb(ctx, "ul", [128, 2 * 512], BF16)
        acc2_sb = sb(ctx, "acc2", [128, 2 * 512], F32)
        tl2_sb = sb(ctx, "tl2", [128, 2 * 512], F32)
        phat_sb = sb(ctx, "phat", [128, 2 * 512], BF16)
        kvst_sb = phat_sb
        stat_sb = sb(ctx, "stat", [128, 64], F32)
        bout_sb = sb(ctx, "bout", [1, 256], BF16)
        eyeb = sb(ctx, "eyeb", [128, 128], BF16)
        onesq_sb = sb(ctx, "onesq", [1, 128], BF16)
        onescol_sb = sb(ctx, "onescol", [128, 1], BF16)
        trsp_sb = sb(ctx, "trsp", [128, 2 * 128], BF16)
        const_sb = sb(ctx, "const", [128, 8], F32)
        deqA_sb = sb(ctx, "deqA", [128, 192], BF16)
        sbm_sb = sb(ctx, "sbm", [88, 256], BF16)
        cen_sb = sb(ctx, "cen", [88, 256], BF16)
        box88 = sb(ctx, "box88", [88, 256], I32)
        sabs_sb = sb(ctx, "sabs", [88, 4], F32)
        meanbf_sb = sb(ctx, "meanbf", [1, 8], BF16)
        mean_sb = sb(ctx, "meanp", [88, 4], F32)
        arst_sb = sb(ctx, "arst", [1, 16], F32)
        zpad_sb = sb(ctx, "zpad", [128, 64], BF16)
        qtl_sb = sb(ctx, "qtl", [128, 3 * 128], BF16)

        pP1 = psum(ctx, "pP1", [128, 1024])
        pA = psum(ctx, "pA", [128, 512])
        pB = psum(ctx, "pB", [128, 512])
        pattn = psum(ctx, "pattn", [128, 512])
        ptrA = psum(ctx, "ptrA", [128, 128], BF16)
        ptrB = psum(ctx, "ptrB", [128, 128], BF16)
        ptr = [ptrA, ptrB]
        pAB = [pA, pB]

        T = lambda n=1: [Trk() for _ in range(n)] if n > 1 else Trk()
        t_x = [Trk() for _ in range(8)]
        t_lnT = Trk()
        t_qT = Trk()
        t_wq = [Trk(), Trk()]
        t_lnscr = [Trk(), Trk()]
        t_phat = [Trk(), Trk()]
        t_stat = Trk()
        t_bout = Trk()
        t_const = Trk()
        t_eye = Trk()
        t_ones = Trk()
        t_onesc = Trk()
        t_trsp = [Trk(), Trk()]
        t_pP1 = Trk()
        t_pAB = [Trk(), Trk()]
        t_pattn = Trk()
        t_ptr = [Trk(), Trk()]
        t_qkvC = Trk()
        t_woutC = Trk()
        t_qkvA = Trk()
        t_woutA = Trk()
        t_kvC = Trk()
        t_kvG = Trk()

        eop("vector", lambda v: v.memset(const_sb[:, 0:1], LNEPS),
            writes=[t_const])
        eop("vector", lambda v: v.memset(const_sb[:, 4:5], -192.0),
            writes=[t_const])
        dop("d_eye", eyeb[:, :], eye_d.ap(), writes=[t_eye])
        dop("d_ones", onesq_sb[:, :], onesq_d.ap(), writes=[t_ones])
        dop("d_onesc", onescol_sb[:, :], onescol_d.ap(), writes=[t_onesc])
        for t in range(8):
            dop(f"d_x{t}", x_sb[:, t * DM:(t + 1) * DM],
                x_in[t * 128:(t + 1) * 128, :], writes=[t_x[t]])

        prep_scope = ExitStack()
        wf_sb = sb(prep_scope, "wf", [128, 2 * 2048], F32)
        wc_sb = sb(prep_scope, "wc", [128, 2 * 2048], BF16)
        bout_f = sb(prep_scope, "bout_f", [1, DM], F32)
        slc_sb = sb(prep_scope, "slc", [128, 5632], BF16)
        sclA_sb = sb(prep_scope, "sclA", [128, 192], BF16)
        rcp_sb = sb(prep_scope, "rcp", [128, 192], F32)
        t_slc = Trk()
        t_sclA = Trk()
        t_rcp = Trk()
        t_acc = Trk()
        t_tl = Trk()
        t_deqA = Trk()
        t_sbm = Trk()
        t_cen = Trk()
        t_qsc = Trk()
        t_sabs = Trk()
        t_mean = Trk()
        t_arst = Trk()
        t_zpad = Trk()
        t_scld = Trk()
        t_ar = Trk()
        t_q1 = Trk()
        t_q2 = Trk()
        t_q3 = Trk()
        t_w1C = Trk()
        t_w2C = Trk()
        t_w3C = Trk()
        t_w1A = Trk()
        t_w2A = Trk()
        t_w3A = Trk()
        t_w3F = Trk()
        t_wf = [Trk(), Trk()]
        t_wc = [Trk(), Trk()]
        t_boutf = Trk()
        t_biasd = Trk()
        t_wq1C, t_wc1C, t_wk1C, t_wv1C = Trk(), Trk(), Trk(), Trk()
        t_wq1A, t_wc1A, t_wk1A, t_wv1A = Trk(), Trk(), Trk(), Trk()
        t_wg = Trk()
        t_tril = Trk()
        t_kv1C = Trk()
        t_kv1G = Trk()

        NF4_TILES = {
            "w1": (w1_sl, w1q_dram, 6, 2048, 32, [128] * 5 + [64]),
            "w2": (w2_sl, w2q_dram, 6, 2048, 32, [128] * 5 + [64]),
            "w3": (w3_sl, w3q_dram, 2, 5632, 88, [128, 128]),
        }

        dop("d_bout", bout_f[:, :], b_out_in, writes=[t_boutf])
        eop("vector", lambda v: v.tensor_copy(wc_sb[0:1, 0:2048], bout_f[:, :]),
            reads=[t_boutf], writes=[t_wc[0]])
        dop("d_bout", bias_dram.ap()[0:1, :], wc_sb[0:1, 0:2048],
            reads=[t_wc[0]], writes=[t_biasd])

        def emit_transpose_cast(src_ap, nrows, contrib, t_contrib, tag):
            for rc in range(nrows // 128):
                f = rc % 2
                dop(f"d_wf{f}", wf_sb[:, f * 2048:(f + 1) * 2048],
                    src_ap[rc * 128:(rc + 1) * 128, :], writes=[t_wf[f]])
                c = rc % 2
                eop("vector", (lambda v, f=f, c=c: v.tensor_copy(
                    wc_sb[:, c * 2048:(c + 1) * 2048],
                    wf_sb[:, f * 2048:(f + 1) * 2048])),
                    reads=[t_wf[f]], writes=[t_wc[c]])
                for cc in range(16):
                    ts = cc % 2
                    last = (cc == 15)
                    eop("tensor", (lambda t, c=c, cc=cc, ts=ts: t.transpose(
                        ptr[ts][:, :],
                        wc_sb[:, c * 2048 + cc * 128: c * 2048 + (cc + 1) * 128],
                        eyeb[:, :])),
                        reads=[t_wc[c], t_eye], writes=[t_ptr[ts]])
                    ss = cc % 2
                    eop("scalar", (lambda s, ts=ts, ss=ss: s.copy(
                        trsp_sb[:, ss * 128:(ss + 1) * 128], ptr[ts][:, :])),
                        reads=[t_ptr[ts]], writes=[t_trsp[ss]])
                    dop(f"d_st_{tag}{ss}",
                        contrib[cc * 128:(cc + 1) * 128, rc * 128:(rc + 1) * 128],
                        trsp_sb[:, ss * 128:(ss + 1) * 128],
                        reads=[t_trsp[ss]], writes=[t_contrib])

        emit_transpose_cast(wqkv_sl, QKV_SH, qkvT_contrib.ap(), t_qkvC, "q")
        emit_transpose_cast(wout_sl, WOUT_SH, woutT_contrib.ap(), t_woutC, "o")

        def emit_cast(src_ap, contrib, cols, t_contrib, tag):
            for rc in range(2):
                f = rc % 2
                dop(f"d_wf{f}", wf_sb[:, f * 2048: f * 2048 + cols],
                    src_ap[rc * 128:(rc + 1) * 128, :], writes=[t_wf[f]])
                eop("vector", (lambda v, f=f, cols=cols: v.tensor_copy(
                    wc_sb[:, f * 2048: f * 2048 + cols],
                    wf_sb[:, f * 2048: f * 2048 + cols])),
                    reads=[t_wf[f]], writes=[t_wc[f]])
                dop(f"d_c1_{tag}", contrib[rc * 128:(rc + 1) * 128, :],
                    wc_sb[:, f * 2048: f * 2048 + cols],
                    reads=[t_wc[f]], writes=[t_contrib])

        emit_cast(wq1_sl, wq1_contrib.ap(), 2048, t_wq1C, "wq1")
        emit_cast(wc1_sl, wc1_contrib.ap(), 2048, t_wc1C, "wc1")
        emit_cast(wk1_sl, wk1_contrib.ap(), 1024, t_wk1C, "wk1")
        emit_cast(wv1_sl, wv1_contrib.ap(), 1024, t_wv1C, "wv1")
        dop("d_wg", wf_sb[:, 0:128].rearrange("p (k c) -> p k c", k=16),
            wg_full.rearrange("(k p) c -> p k c", p=128),
            reads=[t_wf[0]], writes=[t_wf[0]])
        eop("vector", lambda v: v.tensor_copy(wc_sb[:, 0:128], wf_sb[:, 0:128]),
            reads=[t_wf[0]], writes=[t_wc[0]])
        dop("d_wg2", wg_dram.ap(), wc_sb[:, 0:128],
            reads=[t_wc[0]], writes=[t_wg])

        eop("vector", lambda v: v.memset(zpad_sb[:, :], 0.0), writes=[t_zpad])
        eop("vector", lambda v: v.memset(arst_sb[:, :], 0.0), writes=[t_arst])

        absw_sb = sb(prep_scope, "absw", [128, 4], F32)
        t_absw = Trk()
        eop("vector", lambda v: v.memset(absw_sb[:, 0:2], 0.0),
            writes=[t_absw])
        for mi, (srcb, nt, cols) in enumerate(
                ((w1b_sl, 8, 2048), (w2b_sl, 16, 1024))):
            for t in range(nt):
                f = t % 2
                dop(f"d_wf{f}", wf_sb[:, f * 2048: f * 2048 + cols],
                    srcb[t * 128:(t + 1) * 128, :], writes=[t_wf[f]])
                eop("vector", (lambda v, f=f, cols=cols: v.tensor_reduce(
                    absw_sb[:, 2:3], wf_sb[:, f * 2048: f * 2048 + cols],
                    AX.X, ALU.add, apply_absolute_value=True)),
                    reads=[t_wf[f]], writes=[t_absw])
                eop("vector", (lambda v, mi=mi: v.tensor_tensor(
                    absw_sb[:, mi:mi + 1], absw_sb[:, mi:mi + 1],
                    absw_sb[:, 2:3], ALU.add)), writes=[t_absw])
        eop("gpsimd", lambda g: g.tensor_reduce(
            arst_sb[0:1, 3:5], absw_sb[0:128, 0:2], AX.C, ALU.add),
            reads=[t_absw], writes=[t_arst])


        t_slc_l = Trk()

        def nf4_passA(name, midx, t_qd):
            srcd, qd, nt, wdt, nblk, rows = NF4_TILES[name]
            eop("vector", lambda v: v.memset(sclA_sb[:, :], 0.0),
                writes=[t_sclA])
            for t in range(nt):
                r = rows[t]
                r0 = sum(rows[:t])
                dop(f"d_slc", slc_sb[0:r, 0:wdt], srcd[r0:r0 + r, :],
                    writes=[t_slc_l])
                w3v = slc_sb[0:r, 0:wdt].rearrange("p (b k) -> p b k", k=64)
                eop("vector", (lambda v, t=t, r=r, w3v=w3v, nblk=nblk:
                    v.tensor_reduce(
                        sclA_sb[0:r, t * nblk:(t + 1) * nblk], w3v, AX.X,
                        ALU.max, apply_absolute_value=True)),
                    reads=[t_slc_l], writes=[t_sclA])
                eop("vector", (lambda v, t=t, r=r, nblk=nblk: v.reciprocal(
                    rcp_sb[0:r, 0:nblk],
                    sclA_sb[0:r, t * nblk:(t + 1) * nblk])),
                    reads=[t_sclA], writes=[t_rcp])
                rb = rcp_sb[0:r, 0:nblk].rearrange(
                    "p (b o) -> p b o", o=1).broadcast_to([r, nblk, 64])
                eop("vector", (lambda v, w3v=w3v, rb=rb: v.tensor_tensor(
                    w3v, w3v, rb, ALU.mult)),
                    reads=[t_rcp], writes=[t_slc_l])
                dop(f"d_q{name}", qd[r0:r0 + r, :], slc_sb[0:r, 0:wdt],
                    reads=[t_slc_l], writes=[t_qd])
            eop("tensor", lambda te: te.matmul(
                pA[0:1, 0:192], onescol_sb[:, :], sclA_sb[:, 0:192],
                start=True, stop=True),
                reads=[t_sclA, t_onesc], writes=[t_pAB[0]])
            eop("vector", (lambda v, m=midx: v.reduce_sum(
                arst_sb[0:1, m:m + 1], pA[0:1, 0:192], AX.X)),
                reads=[t_pAB[0]], writes=[t_arst])
            for t in range(nt):
                r = rows[t]
                off = midx * 22528 + sum(rows[:t]) * nblk
                dop(f"d_scl{name}", scl_dram.ap()[off:off + r * nblk].rearrange(
                    "(p b) -> p b", p=r),
                    sclA_sb[0:r, t * nblk:(t + 1) * nblk],
                    reads=[t_sclA], writes=[t_scld])

        nf4_passA("w1", 0, t_q1)
        nf4_passA("w2", 1, t_q2)
        nf4_passA("w3", 2, t_q3)

        dop("d_arin", ar_in.ap(), arst_sb[:, :], reads=[t_arst])
        p.op("gpsimd", lambda g: g.collective_compute(
            "AllReduce", ALU.add, replica_groups=[list(range(8))],
            ins=[ar_in.ap().opt()], outs=[ar_out.ap().opt()]),
            waits=[(dma_sems["d_m2"], dma_sems["d_m2"].sched)], sem=s_cc)
        t_ar.wid = (s_cc, s_cc.sched)
        dop("d_arout", arst_sb[:, :], ar_out.ap(), reads=[t_ar], writes=[t_arst])
        prep_done = ([(s, s.sched) for s in dma_sems.values()]
                     + [(s_ve, s_ve.sched), (s_ev, s_ev.sched),
                        (s_mm, s_mm.sched)])
        mark('prep')
        prep_scope.close()

        att = ExitStack()
        qT_sb = sb(att, "qT", [128, NH0 * TOK], BF16)
        lnscr_sb = sb(att, "lnscr", [128, 2 * DM], BF16)
        kTh_sb = sb(att, "kTh", [128, SEQ], BF16)
        vh_sb = sb(att, "vh", [128, SEQ], BF16)
        e_sb = sb(att, "eqm", [128, 2 * 1024], BF16)
        sel_sb = sb(att, "sel", [64, 32 * 128], BF16)
        red_sb = sb(att, "red", [128, 8 * 64], F32)
        lngs_sb = sb(att, "lngs", [128, 512], F32)
        lng2_sb = sb(att, "lng2", [128, 8 * 64], BF16)
        lngT_sb = sb(att, "lngT", [64, 1024], BF16)
        cq_sb = sb(att, "cq", [128, 32], F32)
        nb_sb = sb(att, "nb", [128, 32], F32)
        wf2_sb = nb_sb

        def all_store_waits(tags):
            w = []
            for tg in tags:
                tg = POOL.get(tg, tg)
                if tg in dma_sems:
                    w.append((dma_sems[tg], dma_sems[tg].sched))
            return w

        agw = all_store_waits(["d_st_q0", "d_st_q1"])
        id_ag1 = p.op("gpsimd", lambda g: g.collective_compute(
            "AllGather", ALU.bypass, replica_groups=[list(range(8))],
            ins=[qkvT_contrib.ap().opt()], outs=[qkvT_arena.ap().opt()]),
            waits=agw, sem=s_cc)
        t_qkvA.wid = id_ag1
        agw = all_store_waits(["d_st_o0", "d_st_o1"])
        id_ag2 = p.op("gpsimd", lambda g: g.collective_compute(
            "AllGather", ALU.bypass, replica_groups=[list(range(8))],
            ins=[woutT_contrib.ap().opt()], outs=[woutT_arena.ap().opt()]),
            waits=agw, sem=s_cc)
        t_woutA.wid = id_ag2

        def emit_layernorm(t_x_tiles, x_tile, dst_T, t_dst, lnscr, t_lnscr,
                           first_extra):
            for t in range(8):
                eop("vector", (lambda v, t=t: v.reduce_sum(
                    stat_sb[:, t:t + 1], x_tile(t), AX.X)),
                    reads=[t_x_tiles[t]], writes=[t_stat])
                eop("scalar", (lambda s, t=t: s.activation(
                    pP1[:, :], x_tile(t)[:, 0:1024], AF.Square,
                    accum_out=stat_sb[:, 8 + t:9 + t])),
                    reads=[t_x_tiles[t]], writes=[t_pP1, t_stat])
                eop("scalar", (lambda s, t=t: s.activation(
                    pP1[:, :], x_tile(t)[:, 1024:2048], AF.Square,
                    accum_out=stat_sb[:, 16 + t:17 + t])),
                    reads=[t_x_tiles[t]], writes=[t_pP1, t_stat])
            eop("vector", lambda v: v.tensor_scalar(
                stat_sb[:, 24:32], stat_sb[:, 0:8], 1.0 / DM, None, ALU.mult),
                reads=[t_stat], writes=[t_stat])
            eop("vector", lambda v: v.tensor_tensor(
                stat_sb[:, 32:40], stat_sb[:, 24:32], stat_sb[:, 24:32],
                ALU.mult), writes=[t_stat])
            eop("vector", lambda v: v.tensor_tensor(
                stat_sb[:, 40:48], stat_sb[:, 8:16], stat_sb[:, 16:24],
                ALU.add), writes=[t_stat])
            eop("vector", lambda v: v.tensor_scalar(
                stat_sb[:, 48:56], stat_sb[:, 40:48], 1.0 / DM, None, ALU.mult),
                writes=[t_stat])
            eop("vector", lambda v: v.tensor_tensor(
                stat_sb[:, 40:48], stat_sb[:, 48:56], stat_sb[:, 32:40],
                ALU.subtract), writes=[t_stat])
            eop("scalar", lambda s: s.activation(
                stat_sb[:, 48:56], stat_sb[:, 40:48], AF.Sqrt,
                bias=const_sb[:, 0:1]),
                reads=[t_const], writes=[t_stat])
            eop("vector", lambda v: v.reciprocal(
                stat_sb[:, 56:64], stat_sb[:, 48:56]), writes=[t_stat])
            for t in range(8):
                l = t % 2
                eop("vector", (lambda v, t=t, l=l: v.tensor_scalar(
                    lnscr[:, l * DM:(l + 1) * DM], x_tile(t),
                    stat_sb[:, 24 + t:25 + t], stat_sb[:, 56 + t:57 + t],
                    ALU.subtract, ALU.mult)),
                    reads=[t_x_tiles[t], t_stat], writes=[t_lnscr[l]],
                    extra=first_extra)
                for c in range(16):
                    ts = c % 2
                    eop("tensor", (lambda te, l=l, c=c, ts=ts: te.transpose(
                        ptr[ts][:, :],
                        lnscr[:, l * DM + c * 128: l * DM + (c + 1) * 128],
                        eyeb[:, :])),
                        reads=[t_lnscr[l], t_eye], writes=[t_ptr[ts]])
                    eop("scalar", (lambda s, t=t, c=c, ts=ts: s.copy(
                        dst_T[:, c * TOK + t * 128: c * TOK + (t + 1) * 128],
                        ptr[ts][:, :])),
                        reads=[t_ptr[ts]], writes=[t_dst])

        x_tile = lambda t: x_sb[:, t * DM:(t + 1) * DM]
        emit_layernorm(t_x, x_tile, lnT_sb, t_lnT, lnscr_sb, t_lnscr,
                       prep_done)
        mark('ln0')

        def qkv_colblock_ap(c0, width):
            rank = c0 // QKV_SH
            return qkvT_arena.ap()[rank].rearrange("(k p) c -> p k c", p=128)[
                :, :, c0 - rank * QKV_SH: c0 - rank * QKV_SH + width]

        for ob in range(32):
            slot = ob % 2
            dd = wq_sb[:, slot * 4096: slot * 4096 + 2048].rearrange(
                "p (k c) -> p k c", k=16)
            dop(f"d_wq{slot}", dd, qkv_colblock_ap(ob * 128, 128),
                reads=[t_qkvA], writes=[t_wq[slot]])
            for tc in range(2):
                ps = (ob * 2 + tc) % 2
                pb = pAB[ps]
                for k in range(16):
                    eop("tensor", (lambda te, s=slot, k=k, tc=tc, pb=pb: te.matmul(
                        pb[:, :],
                        wq_sb[:, s * 4096 + k * 128: s * 4096 + (k + 1) * 128],
                        lnT_sb[:, k * TOK + tc * 512: k * TOK + (tc + 1) * 512],
                        start=(k == 0), stop=(k == 15))),
                        reads=([t_wq[slot], t_lnT] if k in (0, 15) else []),
                        writes=([t_pAB[ps]] if k in (0, 15) else []),
                        need_id=(k in (0, 15)))
                if ob < 16:
                    eop("scalar", (lambda s, ob=ob, tc=tc, pb=pb: s.copy(
                        qT_sb[:, ob * TOK + tc * 512: ob * TOK + (tc + 1) * 512],
                        pb[:, :])), reads=[t_pAB[ps]], writes=[t_qT])
                else:
                    ss = (ob * 2 + tc) % 2
                    eop("scalar", (lambda s, ss=ss, pb=pb: s.copy(
                        kvst_sb[:, ss * 512:(ss + 1) * 512], pb[:, :])),
                        reads=[t_pAB[ps]], writes=[t_phat[ss]])
                    ko = ob - 16
                    dop(f"d_kv{ss}",
                        kv0_c_k()[ko * 128:(ko + 1) * 128, tc * 512:(tc + 1) * 512],
                        kvst_sb[:, ss * 512:(ss + 1) * 512],
                        reads=[t_phat[ss]], writes=[t_kvC])

        for oc in range(8):
            o0 = 4096 + oc * 256
            slot = oc % 2
            dd = wq_sb[:, slot * 4096: slot * 4096 + 4096].rearrange(
                "p (k c) -> p k c", k=16)
            dop(f"d_wq{slot}", dd, qkv_colblock_ap(o0, 256),
                reads=[t_qkvA], writes=[t_wq[slot]])
            for tt in range(8):
                ps = tt % 2
                pb = pAB[ps]
                for k in range(16):
                    eop("tensor", (lambda te, s=slot, k=k, tt=tt, pb=pb: te.matmul(
                        pb[:, 0:256],
                        lnT_sb[:, k * TOK + tt * 128: k * TOK + (tt + 1) * 128],
                        wq_sb[:, s * 4096 + k * 256: s * 4096 + (k + 1) * 256],
                        start=(k == 0), stop=(k == 15))),
                        reads=([t_wq[slot], t_lnT] if k in (0, 15) else []),
                        writes=([t_pAB[ps]] if k in (0, 15) else []),
                        need_id=(k in (0, 15)))
                ss = tt % 2
                eop("scalar", (lambda s, ss=ss, pb=pb: s.copy(
                    kvst_sb[:, ss * 512: ss * 512 + 256], pb[:, 0:256])),
                    reads=[t_pAB[ps]], writes=[t_phat[ss]])
                dop(f"d_kv{ss}",
                    kv0_c_v()[tt * 128:(tt + 1) * 128, oc * 256:(oc + 1) * 256],
                    kvst_sb[:, ss * 512: ss * 512 + 256],
                    reads=[t_phat[ss]], writes=[t_kvC])

        mark('l0qkv')
        agw = all_store_waits(["d_kv0", "d_kv1"])
        id_agkv = p.op("gpsimd", lambda g: g.collective_compute(
            "AllGather", ALU.bypass,
            replica_groups=[[0, 1, 2, 3], [4, 5, 6, 7]],
            ins=[kv0_contrib.ap().opt()], outs=[kv0_gather.ap().opt()]),
            waits=agw, sem=s_cc)
        t_kvG.wid = id_agkv
        t_vrep = [Trk(), Trk()]
        vrep_v = vrep_dram.ap().rearrange("(hp) (j d) -> hp j d", j=32)
        for hh in range(16):
            for rr in range(4):
                vs = kv0_g_v(rr).rearrange("(j p) d -> p j d", p=128)[
                    :, :, hh * 128:(hh + 1) * 128]
                dop(f"d_vrp{hh % 2}",
                    vrep_v[hh * 128:(hh + 1) * 128, rr * 8:(rr + 1) * 8, :],
                    vs, reads=[t_kvG], writes=[t_vrep[hh % 2]])

        def nf4_ladder(name, t_qd):
            srcd, qd, nt, wdt, nblk, rows = NF4_TILES[name]
            nrows = sum(rows)
            total = nrows * wdt
            CH = 512
            nch_per_row128 = wdt // CH
            for t in range(nt):
                r = rows[t]
                r0 = sum(rows[:t])
                for ch in range(nch_per_row128):
                    idx = len(ladder_q)
                    eng = "vector" if (idx % 3) else "gpsimd"
                    slot = idx % 2

                    def chunk_emit(qd=qd, t_qd=t_qd, r=r, r0=r0, ch=ch,
                                   eng=eng, slot=slot):
                        seg = ul_sb[0:r, slot * 512:(slot + 1) * 512]
                        accv = acc2_sb[0:r, slot * 512:(slot + 1) * 512]
                        tlv = tl2_sb[0:r, slot * 512:(slot + 1) * 512]
                        dop(f"d_ul{slot}", seg,
                            qd[r0:r0 + r, ch * CH:(ch + 1) * CH],
                            reads=[t_qd], writes=[t_ul[slot]],
                            engine="gpsimd")
                        eop(eng, (lambda e, accv=accv: e.memset(accv, 0.0)),
                            writes=[t_acc2[slot]])
                        for (i, cmp_op, dl) in (
                                [(i, ALU.is_gt, NF4_ASC[i]) for i in range(7, 15)]
                                + [(i, ALU.is_le, NF4_DSC[i]) for i in range(7)]):
                            eop(eng, (lambda e, seg=seg, tlv=tlv, i=i, c=cmp_op,
                                      dl=dl: e.tensor_scalar(
                                          tlv, seg, NF4_MID[i], dl, c, ALU.mult)),
                                reads=[t_ul[slot]], writes=[t_tl2[slot]])
                            eop(eng, (lambda e, accv=accv, tlv=tlv:
                                e.tensor_tensor(accv, accv, tlv, ALU.add)),
                                reads=[t_tl2[slot]], writes=[t_acc2[slot]])
                        eop(eng, (lambda e, seg=seg, accv=accv:
                            e.tensor_copy(seg, accv)),
                            reads=[t_acc2[slot]], writes=[t_ul[slot]])
                        dop(f"d_ulo{slot}", qd[r0:r0 + r, ch * CH:(ch + 1) * CH],
                            seg, reads=[t_ul[slot]], writes=[t_qd],
                            engine="gpsimd")

                    ladder_q.append(chunk_emit)

        t_ul = [Trk(), Trk()]
        t_acc2 = [Trk(), Trk()]
        t_tl2 = [Trk(), Trk()]
        mark('kv0ag')
        ladder_q = []
        nf4_ladder("w1", t_q1)
        nf4_ladder("w2", t_q2)
        nf4_ladder("w3", t_q3)

        t_meanbf = Trk()
        t_qtl = [Trk(), Trk(), Trk()]

        def nf4_deq(name, midx, t_qd, contrib, t_contrib, padmap):
            ems = []
            srcd, qd, nt, wdt, nblk, rows = NF4_TILES[name]
            def prelude():
                eop("vector", (lambda v, m=midx: v.tensor_scalar(
                    arst_sb[0:1, 8 + m:9 + m], arst_sb[0:1, m:m + 1],
                    1.0 / NF4_NBLK, None, ALU.mult)),
                    reads=[t_arst], writes=[t_arst])
                eop("vector", (lambda v, m=midx: v.tensor_copy(
                    meanbf_sb[0:1, m:m + 1], arst_sb[0:1, 8 + m:9 + m])),
                    reads=[t_arst], writes=[t_meanbf])
                eop("tensor", (lambda te, m=midx: te.matmul(
                    pA[0:88, 0:1], onesq_sb[0:1, 0:88], meanbf_sb[0:1, m:m + 1],
                    start=True, stop=True)),
                    reads=[t_meanbf, t_ones], writes=[t_pAB[0]])
                eop("vector", lambda v: v.tensor_copy(mean_sb[0:88, 0:1],
                                                      pA[0:88, 0:1]),
                    reads=[t_pAB[0]], writes=[t_mean])
                dop(f"d_sbm{name}", sbm_sb[:, :],
                    scl_dram.ap()[midx * 22528:(midx + 1) * 22528].rearrange(
                        "(p b) -> p b", p=88),
                    reads=[t_scld], writes=[t_sbm])
                eop("vector", lambda v: v.tensor_scalar(
                    cen_sb[:, :], sbm_sb[:, :], mean_sb[0:88, 0:1], None,
                    ALU.subtract), reads=[t_sbm, t_mean], writes=[t_cen])
                sab_bf = sabs_sb[0:88, 3:4].bitcast(BF16)[0:88, 0:1]
                eop("vector", (lambda v, sab_bf=sab_bf: v.tensor_reduce(
                    sab_bf, cen_sb[:, :], AX.X, ALU.max,
                    apply_absolute_value=True)),
                    reads=[t_cen], writes=[t_sabs])
                two_bf = sabs_sb[0:88, 2:3].bitcast(BF16)[0:88, 0:1]
                eop("vector", (lambda v, sab_bf=sab_bf, two_bf=two_bf:
                    v.tensor_scalar(two_bf, sab_bf, 2.0, None, ALU.mult)),
                    writes=[t_sabs])
                qf_bf = sabs_sb[0:88, 2:3].bitcast(BF16)[0:88, 1:2]
                eop("vector", (lambda v, two_bf=two_bf: v.reciprocal(
                    sabs_sb[0:88, 1:2], two_bf)), writes=[t_sabs])
                eop("vector", (lambda v, qf_bf=qf_bf: v.tensor_scalar(
                    qf_bf, sabs_sb[0:88, 1:2], 256.0, None, ALU.mult)),
                    writes=[t_sabs])
                eop("vector", (lambda v, qf_bf=qf_bf: v.tensor_copy(
                    sabs_sb[0:88, 1:2], qf_bf)), writes=[t_sabs])
                boxf = box88[:, :].bitcast(F32)
                eop("vector", lambda v: v.tensor_scalar(
                    boxf, cen_sb[:, :], sabs_sb[0:88, 1:2], MAGIC,
                    ALU.mult, ALU.add), reads=[t_cen, t_sabs], writes=[t_qsc])
                eop("vector", lambda v: v.tensor_scalar(
                    boxf, boxf, MAGIC, 127.0, ALU.subtract, ALU.min),
                    reads=[t_qsc], writes=[t_qsc])
                eop("vector", lambda v: v.tensor_scalar(
                    boxf, boxf, -128.0, None, ALU.max),
                    writes=[t_qsc])
                eop("vector", lambda v: v.reciprocal(
                    sabs_sb[0:88, 0:1], sabs_sb[0:88, 1:2]), writes=[t_sabs])
                eop("vector", lambda v: v.tensor_scalar(
                    cen_sb[:, :], boxf, sabs_sb[0:88, 0:1], None,
                    ALU.mult), reads=[t_qsc, t_sabs], writes=[t_cen])
                eop("vector", lambda v: v.tensor_scalar(
                    cen_sb[:, :], cen_sb[:, :], mean_sb[0:88, 0:1], None,
                    ALU.add), reads=[t_mean], writes=[t_cen])
                dop(f"d_deq{name}", scl_dram.ap()[midx * 22528:(midx + 1) * 22528
                                                  ].rearrange(
                    "(p b) -> p b", p=88), cen_sb[:, :],
                    reads=[t_cen], writes=[t_scld])
                for t in range(nt):
                    r = rows[t]
                    off = midx * 22528 + sum(rows[:t]) * nblk
                    dop(f"d_deqA{name}",
                        deqA_sb[0:r, t * nblk:(t + 1) * nblk],
                        scl_dram.ap()[off:off + r * nblk].rearrange(
                            "(p b) -> p b", p=r),
                        reads=[t_scld], writes=[t_deqA])
            ems.append(prelude)
            ncc = wdt // 128
            for rc in range(nt):
                r = rows[rc]
                r0 = sum(rows[:rc])
                for cc in range(ncc):
                    qs = (rc * ncc + cc) % 3

                    def unitA(r=r, r0=r0, cc=cc, qs=qs, rc=rc):
                        dop(f"d_qt{qs}", qtl_sb[0:r, qs * 128:(qs + 1) * 128],
                            qd[r0:r0 + r, cc * 128:(cc + 1) * 128],
                            reads=[t_qd], writes=[t_qtl[qs]])
                        db = deqA_sb[0:r, rc * nblk + cc * 2:
                                     rc * nblk + cc * 2 + 2
                                     ].rearrange("p (b o) -> p b o", o=1
                                                 ).broadcast_to([r, 2, 64])
                        wv = qtl_sb[0:r, qs * 128:(qs + 1) * 128].rearrange(
                            "p (b k) -> p b k", k=64)
                        eop("vector", (lambda v, wv=wv, db=db: v.tensor_tensor(
                            wv, wv, db, ALU.mult)),
                            reads=[t_deqA], writes=[t_qtl[qs]])

                    def unitB(r=r, r0=r0, cc=cc, qs=qs, rc=rc):
                        ts_ = cc % 2
                        eop("tensor", (lambda te, qs=qs, ts_=ts_, r=r:
                            te.transpose(
                                ptr[ts_][:, 0:r],
                                qtl_sb[0:r, qs * 128:(qs + 1) * 128],
                                eyeb[0:r, 0:r])),
                            reads=[t_qtl[qs], t_eye], writes=[t_ptr[ts_]])
                        ss = cc % 2
                        eop("scalar", (lambda s, ts_=ts_, ss=ss, r=r: s.copy(
                            trsp_sb[:, ss * 128: ss * 128 + r],
                            ptr[ts_][:, 0:r])),
                            reads=[t_ptr[ts_]], writes=[t_trsp[ss]])
                        for (fr0, fr1, pr0) in padmap(cc):
                            dop(f"d_stw{name}{ss}",
                                contrib[pr0: pr0 + (fr1 - fr0), r0: r0 + r],
                                trsp_sb[fr0 - cc * 128: fr1 - cc * 128,
                                        ss * 128: ss * 128 + r],
                                reads=[t_trsp[ss]], writes=[t_contrib])

                    ems.append((unitA, unitB))
            return ems

        def pm_id(cc):
            return [(cc * 128, (cc + 1) * 128, cc * 128)]

        def pm_w3(cc):
            f0, f1 = cc * 128, (cc + 1) * 128
            out = []
            a = f0
            while a < f1:
                seg_end = min(f1, ((a // 704) + 1) * 704)
                out.append((a, seg_end, a + 64 * (a // 704)))
                a = seg_end
            return out

        mark('nf4ladder')
        deq_lazy = [
            lambda: nf4_deq("w1", 0, t_q1, w1T_contrib.ap(), t_w1C, pm_id),
            lambda: nf4_deq("w2", 1, t_q2, w2T_contrib.ap(), t_w2C, pm_id),
            lambda: nf4_deq("w3", 2, t_q3, w3T_contrib.ap(), t_w3C, pm_w3),
        ]
        deq_qs = [None, None, None]

        def emit_pads():
            _emit_pads()

        def _emit_pads():
            for cc in range(16):
                dop("d_pad1", w1T_contrib.ap()[cc * 128:(cc + 1) * 128, 704:768],
                    zpad_sb[:, :], reads=[t_zpad], writes=[t_w1C])
                dop("d_pad2", w2T_contrib.ap()[cc * 128:(cc + 1) * 128, 704:768],
                    zpad_sb[:, :], reads=[t_zpad], writes=[t_w2C])
            for k in range(8):
                for cseg in range(4):
                    dop("d_pad3", w3T_contrib.ap()[768 * k + 704: 768 * k + 768,
                                                   cseg * 64:(cseg + 1) * 64],
                        zpad_sb[0:64, 0:64], reads=[t_zpad], writes=[t_w3C])

        def emit_wags():
            _emit_wags()

        def _emit_wags():
            for nm, contrib_t, arena_t, tC, tA in (
                    ("w1", w1T_contrib, w1T_arena, t_w1C, t_w1A),
                    ("w2", w2T_contrib, w2T_arena, t_w2C, t_w2A),
                    ("w3", w3T_contrib, w3T_arena, t_w3C, t_w3A)):
                agw = all_store_waits([f"d_stw{nm}0", f"d_stw{nm}1",
                                       "d_pad1", "d_pad2", "d_pad3"])
                ident = p.op("gpsimd", (lambda g, ci=contrib_t, ai=arena_t:
                    g.collective_compute(
                        "AllGather", ALU.bypass, replica_groups=[list(range(8))],
                        ins=[ci.ap().opt()], outs=[ai.ap().opt()])),
                    waits=agw, sem=s_cc)
                tA.wid = ident
            for r in range(8):
                dop("d_w3f", w3T_flat.ap()[:, r * 256:(r + 1) * 256],
                    w3T_arena.ap()[r], reads=[t_w3A], writes=[t_w3F],
                    engine="gpsimd")

            for nm, ci, ai, tA in (("wq1", wq1_contrib, wq1_arena, t_wq1A),
                                   ("wc1", wc1_contrib, wc1_arena, t_wc1A),
                                   ("wk1", wk1_contrib, wk1_arena, t_wk1A),
                                   ("wv1", wv1_contrib, wv1_arena, t_wv1A)):
                agw = all_store_waits([f"d_c1_{nm}"])
                ident = p.op("gpsimd", (lambda g, ci=ci, ai=ai:
                    g.collective_compute(
                        "AllGather", ALU.bypass, replica_groups=[list(range(8))],
                        ins=[ci.ap().opt()], outs=[ai.ap().opt()])),
                    waits=agw, sem=s_cc)
                tA.wid = ident

        mark('nf4deq')

        t_kTh = Trk()
        t_vh = Trk()
        t_e = [Trk(), Trk()]
        t_sel = Trk()
        t_red = Trk()
        t_lngs = Trk()
        t_lng2 = Trk()
        t_lngT = [Trk()]
        t_cq = Trk()
        t_nb = Trk()

        dop("d_sel", sel_sb[:, :], sel_d.ap(), writes=[t_sel],
            extra=prep_done)

        def emit_units(ems):
            for i in range(0, len(ems), 3):
                pair = ems[i:i + 3]
                for em in pair:
                    if isinstance(em, tuple):
                        em[0]()
                    else:
                        em()
                for em in pair:
                    if isinstance(em, tuple):
                        em[1]()

        for h in range(NH0):
            for r in range(4):
                dop(f"d_kTh{r}", kTh_sb[:, r * TOK:(r + 1) * TOK],
                    kv0_g_k(r)[h * 128:(h + 1) * 128, :],
                    reads=[t_kvG], writes=[t_kTh])
                if r == 0:
                    dop("d_vh0", vh_sb[:, :],
                        vrep_dram.ap()[h * 128:(h + 1) * 128, :],
                        reads=[t_vrep[h % 2]], writes=[t_vh])
            kv_waits = [(dma_sems[f"d_kTh{r}"], dma_sems[f"d_kTh{r}"].sched)
                        for r in range(4)]
            kv_waits += [(dma_sems["d_vh0"], dma_sems["d_vh0"].sched)]

            for qt in range(8):
                for quar in range(4):
                    pr = quar % 2
                    t_p1 = (t_pP1 if pr == 0 else t_pAB[0])
                    for kc in range(2):
                        kk = quar * 2 + kc
                        dst = (pP1[:, kc * 512:(kc + 1) * 512] if pr == 0
                               else pAB[kc][:, :])
                        eop("tensor", (lambda te, h=h, qt=qt, kk=kk, dst=dst:
                            te.matmul(
                                dst,
                                qT_sb[:, h * TOK + qt * 128:
                                      h * TOK + (qt + 1) * 128],
                                kTh_sb[:, kk * 512:(kk + 1) * 512],
                                start=True, stop=True)),
                            reads=[t_qT, t_kTh],
                            writes=[t_p1 if pr == 0 or kc == 0 else t_pAB[1]],
                            extra=(kv_waits if (qt == 0 and quar == 0 and kc == 0)
                                   else []))
                    for kc in range(2):
                        pv = (pP1[:, kc * 512:(kc + 1) * 512] if pr == 0
                              else pAB[kc][:, :]).rearrange(
                            "p (j k) -> p j k", j=4)
                        eop("vector", (lambda v, qt=qt, q=quar, kc=kc, pv=pv:
                            v.tensor_reduce(
                                red_sb[:, qt * 64 + 32 + q * 8 + kc * 4:
                                       qt * 64 + 36 + q * 8 + kc * 4],
                                pv, AX.X, ALU.max)),
                            reads=[t_p1 if pr == 0 or kc == 0 else t_pAB[1]],
                            writes=[t_red])
            for qt in range(8):
                mr4 = red_sb[:, qt * 64 + 32: qt * 64 + 64].rearrange(
                    "p (q j) -> p q j", j=8)
                eop("vector", (lambda v, qt=qt, mr4=mr4: v.tensor_reduce(
                    cq_sb[:, qt * 4:(qt + 1) * 4], mr4, AX.X, ALU.max)),
                    reads=[t_red], writes=[t_cq])
            eop("vector", lambda v: v.tensor_scalar(
                nb_sb[:, :], cq_sb[:, :], -SCALE0, None, ALU.mult),
                reads=[t_cq], writes=[t_nb])
            for qt in range(8):
                for quar in range(4):
                    pr = quar % 2
                    t_p1 = (t_pP1 if pr == 0 else t_pAB[0])
                    for kc in range(2):
                        kk = quar * 2 + kc
                        dst = (pP1[:, kc * 512:(kc + 1) * 512] if pr == 0
                               else pAB[kc][:, :])
                        eop("tensor", (lambda te, h=h, qt=qt, kk=kk, dst=dst:
                            te.matmul(
                                dst,
                                qT_sb[:, h * TOK + qt * 128:
                                      h * TOK + (qt + 1) * 128],
                                kTh_sb[:, kk * 512:(kk + 1) * 512],
                                start=True, stop=True)),
                            reads=[t_qT, t_kTh],
                            writes=[t_p1 if pr == 0 or kc == 0 else t_pAB[1]])
                    es = quar % 2
                    for kc in range(2):
                        pv2 = (pP1[:, kc * 512:(kc + 1) * 512] if pr == 0
                               else pAB[kc][:, :])
                        eop("scalar", (lambda s, qt=qt, q=quar, es=es, kc=kc,
                                       pv2=pv2: s.activation(
                            e_sb[:, es * 1024 + kc * 512:
                                 es * 1024 + (kc + 1) * 512],
                            pv2, AF.Exp, scale=SCALE0,
                            bias=nb_sb[:, qt * 4 + q: qt * 4 + q + 1])),
                            reads=[(t_p1 if pr == 0 or kc == 0 else t_pAB[1]),
                                   t_nb],
                            writes=[t_e[es]])
                    e3 = e_sb[:, es * 1024:(es + 1) * 1024].rearrange(
                        "p (j k) -> p j k", j=8)
                    eop("vector", (lambda v, qt=qt, q=quar, e3=e3: v.reduce_sum(
                        red_sb[:, qt * 64 + q * 8: qt * 64 + q * 8 + 8],
                        e3, AX.X)),
                        reads=[t_e[es]], writes=[t_red])
            r3 = lambda o: red_sb[:, :].rearrange("p (qt c) -> p qt c", qt=8)[
                :, :, o:o + 32]
            l3 = lambda a, b: lngs_sb[:, a:b].rearrange("p (qt c) -> p qt c", qt=8)
            cq4 = cq_sb[:, :].rearrange("p (qt q) -> p qt q", qt=8)
            cqb = cq_sb[:, :].rearrange("p (qt q o) -> p qt q o", qt=8, o=1
                                        ).broadcast_to([128, 8, 4, 8])
            eop("vector", lambda v: v.tensor_scalar(
                l3(0, 256), r3(0), 1e-38, None, ALU.max),
                reads=[t_red], writes=[t_lngs])
            eop("scalar", lambda s: s.activation(
                lngs_sb[:, 0:256], lngs_sb[:, 0:256], AF.Ln),
                reads=[t_lngs], writes=[t_lngs])
            eop("vector", lambda v: v.scalar_tensor_tensor(
                l3(256, 512), r3(32), -SCALE0, l3(0, 256),
                ALU.mult, ALU.add),
                reads=[t_red, t_lngs], writes=[t_lngs])
            l4 = lambda a, b: lngs_sb[:, a:b].rearrange(
                "p (qt q j) -> p qt q j", qt=8, q=4)
            eop("vector", lambda v: v.scalar_tensor_tensor(
                l4(256, 512), cqb, SCALE0, l4(256, 512),
                ALU.mult, ALU.add),
                reads=[t_cq, t_lngs], writes=[t_lngs])
            eop("vector", lambda v: v.tensor_reduce(
                stat_sb[:, 0:8], cq4, AX.X, ALU.max),
                reads=[t_cq], writes=[t_stat])
            eop("vector", lambda v: v.tensor_scalar(
                stat_sb[:, 0:8], stat_sb[:, 0:8], SCALE0, None, ALU.mult),
                writes=[t_stat])
            cmax_b = stat_sb[:, 0:8].rearrange("p (qt o) -> p qt o", o=1
                                               ).broadcast_to([128, 8, 4])
            eop("vector", lambda v: v.scalar_tensor_tensor(
                wf2_sb[:, :].rearrange("p (qt q) -> p qt q", qt=8),
                cq4, SCALE0, cmax_b, ALU.mult, ALU.subtract),
                reads=[t_cq, t_stat], writes=[t_nb])
            eop("scalar", lambda s: s.activation(
                wf2_sb[:, :], wf2_sb[:, :], AF.Exp),
                reads=[t_nb], writes=[t_nb])
            wfb = wf2_sb[:, :].rearrange("p (qt q o) -> p qt q o", qt=8, o=1
                                         ).broadcast_to([128, 8, 4, 8])
            r4m = red_sb[:, :].rearrange("p (qt c) -> p qt c", qt=8)[
                :, :, 0:32].rearrange("p qt (q j) -> p qt q j", q=4)
            sc4 = lambda a, b: lngs_sb[:, a:b].rearrange(
                "p (qt q j) -> p qt q j", qt=8, q=4)
            eop("vector", lambda v: v.tensor_tensor(
                sc4(0, 256), r4m, wfb, ALU.mult),
                reads=[t_red, t_nb], writes=[t_lngs])
            eop("vector", lambda v: v.reduce_sum(
                stat_sb[:, 8:16], l3(0, 256), AX.X),
                reads=[t_lngs], writes=[t_stat])
            eop("scalar", lambda s: s.activation(
                stat_sb[:, 8:16], stat_sb[:, 8:16], AF.Ln),
                reads=[t_stat], writes=[t_stat])
            eop("vector", lambda v: v.tensor_tensor(
                stat_sb[:, 8:16], stat_sb[:, 8:16], stat_sb[:, 0:8], ALU.add),
                writes=[t_stat])
            lnL_b = stat_sb[:, 8:16].rearrange("p (qt o) -> p qt o", o=1
                                               ).broadcast_to([128, 8, 32])
            eop("vector", lambda v: v.tensor_tensor(
                l3(256, 512), l3(256, 512), lnL_b, ALU.subtract),
                reads=[t_stat], writes=[t_lngs])
            hi_v = lng2_sb[:, :].rearrange("p (qt j two) -> p qt j two",
                                           qt=8, two=2)
            eop("vector", lambda v: v.tensor_scalar(
                hi_v[:, :, :, 0:1].squeeze(3), l3(256, 512),
                1.0 / SCALE0, None, ALU.mult),
                reads=[t_lngs], writes=[t_lng2])
            eop("vector", lambda v: v.tensor_scalar(
                lngs_sb[:, 0:256], lngs_sb[:, 256:512], 1.0 / SCALE0, None,
                ALU.mult), reads=[t_lngs], writes=[t_lngs])
            eop("vector", lambda v: v.tensor_tensor(
                hi_v[:, :, :, 1:2].squeeze(3), l3(0, 256),
                hi_v[:, :, :, 0:1].squeeze(3), ALU.subtract),
                reads=[t_lngs, t_lng2], writes=[t_lng2])

            ls = 0
            for qt in range(8):
                ts = qt % 2
                eop("tensor", (lambda te, qt=qt, ts=ts: te.transpose(
                    ptr[ts][0:64, 0:128],
                    lng2_sb[:, qt * 64:(qt + 1) * 64], eyeb[:, :])),
                    reads=[t_lng2, t_eye], writes=[t_ptr[ts]])
                eop("scalar", (lambda s, qt=qt, ts=ts, ls=ls: s.copy(
                    lngT_sb[0:64, qt * 128: (qt + 1) * 128],
                    ptr[ts][0:64, 0:128])),
                    reads=[t_ptr[ts]], writes=[t_lngT[ls]])

            for qc in range(2):
                def pv0(j):
                    ph = j % 2
                    eop("tensor", (lambda te, j=j, ph=ph: te.matmul(
                        pattn[:, :],
                        vh_sb[:, j * 128:(j + 1) * 128],
                        phat_sb[:, ph * 512:(ph + 1) * 512],
                        start=(j == 0), stop=(j == 31))),
                        reads=[t_vh, t_phat[ph]],
                        writes=([t_pattn] if j in (0, 31) else []))
                for j in range(32):
                    ps = j % 2
                    pb = pAB[ps]
                    eop("tensor", (lambda te, h=h, j=j, qc=qc, pb=pb: te.matmul(
                        pb[:, :],
                        kTh_sb[:, j * 128:(j + 1) * 128],
                        qT_sb[:, h * TOK + qc * 512: h * TOK + (qc + 1) * 512],
                        start=True, stop=False)),
                        reads=[t_kTh, t_qT], writes=[t_pAB[ps]])
                    eop("tensor", (lambda te, j=j, qc=qc, ls=ls, pb=pb: te.matmul(
                        pb[:, :], sel_sb[:, j * 128:(j + 1) * 128],
                        lngT_sb[0:64, qc * 512: (qc + 1) * 512],
                        start=False, stop=True)),
                        reads=[t_sel, t_lngT[ls]], writes=[t_pAB[ps]])
                    ph = j % 2
                    eop("scalar", (lambda s, ph=ph, pb=pb: s.activation(
                        phat_sb[:, ph * 512:(ph + 1) * 512], pb[:, :],
                        AF.Exp, scale=SCALE0)),
                        reads=[t_pAB[ps]], writes=[t_phat[ph]])
                    if j > 0:
                        pv0(j - 1)
                pv0(31)
                eop("scalar", (lambda s, h=h, qc=qc: s.copy(
                    lnT_sb[:, h * TOK + qc * 512: h * TOK + (qc + 1) * 512],
                    pattn[:, :])), reads=[t_pattn], writes=[t_lnT])
            mark(f'h{h}')
            if h < 12:
                n_per = (len(ladder_q) + 11) // 12
                for em in ladder_q[h * n_per:(h + 1) * n_per]:
                    em()
            if 6 <= h <= 10:
                if deq_qs[0] is None:
                    deq_qs[0] = deq_lazy[0]()
                q = deq_qs[0]
                n_per = (len(q) + 4) // 5
                emit_units(q[(h - 6) * n_per:(h - 5) * n_per])
            if 11 <= h <= 12:
                if deq_qs[1] is None:
                    deq_qs[1] = deq_lazy[1]()
                q = deq_qs[1]
                n_per = (len(q) + 1) // 2
                emit_units(q[(h - 11) * n_per:(h - 10) * n_per])
            if 13 <= h <= 14:
                if deq_qs[2] is None:
                    deq_qs[2] = deq_lazy[2]()
                q = deq_qs[2]
                n_per = (len(q) + 1) // 2
                emit_units(q[(h - 13) * n_per:(h - 12) * n_per])
            if h == 14:
                emit_pads()
                emit_wags()
        mark('l0attn')
        att_done = [(s_mm, s_mm.sched), (s_ev, s_ev.sched),
                    (s_ve, s_ve.sched)]
        att.close()

        for oc in range(8):
            slot = oc % 2
            rank = (oc * 256) // WOUT_SH
            a = woutT_arena.ap()[rank].rearrange("(k p) c -> p k c", p=128)
            dd = wq_sb[:, slot * 4096: slot * 4096 + 4096].rearrange(
                "p (k c) -> p k c", k=16)
            dop(f"d_wq{slot}", dd, a, reads=[t_woutA], writes=[t_wq[slot]])
            dop("d_bo", bout_sb[:, :], bias_dram.ap()[0:1, oc * 256:(oc + 1) * 256],
                reads=[t_biasd], writes=[t_bout])
            for qt in range(8):
                ps = qt % 2
                pb = pAB[ps]
                for k in range(16):
                    eop("tensor", (lambda te, s=slot, k=k, qt=qt, pb=pb: te.matmul(
                        pb[:, 0:256],
                        lnT_sb[:, k * TOK + qt * 128: k * TOK + (qt + 1) * 128],
                        wq_sb[:, s * 4096 + k * 256: s * 4096 + (k + 1) * 256],
                        start=(k == 0), stop=False)),
                        reads=([t_wq[slot], t_lnT] if k == 0 else []),
                        writes=([t_pAB[ps]] if k == 0 else []), need_id=(k == 0))
                eop("tensor", (lambda te, oc=oc, pb=pb: te.matmul(
                    pb[:, 0:256], onesq_sb[:, :],
                    bout_sb[0:1, :],
                    start=False, stop=True)),
                    reads=[t_ones, t_bout, t_lnT, t_wq[slot]],
                    writes=[t_pAB[ps]])
                eop("vector", (lambda v, qt=qt, oc=oc, pb=pb: v.tensor_tensor(
                    x_sb[:, qt * DM + oc * 256: qt * DM + (oc + 1) * 256],
                    x_sb[:, qt * DM + oc * 256: qt * DM + (oc + 1) * 256],
                    pb[:, 0:256], ALU.add)),
                    reads=[t_pAB[ps]], writes=[t_x[qt]])

        mark('l0outproj')
        ffn = ExitStack()
        h_sb = sb(ffn, "h", [128, 48 * 512], BF16)
        lnscr2_sb = sb(ffn, "lnscr2", [128, 2 * DM], BF16)
        w3r_sb = sb(ffn, "w3r", [128, 2 * 512], BF16)
        t_lnscr2 = [Trk(), Trk()]
        t_h = [Trk() for _ in range(48)]
        t_w3r = [Trk(), Trk()]

        emit_layernorm(t_x, x_tile, lnT_sb, t_lnT, lnscr2_sb, t_lnscr2,
                       att_done)

        def w1_colblock_ap(arena, fc):
            rank, c0 = fc // 6, (fc % 6) * 128
            return arena.ap()[rank].rearrange("(k p) c -> p k c", p=128)[
                :, :, c0:c0 + 128]

        accs = [(pA[:, :], t_pAB[0]), (pB[:, :], t_pAB[1]),
                (pP1[:, 0:512], t_pP1), (pP1[:, 512:1024], t_pP1)]

        for th in range(2):
            up_ps = [(pA[:, :], t_pAB[0]), (pB[:, :], t_pAB[1]),
                     (pP1[:, 0:512], t_pP1), (pP1[:, 512:1024], t_pP1)]

            def evac_fc(fc):
                pa, ta = up_ps[(fc % 2) * 2]
                pb_, tb = up_ps[(fc % 2) * 2 + 1]
                ph = fc % 2
                eop("scalar", (lambda s, ph=ph, pa=pa: s.activation(
                    phat_sb[:, ph * 512:(ph + 1) * 512], pa, AF.Silu)),
                    reads=[ta], writes=[t_phat[ph]])
                eop("vector", (lambda v, fc=fc, ph=ph, pb_=pb_: v.tensor_tensor(
                    h_sb[:, fc * 512:(fc + 1) * 512],
                    phat_sb[:, ph * 512:(ph + 1) * 512], pb_, ALU.mult)),
                    reads=[t_phat[ph], tb], writes=[t_h[fc]])
            for fc in range(48):
                slot = fc % 2
                d1 = wq_sb[:, slot * 4096: slot * 4096 + 2048].rearrange(
                    "p (k c) -> p k c", k=16)
                d2 = wq_sb[:, slot * 4096 + 2048: slot * 4096 + 4096].rearrange(
                    "p (k c) -> p k c", k=16)
                dop(f"d_wq{slot}", d1, w1_colblock_ap(w1T_arena, fc),
                    reads=[t_w1A], writes=[t_wq[slot]])
                dop(f"d_wq{slot}", d2, w1_colblock_ap(w2T_arena, fc),
                    reads=[t_w2A], writes=[t_wq[slot]])
                for ab in range(2):
                    pb, tp = up_ps[(fc % 2) * 2 + ab]
                    co = ab * 2048
                    for k in range(16):
                        eop("tensor", (lambda te, s=slot, k=k, th=th, pb=pb,
                                       co=co: te.matmul(
                            pb,
                            wq_sb[:, s * 4096 + co + k * 128:
                                  s * 4096 + co + (k + 1) * 128],
                            lnT_sb[:, k * TOK + th * 512:
                                   k * TOK + th * 512 + 512],
                            start=(k == 0), stop=(k == 15))),
                            reads=([t_wq[slot], t_lnT] if k in (0, 15) else []),
                            writes=([tp] if k in (0, 15) else []),
                            need_id=(k in (0, 15)))
                if fc > 0:
                    evac_fc(fc - 1)
            evac_fc(47)
            for dc in range(4):
                for fc in range(48):
                    ws = fc % 2
                    dop(f"d_w3r{ws}", w3r_sb[:, ws * 512:(ws + 1) * 512],
                        w3T_flat.ap()[fc * 128:(fc + 1) * 128,
                                      dc * 512:(dc + 1) * 512],
                        reads=[t_w3F], writes=[t_w3r[ws]], extra=att_done)
                    for tb in range(4):
                        pb, tp = accs[tb]
                        eop("tensor", (lambda te, fc=fc, tb=tb, ws=ws, pb=pb:
                            te.matmul(
                                pb,
                                h_sb[:, fc * 512 + tb * 128:
                                     fc * 512 + (tb + 1) * 128],
                                w3r_sb[:, ws * 512:(ws + 1) * 512],
                                start=(fc == 0), stop=(fc == 47))),
                            reads=[t_h[fc], t_w3r[ws]],
                            writes=([tp] if fc in (0, 47) else []))
                for tb in range(4):
                    pb, tp = accs[tb]
                    tg = th * 4 + tb
                    eop("vector", (lambda v, tg=tg, dc=dc, pb=pb: v.tensor_tensor(
                        x_sb[:, tg * DM + dc * 512: tg * DM + (dc + 1) * 512],
                        x_sb[:, tg * DM + dc * 512: tg * DM + (dc + 1) * 512],
                        pb, ALU.add)),
                        reads=[tp], writes=[t_x[tg]])
        ffn.close()



        mark('l0ffn')
        l1w_pre = [(s_mm, s_mm.sched), (s_ev, s_ev.sched), (s_ve, s_ve.sched)]
        l1w = ExitStack()
        wfq_sb = sb(l1w, "wfq", [128, 2 * 2048], F32)
        qz_sb = sb(l1w, "qz", [128, 2 * 2048], BF16)
        boxw_sb = sb(l1w, "boxw", [128, 2048], I32)
        scol_sb = sb(l1w, "scol", [128, 4], F32)
        t_wfq = [Trk(), Trk()]
        t_qz = [Trk(), Trk()]
        t_boxw = Trk()
        t_scol = Trk()
        t_w1eC = Trk()
        t_w2eC = Trk()
        t_w1eA = Trk()
        t_w2eA = Trk()

        for mi in range(2):
            eop("vector", (lambda v, mi=mi: v.tensor_scalar(
                arst_sb[0:1, 5 + 2 * mi:6 + 2 * mi],
                arst_sb[0:1, 3 + mi:4 + mi],
                1.0 / 16777216.0, 1e-5, ALU.mult, ALU.max)),
                reads=[t_arst], writes=[t_arst], extra=l1w_pre)
            eop("vector", (lambda v, mi=mi: v.reciprocal(
                arst_sb[0:1, 6 + 2 * mi:7 + 2 * mi],
                arst_sb[0:1, 5 + 2 * mi:6 + 2 * mi])), writes=[t_arst])
            hv = arst_sb[0:1, 9 + mi:10 + mi].bitcast(BF16)
            eop("vector", (lambda v, mi=mi, hv=hv: v.tensor_copy(
                hv[0:1, 0:1], arst_sb[0:1, 6 + 2 * mi:7 + 2 * mi])),
                writes=[t_arst])
            eop("vector", (lambda v, mi=mi, hv=hv: v.tensor_tensor(
                hv[0:1, 1:2], arst_sb[0:1, 6 + 2 * mi:7 + 2 * mi],
                hv[0:1, 0:1], ALU.subtract)), writes=[t_arst])
            eop("tensor", (lambda te, hv=hv: te.matmul(
                pA[:, 0:1], onesq_sb[:, :], hv[0:1, 0:1],
                start=True, stop=False)),
                reads=[t_arst, t_ones], writes=[t_pAB[0]], extra=l1w_pre)
            eop("tensor", (lambda te, hv=hv: te.matmul(
                pA[:, 0:1], onesq_sb[:, :], hv[0:1, 1:2],
                start=False, stop=True)),
                reads=[t_arst, t_ones], writes=[t_pAB[0]])
            eop("vector", (lambda v, mi=mi: v.tensor_copy(
                scol_sb[:, mi:mi + 1], pA[:, 0:1])),
                reads=[t_pAB[0]], writes=[t_scol], extra=l1w_pre)

        boxw_f = boxw_sb[:, :].bitcast(F32)

        def tern_quant(srcb, nt, cols, contrib, t_contrib, scx):
            for t in range(nt):
                f = t % 2
                dop(f"d_qq", wfq_sb[:, f * 2048: f * 2048 + cols],
                    srcb[t * 128:(t + 1) * 128, :], writes=[t_wfq[f]],
                    extra=l1w_pre)
                bx = boxw_f[:, 0:cols]
                eop("vector", (lambda v, f=f, bx=bx, cols=cols: v.tensor_scalar(
                    bx, wfq_sb[:, f * 2048: f * 2048 + cols],
                    scol_sb[:, scx:scx + 1], MAGIC, ALU.mult, ALU.add)),
                    reads=[t_wfq[f], t_scol], writes=[t_boxw], extra=l1w_pre)
                eop("vector", (lambda v, bx=bx: v.tensor_scalar(
                    bx, bx, MAGIC, 1.0, ALU.subtract, ALU.min)),
                    writes=[t_boxw])
                eop("vector", (lambda v, f=f, bx=bx, cols=cols: v.tensor_scalar(
                    qz_sb[:, f * 2048: f * 2048 + cols], bx,
                    -1.0, None, ALU.max)),
                    reads=[t_boxw], writes=[t_qz[f]])
                for cc in range(cols // 128):
                    ts_ = cc % 2
                    eop("tensor", (lambda te, f=f, cc=cc, ts_=ts_: te.transpose(
                        ptr[ts_][:, :],
                        qz_sb[:, f * 2048 + cc * 128: f * 2048 + (cc + 1) * 128],
                        eyeb[:, :])),
                        reads=[t_qz[f], t_eye], writes=[t_ptr[ts_]])
                    ss = cc % 2
                    eop("scalar", (lambda s, ts_=ts_, ss=ss: s.copy(
                        trsp_sb[:, ss * 128:(ss + 1) * 128], ptr[ts_][:, :])),
                        reads=[t_ptr[ts_]], writes=[t_trsp[ss]])
                    dop(f"d_stww1{ss}",
                        contrib[cc * 128:(cc + 1) * 128, t * 128:(t + 1) * 128],
                        trsp_sb[:, ss * 128:(ss + 1) * 128],
                        reads=[t_trsp[ss]], writes=[t_contrib])

        tern_quant(w1b_sl, 8, 2048, w1e_contrib.ap(), t_w1eC, 0)
        tern_quant(w2b_sl, 16, 1024, w2e_contrib.ap(), t_w2eC, 1)
        l1w.close()

        for nm, ci, ai, tA in (("w1e", w1e_contrib, w1e_arena, t_w1eA),
                               ("w2e", w2e_contrib, w2e_arena, t_w2eA)):
            agw = all_store_waits(["d_sw0", "d_sw1"])
            ident = p.op("gpsimd", (lambda g, ci=ci, ai=ai:
                g.collective_compute(
                    "AllGather", ALU.bypass, replica_groups=[list(range(8))],
                    ins=[ci.ap().opt()], outs=[ai.ap().opt()])),
                waits=agw, sem=s_cc)
            tA.wid = ident

        mark('l1wprep')
        l1pre = [(s_mm, s_mm.sched), (s_ev, s_ev.sched), (s_ve, s_ve.sched)]
        att1 = ExitStack()
        qT1_sb = sb(att1, "qT1", [128, 16 * TOK], BF16)
        kT1_sb = sb(att1, "kT1", [128, SEQ], BF16)
        v1_sb = sb(att1, "v1", [128, SEQ], BF16)
        msqF_sb = sb(att1, "msqF", [128, 2 * 256], F32)
        sqtB_sb = sb(att1, "sqtB", [128, 256], F32)
        mskB_sb = sb(att1, "mskB", [128, 256], F32)
        ptT_sb = sb(att1, "ptT", [128, 256], F32)
        ptm_sb = sb(att1, "ptm", [128, 64], F32)
        eps_sb = sb(att1, "eps1", [128, 64], F32)
        sdia_sb = sb(att1, "sdia", [128, 64], F32)
        gloc_sb = sb(att1, "gloc", [8, 1024], F32)
        tot_sb = sb(att1, "tot", [8, 16], F32)
        wg1_sb = sb(att1, "wg1", [128, 128], BF16)
        tril_sb = sb(att1, "tril", [128, 128], BF16)
        cb_sb = sb(att1, "cb", [1, 256], F32)
        cbb_sb = sb(att1, "cbb", [1, 256], BF16)
        scat_sb = sb(att1, "scat", [128, 16], F32)
        scatb_sb = sb(att1, "scatb", [128, 16], BF16)
        pden = psum(att1, "pden", [1, 512])

        t_qT1 = Trk()
        t_kT1 = Trk()
        t_v1 = Trk()
        t_msqF = Trk()
        t_sqtB = Trk()
        t_mskB = Trk()
        t_ptT = Trk()
        t_ptm = Trk()
        t_eps1 = Trk()
        t_sdia = Trk()
        t_gloc = Trk()
        t_tot = Trk()
        t_den1 = [Trk(), Trk()]
        t_wg1 = Trk()
        t_tril = Trk()
        t_cb = Trk()
        t_cbb = Trk()
        t_scat = Trk()
        t_pden = Trk()
        t_pgd = Trk()
        t_dend = [Trk(), Trk()]
        t_kdia = Trk()
        t_vdia = Trk()
        t_yt = Trk()
        t_ytm = Trk()
        t_rcol = Trk()
        t_cold = [Trk(), Trk()]

        eop("vector", lambda v: v.memset(const_sb[:, 1:2], -GOFF),
            writes=[t_const])
        eop("vector", lambda v: v.memset(const_sb[:, 2:3], 1.0),
            writes=[t_const])
        eop("vector", lambda v: v.memset(const_sb[:, 3:4], LNC1),
            writes=[t_const])

        dop("d_wgl", wg1_sb[:, :], wg_dram.ap(), reads=[t_wg],
            writes=[t_wg1], extra=l1pre)
        dop("d_tril", tril_sb[:, :], tril_d.ap(), writes=[t_tril],
            extra=l1pre)
        dop("d_caus", cb_sb[:, :], caus_in, writes=[t_cb], extra=l1pre)
        eop("vector", lambda v: v.tensor_copy(cbb_sb[:, :], cb_sb[:, :]),
            reads=[t_cb], writes=[t_cbb])
        dop("d_bc1", scat_sb[:, :],
            bc_in.rearrange("o (c p) -> p (o c)", p=128),
            writes=[t_scat], extra=l1pre)
        eop("vector", lambda v: v.tensor_copy(scatb_sb[:, :], scat_sb[:, :]),
            reads=[t_scat], writes=[t_scat])
        dop("d_bc2", bias_dram.ap()[1:2, :].rearrange(
            "o (c p) -> p (o c)", p=128),
            scatb_sb[:, :], reads=[t_scat], writes=[t_biasd])

        s1 = ExitStack()
        lnscr1_sb = sb(s1, "lnscr1", [128, 2 * DM], BF16)
        t_lnscr1 = [Trk(), Trk()]
        emit_layernorm(t_x, x_tile, lnT_sb, t_lnT, lnscr1_sb, t_lnscr1, l1pre)
        s1.close()

        def arena_colblock(arena_t, c0, width):
            return arena_t.ap().rearrange("(k p) c -> p k c", p=128)[
                :, :, c0:c0 + width]

        for ob in range(16):
            slot = ob % 2
            dd = wq_sb[:, slot * 4096: slot * 4096 + 2048].rearrange(
                "p (k c) -> p k c", k=16)
            dop(f"d_wq{slot}", dd, arena_colblock(wq1_arena, ob * 128, 128),
                reads=[t_wq1A], writes=[t_wq[slot]])
            for tc in range(2):
                ps = tc % 2
                pb = pAB[ps]
                for k in range(16):
                    eop("tensor", (lambda te, s=slot, k=k, tc=tc, pb=pb: te.matmul(
                        pb[:, :],
                        wq_sb[:, s * 4096 + k * 128: s * 4096 + (k + 1) * 128],
                        lnT_sb[:, k * TOK + tc * 512: k * TOK + (tc + 1) * 512],
                        start=(k == 0), stop=(k == 15))),
                        reads=([t_wq[slot], t_lnT] if k in (0, 15) else []),
                        writes=([t_pAB[ps]] if k in (0, 15) else []),
                        need_id=(k in (0, 15)))
                eop("scalar", (lambda s, ob=ob, tc=tc, pb=pb: s.copy(
                    qT1_sb[:, ob * TOK + tc * 512: ob * TOK + (tc + 1) * 512],
                    pb[:, :])), reads=[t_pAB[ps]], writes=[t_qT1])

        for ob in range(8):
            slot = ob % 2
            dd = wq_sb[:, slot * 4096: slot * 4096 + 2048].rearrange(
                "p (k c) -> p k c", k=16)
            dop(f"d_wq{slot}", dd, arena_colblock(wk1_arena, ob * 128, 128),
                reads=[t_wk1A], writes=[t_wq[slot]])
            for tc in range(2):
                ps = tc % 2
                pb = pAB[ps]
                for k in range(16):
                    eop("tensor", (lambda te, s=slot, k=k, tc=tc, pb=pb: te.matmul(
                        pb[:, :],
                        wq_sb[:, s * 4096 + k * 128: s * 4096 + (k + 1) * 128],
                        lnT_sb[:, k * TOK + tc * 512: k * TOK + (tc + 1) * 512],
                        start=(k == 0), stop=(k == 15))),
                        reads=([t_wq[slot], t_lnT] if k in (0, 15) else []),
                        writes=([t_pAB[ps]] if k in (0, 15) else []),
                        need_id=(k in (0, 15)))
                ss = tc % 2
                eop("scalar", (lambda s, ss=ss, pb=pb: s.copy(
                    kvst_sb[:, ss * 512:(ss + 1) * 512], pb[:, :])),
                    reads=[t_pAB[ps]], writes=[t_phat[ss]])
                dop(f"d_k1c{ss}",
                    kv1_c_k()[ob * 128:(ob + 1) * 128, tc * 512:(tc + 1) * 512],
                    kvst_sb[:, ss * 512:(ss + 1) * 512],
                    reads=[t_phat[ss]], writes=[t_kv1C])

        for oc in range(4):
            o0 = oc * 256
            slot = oc % 2
            dd = wq_sb[:, slot * 4096: slot * 4096 + 4096].rearrange(
                "p (k c) -> p k c", k=16)
            dop(f"d_wq{slot}", dd, arena_colblock(wv1_arena, o0, 256),
                reads=[t_wv1A], writes=[t_wq[slot]])
            for tt in range(8):
                ps = tt % 2
                pb = pAB[ps]
                for k in range(16):
                    eop("tensor", (lambda te, s=slot, k=k, tt=tt, pb=pb: te.matmul(
                        pb[:, 0:256],
                        lnT_sb[:, k * TOK + tt * 128: k * TOK + (tt + 1) * 128],
                        wq_sb[:, s * 4096 + k * 256: s * 4096 + (k + 1) * 256],
                        start=(k == 0), stop=(k == 15))),
                        reads=([t_wq[slot], t_lnT] if k in (0, 15) else []),
                        writes=([t_pAB[ps]] if k in (0, 15) else []),
                        need_id=(k in (0, 15)))
                ss = tt % 2
                eop("scalar", (lambda s, ss=ss, pb=pb: s.copy(
                    kvst_sb[:, ss * 512: ss * 512 + 256], pb[:, 0:256])),
                    reads=[t_pAB[ps]], writes=[t_phat[ss]])
                dop(f"d_v1c{ss}",
                    kv1_c_v()[tt * 128:(tt + 1) * 128, oc * 256:(oc + 1) * 256],
                    kvst_sb[:, ss * 512: ss * 512 + 256],
                    reads=[t_phat[ss]], writes=[t_kv1C])

        for tc in range(2):
            ps = tc % 2
            pb = pAB[ps]
            for k in range(16):
                eop("tensor", (lambda te, k=k, tc=tc, pb=pb: te.matmul(
                    pb[0:8, 0:512],
                    wg1_sb[:, k * 8:(k + 1) * 8],
                    lnT_sb[:, k * TOK + tc * 512: k * TOK + (tc + 1) * 512],
                    start=(k == 0), stop=(k == 15))),
                    reads=([t_wg1, t_lnT] if k in (0, 15) else []),
                    writes=([t_pAB[ps]] if k in (0, 15) else []),
                    need_id=(k in (0, 15)))
            eop("vector", (lambda v, tc=tc, pb=pb: v.tensor_copy(
                gloc_sb[0:8, tc * 512:(tc + 1) * 512], pb[0:8, 0:512])),
                reads=[t_pAB[ps]], writes=[t_gloc])
        dop("d_g1c", kv1_c_g(), gloc_sb[:, :].bitcast(BF16),
            reads=[t_gloc], writes=[t_kv1C])

        agw = all_store_waits(["d_k1c0", "d_k1c1", "d_v1c0", "d_v1c1", "d_g1c"])
        id_agkv1 = p.op("gpsimd", lambda g: g.collective_compute(
            "AllGather", ALU.bypass,
            replica_groups=[[0, 1, 2, 3], [4, 5, 6, 7]],
            ins=[kv1_contrib.ap().opt()], outs=[kv1_gather.ap().opt()]),
            waits=agw, sem=s_cc)
        t_kv1G.wid = id_agkv1
        t_vrep1 = [Trk(), Trk()]
        vrep1_v = vrep1_dram.ap().rearrange("(hp) (j d) -> hp j d", j=32)
        for hh in range(8):
            for rr in range(4):
                vs = kv1_g_v(rr).rearrange("(j p) d -> p j d", p=128)[
                    :, :, hh * 128:(hh + 1) * 128]
                dop(f"d_vrp{hh % 2}",
                    vrep1_v[hh * 128:(hh + 1) * 128, rr * 8:(rr + 1) * 8, :],
                    vs, reads=[t_kv1G], writes=[t_vrep1[hh % 2]])

        mark('l1proj')
        s2 = ExitStack()
        gch_sb = sb(s2, "gch", [8, 1024], F32)
        pg_sb = sb(s2, "pgs", [8, 1024], F32)
        hi_sb = pg_sb[:, :].bitcast(BF16)[0:8, 0:1024]
        lo_sb = pg_sb[:, :].bitcast(BF16)[0:8, 1024:2048]
        mr_sb = sb(s2, "mr", [8, 4], F32)
        t_gch = Trk()
        t_pg = Trk()
        t_hi = t_pg
        t_lo = t_pg
        t_mr = Trk()
        eop("vector", lambda v: v.memset(tot_sb[:, 4:5], 0.0), writes=[t_tot],
            extra=l1pre)
        dop("d_mr", mr_sb[:, :], maskR_in, writes=[t_mr], extra=l1pre)

        def gate_chunk(load_fn, dst_sb, t_dst, base_j):
            load_fn()
            eop("scalar", lambda s: s.activation(
                pg_sb[0:8, :], gch_sb[0:8, :], AF.Exp, scale=-1.0,
                bias=const_sb[0:8, 1:2]),
                reads=[t_gch, t_const], writes=[t_pg])
            eop("scalar", lambda s: s.activation(
                pg_sb[0:8, :], pg_sb[0:8, :], AF.Ln,
                bias=const_sb[0:8, 2:3]),
                reads=[t_const], writes=[t_pg])
            eop("vector", lambda v: v.tensor_tensor_scan(
                gch_sb[0:8, :], pg_sb[0:8, :], pg_sb[0:8, :], 0.0,
                ALU.add, ALU.bypass),
                reads=[t_pg], writes=[t_gch])
            eop("vector", lambda v: v.tensor_scalar(
                gch_sb[0:8, :], gch_sb[0:8, :], tot_sb[0:8, 4:5], None,
                ALU.add), reads=[t_tot], writes=[t_gch])
            eop("vector", lambda v: v.tensor_copy(hi_sb, gch_sb[0:8, :]),
                reads=[t_gch], writes=[t_hi])
            eop("vector", lambda v: v.tensor_tensor(
                lo_sb, gch_sb[0:8, :], hi_sb, ALU.subtract),
                reads=[t_gch, t_hi], writes=[t_lo])
            for jl in range(8):
                eop("tensor", (lambda te, jl=jl: te.transpose(
                    ptrA[0:128, 0:8], hi_sb[0:8, jl * 128:(jl + 1) * 128],
                    eyeb[0:8, 0:8])),
                    reads=[t_hi, t_eye], writes=[t_ptr[0]])
                eop("scalar", lambda s: s.copy(trsp_sb[:, 0:8],
                                               ptrA[0:128, 0:8]),
                    reads=[t_ptr[0]], writes=[t_trsp[0]])
                eop("tensor", (lambda te, jl=jl: te.transpose(
                    ptrB[0:128, 0:8], lo_sb[0:8, jl * 128:(jl + 1) * 128],
                    eyeb[0:8, 0:8])),
                    reads=[t_lo, t_eye], writes=[t_ptr[1]])
                eop("vector", (lambda v, jl=jl: v.tensor_tensor(
                    dst_sb[:, (base_j + jl) * 8:(base_j + jl + 1) * 8],
                    trsp_sb[:, 0:8], ptrB[0:128, 0:8], ALU.add)),
                    reads=[t_trsp[0], t_ptr[1]], writes=[t_dst])

        for rr in range(4):
            def load_rr(rr=rr):
                dop("d_g1l", gch_sb[:, :].bitcast(BF16), kv1_g_g(rr),
                    reads=[t_kv1G], writes=[t_gch])
            gate_chunk(load_rr, ptT_sb, t_ptT, rr * 8)
            eop("vector", (lambda v, rr=rr: v.tensor_copy(
                tot_sb[0:8, rr:rr + 1], gch_sb[0:8, 1023:1024])),
                reads=[t_gch], writes=[t_tot])
            eop("vector", (lambda v, rr=rr: v.tensor_copy(
                tot_sb[0:8, 4:5], tot_sb[0:8, rr:rr + 1])),
                writes=[t_tot])

        eop("vector", lambda v: v.tensor_tensor(
            tot_sb[0:8, 8:12], tot_sb[0:8, 0:4], mr_sb[0:8, :], ALU.mult),
            reads=[t_mr], writes=[t_tot])
        eop("vector", lambda v: v.reduce_sum(
            tot_sb[0:8, 5:6], tot_sb[0:8, 8:12], AX.X), writes=[t_tot])
        eop("vector", lambda v: v.tensor_copy(gch_sb[0:8, :], gloc_sb[0:8, :]),
            reads=[t_gloc], writes=[t_gch])
        eop("scalar", lambda s: s.activation(
            pg_sb[0:8, :], gch_sb[0:8, :], AF.Exp, scale=-1.0,
            bias=const_sb[0:8, 1:2]),
            reads=[t_gch, t_const], writes=[t_pg])
        eop("scalar", lambda s: s.activation(
            pg_sb[0:8, :], pg_sb[0:8, :], AF.Ln, bias=const_sb[0:8, 2:3]),
            reads=[t_const], writes=[t_pg])
        eop("vector", lambda v: v.tensor_tensor_scan(
            gch_sb[0:8, :], pg_sb[0:8, :], pg_sb[0:8, :], 0.0,
            ALU.add, ALU.bypass),
            reads=[t_pg], writes=[t_gch])
        eop("vector", lambda v: v.tensor_scalar(
            gch_sb[0:8, :], gch_sb[0:8, :], tot_sb[0:8, 5:6], None, ALU.add),
            reads=[t_tot], writes=[t_gch])
        eop("vector", lambda v: v.tensor_copy(hi_sb, gch_sb[0:8, :]),
            reads=[t_gch], writes=[t_hi])
        eop("vector", lambda v: v.tensor_tensor(
            lo_sb, gch_sb[0:8, :], hi_sb, ALU.subtract),
            reads=[t_gch, t_hi], writes=[t_lo])
        for jl in range(8):
            eop("tensor", (lambda te, jl=jl: te.transpose(
                ptrA[0:128, 0:8], hi_sb[0:8, jl * 128:(jl + 1) * 128],
                eyeb[0:8, 0:8])),
                reads=[t_hi, t_eye], writes=[t_ptr[0]])
            eop("scalar", lambda s: s.copy(trsp_sb[:, 0:8], ptrA[0:128, 0:8]),
                reads=[t_ptr[0]], writes=[t_trsp[0]])
            eop("tensor", (lambda te, jl=jl: te.transpose(
                ptrB[0:128, 0:8], lo_sb[0:8, jl * 128:(jl + 1) * 128],
                eyeb[0:8, 0:8])),
                reads=[t_lo, t_eye], writes=[t_ptr[1]])
            eop("vector", (lambda v, jl=jl: v.tensor_tensor(
                ptm_sb[:, jl * 8:(jl + 1) * 8],
                trsp_sb[:, 0:8], ptrB[0:128, 0:8], ALU.add)),
                reads=[t_trsp[0], t_ptr[1]], writes=[t_ptm])
        s2.close()

        eop("scalar", lambda s: s.activation(
            eps_sb[:, :], ptm_sb[:, :], AF.Exp, bias=const_sb[:, 3:4]),
            reads=[t_ptm, t_const], writes=[t_eps1])
        eop("scalar", lambda s: s.activation(
            sdia_sb[:, :], ptm_sb[:, :], AF.Exp, scale=0.5),
            reads=[t_ptm], writes=[t_sdia])
        eop("scalar", lambda s: s.activation(
            sqtB_sb[:, :], ptT_sb[:, :], AF.Exp, scale=0.5),
            reads=[t_ptT], writes=[t_sqtB])
        eop("tensor", lambda te: te.matmul(
            pA[:, 0:256], onesq_sb[:, :], cbb_sb[:, :], start=True, stop=True),
            reads=[t_cbb, t_ones], writes=[t_pAB[0]])
        eop("vector", lambda v: v.tensor_copy(mskB_sb[:, :], pA[:, 0:256]),
            reads=[t_pAB[0]], writes=[t_mskB])
        for qc in range(2):
            mv = mskB_sb[:, qc * 32:(qc + 1) * 32].rearrange(
                "p (j o) -> p j o", o=1).broadcast_to([128, 32, 8])
            eop("vector", (lambda v, qc=qc, mv=mv: v.tensor_tensor(
                msqF_sb[:, qc * 256:(qc + 1) * 256].rearrange(
                    "p (j h) -> p j h", j=32),
                sqtB_sb[:, :].rearrange("p (j h) -> p j h", j=32),
                mv, ALU.mult)),
                reads=[t_sqtB, t_mskB], writes=[t_msqF])

        mark('l1gate')
        den_sb = sb(att1, "den1", [1, 2 * 512], F32)
        kdia_sb = sb(att1, "kdia", [128, 1024], BF16)
        vdia_sb = sb(att1, "vdia", [128, 1024], BF16)
        yt_sb = sb(att1, "yt", [128, 512], BF16)
        ytm_sb = sb(att1, "ytm", [128, 128], BF16)
        rcol_sb = sb(att1, "rcol", [128, 8], F32)
        cold_sb = sb(att1, "cold", [128, 2 * 4], F32)
        g_done = [(s_ve, s_ve.sched), (s_ev, s_ev.sched), (s_mm, s_mm.sched)]

        for h in range(8):
            for r in range(4):
                dop(f"d_kT1{r}", kT1_sb[:, r * TOK:(r + 1) * TOK],
                    kv1_g_k(r)[h * 128:(h + 1) * 128, :],
                    reads=[t_kv1G], writes=[t_kT1],
                    extra=(g_done if h == 0 else []))
                if r == 0:
                    dop("d_v10", v1_sb[:, :],
                        vrep1_dram.ap()[h * 128:(h + 1) * 128, :],
                        reads=[t_vrep1[h % 2]], writes=[t_v1],
                        extra=(g_done if h == 0 else []))
            dop("d_kd1", kdia_sb[:, :], kv1_c_k()[h * 128:(h + 1) * 128, :],
                reads=[t_kv1C], writes=[t_kdia],
                extra=(g_done if h == 0 else []))
            vdd = vdia_sb[:, :].rearrange("q (j d) -> q j d", j=8)
            vds = kv1_c_v().rearrange("(j q) d -> q j d", q=128)[
                :, :, h * 128:(h + 1) * 128]
            dop("d_vd1", vdd, vds, reads=[t_kv1C], writes=[t_vdia],
                extra=(g_done if h == 0 else []))
            kv_waits = [(dma_sems[f"d_kT1{r}"], dma_sems[f"d_kT1{r}"].sched)
                        for r in range(4)]
            kv_waits += [(dma_sems["d_v10"], dma_sems["d_v10"].sched)]
            kv_waits += [(dma_sems["d_kd1"], dma_sems["d_kd1"].sched),
                         (dma_sems["d_vd1"], dma_sems["d_vd1"].sched)]

            for hqh in range(2):
                hq = 2 * h + hqh
                for qc in range(2):
                    def pv1(j):
                        sl = j % 2
                        eop("tensor", (lambda te, j=j, sl=sl: te.matmul(
                            pattn[:, :],
                            v1_sb[:, j * 128:(j + 1) * 128],
                            phat_sb[:, sl * 512:(sl + 1) * 512],
                            start=(j == 0), stop=False)),
                            reads=([t_v1, t_phat[sl]]),
                            writes=([t_pattn] if j == 0 else []))
                        eop("tensor", (lambda te, j=j, sl=sl: te.matmul(
                            pden[0:1, 0:512],
                            onescol_sb[:, :],
                            phat_sb[:, sl * 512:(sl + 1) * 512],
                            start=(j == 0), stop=False)),
                            reads=([t_onesc, t_phat[sl]]),
                            writes=([t_pden] if j == 0 else []))
                    for j in range(28):
                        ps = j % 2
                        pb = pAB[ps]
                        col = qc * 256 + j * 8 + h
                        eop("tensor", (lambda te, hq=hq, qc=qc, j=j, pb=pb:
                            te.matmul(
                                pb[:, :],
                                kT1_sb[:, j * 128:(j + 1) * 128],
                                qT1_sb[:, hq * TOK + qc * 512:
                                       hq * TOK + (qc + 1) * 512],
                                start=True, stop=True)),
                            reads=([t_kT1, t_qT1] if j in (0, 27) else []),
                            writes=[t_pAB[ps]],
                            extra=(kv_waits if (hqh == 0 and qc == 0 and j == 0)
                                   else []))
                        sl = j % 2
                        eop("scalar", (lambda s, sl=sl, col=col, pb=pb:
                            s.activation(
                                phat_sb[:, sl * 512:(sl + 1) * 512],
                                pb[:, :],
                                AF.Square, scale=msqF_sb[:, col:col + 1])),
                            reads=[t_pAB[ps], t_msqF], writes=[t_phat[sl]])
                        if j > 0:
                            pv1(j - 1)
                    pv1(27)
                    for qtl in range(4):
                        qt = qc * 4 + qtl
                        for w in range(qtl + 1):
                            lb = qc * 4 + w
                            ps = w % 2
                            pb = pAB[ps]
                            eop("tensor", (lambda te, hq=hq, qt=qt, lb=lb,
                                           pb=pb: te.matmul(
                                pb[:, 0:128],
                                kdia_sb[:, lb * 128:(lb + 1) * 128],
                                qT1_sb[:, hq * TOK + qt * 128:
                                       hq * TOK + (qt + 1) * 128],
                                start=True, stop=True)),
                                reads=[t_kdia, t_qT1], writes=[t_pAB[ps]])
                            sl = w % 2
                            dslice = phat_sb[:, sl * 512: sl * 512 + 128]
                            eop("scalar", (lambda s, pb=pb, lb=lb, h=h,
                                           dslice=dslice: s.activation(
                                dslice, pb[:, 0:128], AF.Square,
                                scale=sdia_sb[:, lb * 8 + h: lb * 8 + h + 1])),
                                reads=[t_pAB[ps], t_sdia], writes=[t_phat[sl]])
                            if w == qtl:
                                eop("vector", (lambda v, dslice=dslice:
                                    v.tensor_tensor(
                                        dslice, dslice, tril_sb[:, :],
                                        ALU.mult)),
                                    reads=[t_tril], writes=[t_phat[sl]])
                            eop("tensor", (lambda te, lb=lb, qtl=qtl, w=w,
                                           dslice=dslice: te.matmul(
                                pattn[:, qtl * 128:(qtl + 1) * 128],
                                vdia_sb[:, lb * 128:(lb + 1) * 128],
                                dslice, start=False, stop=(w == qtl))),
                                reads=[t_vdia, t_phat[sl]], writes=[t_pattn])
                            eop("tensor", (lambda te, qtl=qtl, w=w,
                                           dslice=dslice: te.matmul(
                                pden[0:1, qtl * 128:(qtl + 1) * 128],
                                onescol_sb[:, :], dslice,
                                start=False, stop=(w == qtl))),
                                reads=[t_onesc, t_phat[sl]], writes=[t_pden])
                    dsl = (hq * 2 + qc) % 2
                    eop("vector", (lambda v, dsl=dsl: v.tensor_copy(
                        den_sb[0:1, dsl * 512:(dsl + 1) * 512], pden[0:1, :])),
                        reads=[t_pden], writes=[t_den1[dsl]])
                    dop(f"d_den{dsl}",
                        pg_dram.ap()[dsl:dsl + 1, 0:512],
                        den_sb[0:1, dsl * 512:(dsl + 1) * 512],
                        reads=[t_den1[dsl]], writes=[t_dend[dsl]])
                    dop(f"d_cold{dsl}", cold_sb[:, dsl * 4:(dsl + 1) * 4],
                        pg_dram.ap()[dsl:dsl + 1, 0:512].rearrange(
                            "o (q p) -> p (o q)", p=128),
                        reads=[t_dend[dsl]], writes=[t_cold[dsl]])
                    for qtl in range(4):
                        qt = qc * 4 + qtl
                        eop("vector", (lambda v, qtl=qtl, qt=qt, h=h, dsl=dsl:
                            v.tensor_tensor(
                                rcol_sb[:, qtl:qtl + 1],
                                cold_sb[:, dsl * 4 + qtl:dsl * 4 + qtl + 1],
                                eps_sb[:, qt * 8 + h: qt * 8 + h + 1],
                                ALU.add)),
                            reads=[t_cold[dsl], t_eps1], writes=[t_rcol])
                        eop("vector", (lambda v, qtl=qtl: v.reciprocal(
                            rcol_sb[:, qtl:qtl + 1],
                            rcol_sb[:, qtl:qtl + 1])),
                            writes=[t_rcol])
                    eop("scalar", lambda s: s.copy(yt_sb[:, :], pattn[:, :]),
                        reads=[t_pattn], writes=[t_yt])
                    for qtl in range(4):
                        ts = qtl % 2
                        eop("tensor", (lambda te, qtl=qtl, ts=ts: te.transpose(
                            ptr[ts][:, :], yt_sb[:, qtl * 128:(qtl + 1) * 128],
                            eyeb[:, :])),
                            reads=[t_yt, t_eye], writes=[t_ptr[ts]])
                        eop("scalar", (lambda s, qtl=qtl, ts=ts: s.activation(
                            ytm_sb[:, :], ptr[ts][:, :], AF.Copy,
                            scale=rcol_sb[:, qtl:qtl + 1])),
                            reads=[t_ptr[ts], t_rcol], writes=[t_ytm])
                        eop("tensor", (lambda te, ts=ts: te.transpose(
                            ptr[ts][:, :], ytm_sb[:, :], eyeb[:, :])),
                            reads=[t_ytm, t_eye], writes=[t_ptr[ts]])
                        eop("scalar", (lambda s, hq=hq, qc=qc, qtl=qtl, ts=ts:
                            s.copy(
                                lnT_sb[:, hq * TOK + qc * 512 + qtl * 128:
                                       hq * TOK + qc * 512 + (qtl + 1) * 128],
                                ptr[ts][:, :])),
                            reads=[t_ptr[ts]], writes=[t_lnT])
        mark('l1attn')
        att1.close()

        for oc in range(8):
            slot = oc % 2
            dd = wq_sb[:, slot * 4096: slot * 4096 + 4096].rearrange(
                "p (k c) -> p k c", k=16)
            dop(f"d_wq{slot}", dd, arena_colblock(wc1_arena, oc * 256, 256),
                reads=[t_wc1A], writes=[t_wq[slot]])
            dop("d_bo", bout_sb[:, :],
                bias_dram.ap()[1:2, oc * 256:(oc + 1) * 256],
                reads=[t_biasd], writes=[t_bout])
            for qt in range(8):
                ps = qt % 2
                pb = pAB[ps]
                for k in range(16):
                    eop("tensor", (lambda te, s=slot, k=k, qt=qt, pb=pb: te.matmul(
                        pb[:, 0:256],
                        lnT_sb[:, k * TOK + qt * 128: k * TOK + (qt + 1) * 128],
                        wq_sb[:, s * 4096 + k * 256: s * 4096 + (k + 1) * 256],
                        start=(k == 0), stop=False)),
                        reads=([t_wq[slot], t_lnT] if k == 0 else []),
                        writes=([t_pAB[ps]] if k == 0 else []), need_id=(k == 0))
                eop("tensor", (lambda te, oc=oc, pb=pb: te.matmul(
                    pb[:, 0:256], onesq_sb[:, :],
                    bout_sb[0:1, :],
                    start=False, stop=True)),
                    reads=[t_ones, t_bout, t_lnT, t_wq[slot]],
                    writes=[t_pAB[ps]])
                eop("vector", (lambda v, qt=qt, oc=oc, pb=pb: v.tensor_tensor(
                    x_sb[:, qt * DM + oc * 256: qt * DM + (oc + 1) * 256],
                    x_sb[:, qt * DM + oc * 256: qt * DM + (oc + 1) * 256],
                    pb[:, 0:256], ALU.add)),
                    reads=[t_pAB[ps]], writes=[t_x[qt]])


        mark('l1outproj')
        l1f_pre = [(s_mm, s_mm.sched), (s_ev, s_ev.sched), (s_ve, s_ve.sched)]
        l1f = ExitStack()
        c12_sb = sb(l1f, "c12", [128, 24], F32)
        mcol_sb = sb(l1f, "mcol", [128, 2], F32)
        w2r_sb = sb(l1f, "w2r", [128, 4 * 1024], BF16)
        evt_sb = sb(l1f, "evt", [128, 512], F32)
        t_c12 = Trk()
        t_mcol = Trk()
        t_w2r = [Trk(), Trk(), Trk(), Trk()]
        t_evt = [Trk(), Trk()]

        for mi in range(2):
            hv = arst_sb[0:1, 11 + mi:12 + mi].bitcast(BF16)
            eop("vector", (lambda v, mi=mi, hv=hv: v.tensor_copy(
                hv[0:1, 0:1], arst_sb[0:1, 5 + 2 * mi:6 + 2 * mi])),
                reads=[t_arst], writes=[t_arst], extra=l1f_pre)
            eop("vector", (lambda v, mi=mi, hv=hv: v.tensor_tensor(
                hv[0:1, 1:2], arst_sb[0:1, 5 + 2 * mi:6 + 2 * mi],
                hv[0:1, 0:1], ALU.subtract)), writes=[t_arst])
            eop("tensor", (lambda te, hv=hv: te.matmul(
                pA[:, 0:1], onesq_sb[:, :], hv[0:1, 0:1],
                start=True, stop=False)),
                reads=[t_arst, t_ones], writes=[t_pAB[0]], extra=l1f_pre)
            eop("tensor", (lambda te, hv=hv: te.matmul(
                pA[:, 0:1], onesq_sb[:, :], hv[0:1, 1:2],
                start=False, stop=True)),
                reads=[t_arst, t_ones], writes=[t_pAB[0]])
            eop("vector", (lambda v, mi=mi: v.tensor_copy(
                mcol_sb[:, mi:mi + 1], pA[:, 0:1])),
                reads=[t_pAB[0]], writes=[t_mcol])

        ph1 = ExitStack()
        ln3_sb = sb(ph1, "ln3", [128, 2 * 2048], F32)
        boxq_sb = sb(ph1, "boxq", [128, 2048], I32)
        k1b_sb = sb(ph1, "k1b", [128, 2048], BF16)
        t_ln3 = [Trk(), Trk()]
        t_boxq = Trk()
        t_k1b = Trk()
        boxq_f = boxq_sb[:, :].bitcast(F32)

        for t in range(8):
            eop("vector", (lambda v, t=t: v.reduce_sum(
                stat_sb[:, t:t + 1], x_tile(t), AX.X)),
                reads=[t_x[t]], writes=[t_stat], extra=l1f_pre)
            eop("scalar", (lambda s, t=t: s.activation(
                pP1[:, :], x_tile(t)[:, 0:1024], AF.Square,
                accum_out=stat_sb[:, 8 + t:9 + t])),
                reads=[t_x[t]], writes=[t_pP1, t_stat], extra=l1f_pre)
            eop("scalar", (lambda s, t=t: s.activation(
                pP1[:, :], x_tile(t)[:, 1024:2048], AF.Square,
                accum_out=stat_sb[:, 16 + t:17 + t])),
                reads=[t_x[t]], writes=[t_pP1, t_stat])
        eop("vector", lambda v: v.tensor_scalar(
            stat_sb[:, 24:32], stat_sb[:, 0:8], 1.0 / DM, None, ALU.mult),
            reads=[t_stat], writes=[t_stat])
        eop("vector", lambda v: v.tensor_tensor(
            stat_sb[:, 32:40], stat_sb[:, 24:32], stat_sb[:, 24:32],
            ALU.mult), writes=[t_stat])
        eop("vector", lambda v: v.tensor_tensor(
            stat_sb[:, 40:48], stat_sb[:, 8:16], stat_sb[:, 16:24],
            ALU.add), writes=[t_stat])
        eop("vector", lambda v: v.tensor_scalar(
            stat_sb[:, 48:56], stat_sb[:, 40:48], 1.0 / DM, None, ALU.mult),
            writes=[t_stat])
        eop("vector", lambda v: v.tensor_tensor(
            stat_sb[:, 40:48], stat_sb[:, 48:56], stat_sb[:, 32:40],
            ALU.subtract), writes=[t_stat])
        eop("scalar", lambda s: s.activation(
            stat_sb[:, 48:56], stat_sb[:, 40:48], AF.Sqrt,
            bias=const_sb[:, 0:1]),
            reads=[t_const], writes=[t_stat])
        eop("vector", lambda v: v.reciprocal(
            stat_sb[:, 56:64], stat_sb[:, 48:56]), writes=[t_stat])
        for t in range(8):
            l = t % 2
            eop("vector", (lambda v, t=t, l=l: v.tensor_scalar(
                ln3_sb[:, l * DM:(l + 1) * DM], x_tile(t),
                stat_sb[:, 24 + t:25 + t], stat_sb[:, 56 + t:57 + t],
                ALU.subtract, ALU.mult)),
                reads=[t_x[t], t_stat], writes=[t_ln3[l]], extra=l1f_pre)
            eop("vector", (lambda v, t=t, l=l: v.tensor_reduce(
                c12_sb[:, 16:17], ln3_sb[:, l * DM:(l + 1) * DM], AX.X,
                ALU.max, apply_absolute_value=True)),
                reads=[t_ln3[l]], writes=[t_c12])
            eop("vector", lambda v: v.tensor_scalar(
                c12_sb[:, 16:17], c12_sb[:, 16:17], 1e-5, None, ALU.max),
                writes=[t_c12])
            eop("vector", lambda v: v.reciprocal(
                c12_sb[:, 17:18], c12_sb[:, 16:17]), writes=[t_c12])
            eop("vector", (lambda v, t=t: v.tensor_scalar(
                c12_sb[:, 18:19], c12_sb[:, 17:18], 127.0, None, ALU.mult)),
                writes=[t_c12])
            eop("vector", (lambda v, t=t: v.tensor_scalar(
                c12_sb[:, t:t + 1], c12_sb[:, 16:17], mcol_sb[:, 0:1],
                1.0 / 127.0, ALU.mult, ALU.mult)),
                reads=[t_mcol], writes=[t_c12])
            eop("vector", (lambda v, l=l: v.tensor_scalar(
                boxq_f, ln3_sb[:, l * DM:(l + 1) * DM], c12_sb[:, 18:19],
                MAGIC, ALU.mult, ALU.add)),
                reads=[t_ln3[l], t_c12], writes=[t_boxq])
            eop("vector", lambda v: v.tensor_scalar(
                k1b_sb[:, :], boxq_f, MAGIC, None, ALU.subtract),
                reads=[t_boxq], writes=[t_k1b])
            for c in range(16):
                ts_ = c % 2
                eop("tensor", (lambda te, c=c, ts_=ts_: te.transpose(
                    ptr[ts_][:, :], k1b_sb[:, c * 128:(c + 1) * 128],
                    eyeb[:, :])),
                    reads=[t_k1b, t_eye], writes=[t_ptr[ts_]])
                eop("scalar", (lambda s, t=t, c=c, ts_=ts_: s.copy(
                    lnT_sb[:, c * TOK + t * 128: c * TOK + (t + 1) * 128],
                    ptr[ts_][:, :])),
                    reads=[t_ptr[ts_]], writes=[t_lnT],
                    extra=(l1f_pre if (t == 0 and c == 0) else []))
        ph1.close()

        ph2 = ExitStack()
        h1_sb = sb(ph2, "h1", [128, 2 * 8192], BF16)
        k2T_sb = sb(ph2, "k2T", [128, 2 * 8192], BF16)
        k2b_sb = sb(ph2, "k2b", [128, 512], BF16)
        box2_sb = sb(ph2, "box2", [128, 512], I32)
        t_h1 = [Trk(), Trk()]
        t_k2T = [Trk(), Trk()]
        t_k2b = [Trk(), Trk()]
        t_box2 = Trk()
        box2_f = box2_sb[:, :].bitcast(F32)

        def w1e_colblock(q):
            rank, c0 = q // 4, (q % 4) * 256
            return w1e_arena.ap()[rank].rearrange("(k p) c -> p k c", p=128)[
                :, :, c0:c0 + 256]

        accs2 = [(pA[:, :], t_pAB[0]), (pB[:, :], t_pAB[1]),
                 (pP1[:, 0:512], t_pP1), (pP1[:, 512:1024], t_pP1)]

        for g in range(4):
            up_ps1 = [(pA[:, 0:256], t_pAB[0]), (pB[:, 0:256], t_pAB[1]),
                      (pP1[:, 0:256], t_pP1), (pP1[:, 512:768], t_pP1)]

            def evac_q(q, g=g):
                for tl in range(2):
                    t = 2 * g + tl
                    pb, tp = up_ps1[(q % 2) * 2 + tl]
                    eop("scalar", (lambda s, q=q, t=t, tl=tl, pb=pb:
                        s.activation(
                            h1_sb[:, tl * 8192 + q * 256:
                                  tl * 8192 + (q + 1) * 256], pb,
                            AF.Silu, scale=c12_sb[:, t:t + 1])),
                        reads=[tp, t_c12], writes=[t_h1[tl]],
                        extra=(l1f_pre if (g == 0 and tl == 0 and q == 0)
                               else []))
            for q in range(32):
                slot = q % 2
                dd = wq_sb[:, slot * 4096: slot * 4096 + 4096].rearrange(
                    "p (k c) -> p k c", k=16)
                dop(f"d_wq{slot}", dd, w1e_colblock(q),
                    reads=[t_w1eA], writes=[t_wq[slot]])
                for tl in range(2):
                    t = 2 * g + tl
                    pb, tp = up_ps1[(q % 2) * 2 + tl]
                    for k in range(16):
                        eop("tensor", (lambda te, s=slot, k=k, t=t, pb=pb:
                            te.matmul(
                                pb,
                                lnT_sb[:, k * TOK + t * 128:
                                       k * TOK + (t + 1) * 128],
                                wq_sb[:, s * 4096 + k * 256:
                                      s * 4096 + (k + 1) * 256],
                                start=(k == 0), stop=(k == 15))),
                            reads=([t_wq[slot], t_lnT] if k in (0, 15) else []),
                            writes=([tp] if k in (0, 15) else []),
                            need_id=(k in (0, 15)))
                if q > 0:
                    evac_q(q - 1)
            evac_q(31)
            for tl in range(2):
                t = 2 * g + tl
                eop("vector", (lambda v, tl=tl: v.tensor_reduce(
                    c12_sb[:, 19:20], h1_sb[:, tl * 8192:(tl + 1) * 8192],
                    AX.X, ALU.max,
                    apply_absolute_value=True)),
                    reads=[t_h1[tl]], writes=[t_c12])
                eop("vector", lambda v: v.tensor_scalar(
                    c12_sb[:, 19:20], c12_sb[:, 19:20], 1e-5, None, ALU.max),
                    writes=[t_c12])
                eop("vector", lambda v: v.reciprocal(
                    c12_sb[:, 20:21], c12_sb[:, 19:20]), writes=[t_c12])
                eop("vector", lambda v: v.tensor_scalar(
                    c12_sb[:, 21:22], c12_sb[:, 20:21], 127.0, None, ALU.mult),
                    writes=[t_c12])
                eop("vector", (lambda v, t=t: v.tensor_scalar(
                    c12_sb[:, 8 + t:9 + t], c12_sb[:, 19:20], mcol_sb[:, 1:2],
                    1.0 / 127.0, ALU.mult, ALU.mult)),
                    reads=[t_mcol], writes=[t_c12])
                for ch in range(16):
                    eop("vector", (lambda v, ch=ch, tl=tl: v.tensor_scalar(
                        box2_f, h1_sb[:, tl * 8192 + ch * 512:
                                      tl * 8192 + (ch + 1) * 512],
                        c12_sb[:, 21:22], MAGIC, ALU.mult, ALU.add)),
                        reads=[t_h1[tl], t_c12], writes=[t_box2])
                    kslot = 0
                    eop("vector", lambda v: v.tensor_scalar(
                        k2b_sb[:, 0:512], box2_f,
                        MAGIC, None, ALU.subtract),
                        reads=[t_box2], writes=[t_k2b[0]])
                    for c4 in range(4):
                        fcc = ch * 4 + c4
                        ts_ = c4 % 2
                        eop("tensor", (lambda te, c4=c4, ts_=ts_:
                            te.transpose(
                                ptr[ts_][:, :],
                                k2b_sb[:, c4 * 128:(c4 + 1) * 128],
                                eyeb[:, :])),
                            reads=[t_k2b[0], t_eye], writes=[t_ptr[ts_]])
                        eop("scalar", (lambda s, tl=tl, fcc=fcc, ts_=ts_:
                            s.copy(
                                k2T_sb[:, tl * 8192 + fcc * 128:
                                       tl * 8192 + (fcc + 1) * 128],
                                ptr[ts_][:, :])),
                            reads=[t_ptr[ts_]], writes=[t_k2T[tl]])
            for dcp in range(2):
                for fc in range(64):
                    ws = fc % 4
                    dop(f"d_w2r{ws % 2}", w2r_sb[:, ws * 1024:(ws + 1) * 1024],
                        w2e_arena.ap()[fc * 128:(fc + 1) * 128,
                                       dcp * 1024:(dcp + 1) * 1024],
                        reads=[t_w2eA], writes=[t_w2r[ws]], extra=l1f_pre)
                    for tl in range(2):
                        for dc2 in range(2):
                            pb, tp = accs2[tl * 2 + dc2]
                            eop("tensor", (lambda te, fc=fc, tl=tl, dc2=dc2,
                                           ws=ws, pb=pb: te.matmul(
                                pb,
                                k2T_sb[:, tl * 8192 + fc * 128:
                                       tl * 8192 + (fc + 1) * 128],
                                w2r_sb[:, ws * 1024 + dc2 * 512:
                                       ws * 1024 + (dc2 + 1) * 512],
                                start=(fc == 0), stop=(fc == 63))),
                                reads=[t_k2T[tl], t_w2r[ws]],
                                writes=([tp] if fc in (0, 63) else []))
                for tl in range(2):
                    t = 2 * g + tl
                    for dc2 in range(2):
                        pb, tp = accs2[tl * 2 + dc2]
                        eop("scalar", (lambda s, pb=pb, t=t:
                            s.activation(
                                evt_sb[:, :], pb,
                                AF.Copy, scale=c12_sb[:, 8 + t:9 + t])),
                            reads=[tp, t_c12], writes=[t_evt[0]])
                        dcg = dcp * 1024 + dc2 * 512
                        eop("vector", (lambda v, t=t, dcg=dcg:
                            v.tensor_tensor(
                                x_sb[:, t * DM + dcg: t * DM + dcg + 512],
                                x_sb[:, t * DM + dcg: t * DM + dcg + 512],
                                evt_sb[:, :], ALU.add)),
                            reads=[t_evt[0]], writes=[t_x[t]])
        ph2.close()
        l1f.close()

        mark('l1ffn')
        fin_ids = []
        for t in range(8):
            fin_ids.append(dop("d_out", out_x[t * 128:(t + 1) * 128, :],
                               x_sb[:, t * DM:(t + 1) * DM], reads=[t_x[t]]))
        p.op("sync", lambda e: e.nop(),
             waits=[(dma_sems["d_out"], dma_sems["d_out"].sched)])

        p.emit(nc, ctx)
    return nc


def shard_inputs(inputs):
    x = np.ascontiguousarray(np.asarray(inputs["L0_attn_x"]))
    wqkv = np.ascontiguousarray(np.asarray(inputs["L0_attn_w_qkv"]))
    wout = np.ascontiguousarray(np.asarray(inputs["L0_attn_w_out"]))
    bout = np.ascontiguousarray(np.asarray(inputs["L0_attn_b_out"]).reshape(1, DM))
    w1 = np.asarray(inputs["L0_ffn_weight1"])
    w2 = np.asarray(inputs["L0_ffn_weight2"])
    w3 = np.asarray(inputs["L0_ffn_weight3"])
    wq1 = np.ascontiguousarray(np.asarray(inputs["L1_attn_Wq"]))
    wc1 = np.ascontiguousarray(np.asarray(inputs["L1_attn_Wc"]))
    wk1 = np.ascontiguousarray(np.asarray(inputs["L1_attn_Wk"]))
    wv1 = np.ascontiguousarray(np.asarray(inputs["L1_attn_Wv"]))
    wg1 = np.ascontiguousarray(np.asarray(inputs["L1_attn_Wg"]))
    bc1 = np.ascontiguousarray(np.asarray(inputs["L1_attn_bc"]).reshape(1, DM))
    w1b = np.ascontiguousarray(np.asarray(inputs["L1_ffn_W1"]))
    w2b = np.ascontiguousarray(np.asarray(inputs["L1_ffn_W2"]))
    maps = []
    for c in range(NCORE):
        s, r = c // 4, c % 4
        maps.append({
            "x_shard": np.ascontiguousarray(x[s, r * TOK:(r + 1) * TOK, :]),
            "wqkv_sl": np.ascontiguousarray(wqkv[c * QKV_SH:(c + 1) * QKV_SH, :]),
            "wout_sl": np.ascontiguousarray(wout[c * WOUT_SH:(c + 1) * WOUT_SH, :]),
            "b_out": bout,
            "w1_sl": np.ascontiguousarray(w1[c * W1_SH:(c + 1) * W1_SH, :]),
            "w2_sl": np.ascontiguousarray(w2[c * W1_SH:(c + 1) * W1_SH, :]),
            "w3_sl": np.ascontiguousarray(w3[c * W3_SH:(c + 1) * W3_SH, :]),
            "wq1_sl": np.ascontiguousarray(wq1[c * 256:(c + 1) * 256, :]),
            "wc1_sl": np.ascontiguousarray(wc1[c * 256:(c + 1) * 256, :]),
            "wk1_sl": np.ascontiguousarray(wk1[c * 256:(c + 1) * 256, :]),
            "wv1_sl": np.ascontiguousarray(wv1[c * 256:(c + 1) * 256, :]),
            "wg_full": wg1,
            "bc_in": bc1,
            "caus_in": _caus_mask(r),
            "maskR_in": _maskR(r),
            "w1b_sl": np.ascontiguousarray(w1b[c * 1024:(c + 1) * 1024, :]),
            "w2b_sl": np.ascontiguousarray(w2b[:, c * 1024:(c + 1) * 1024]),
        })
    return maps


def _caus_mask(r):
    m = np.zeros((1, 256), dtype=np.float32)
    for qc in range(2):
        for j in range(32):
            if j < 8 * r + 4 * qc:
                m[0, qc * 32 + j] = 1.0
    return m


def _maskR(r):
    m = np.zeros((8, 4), dtype=np.float32)
    if r > 0:
        m[:, r - 1] = 1.0
    return m


_NC_CACHE = {}


def kernel(**inputs):
    maps = shard_inputs(inputs)
    if "nc" not in _NC_CACHE:
        _NC_CACHE["nc"] = build_nc()
    nc = _NC_CACHE["nc"]
    res = run_bass_kernel_spmd(nc, maps, core_ids=list(range(NCORE)))
    out = np.zeros((2, SEQ, DM), dtype=np.float32)
    for c in range(NCORE):
        s, r = c // 4, c % 4
        out[s, r * TOK:(r + 1) * TOK, :] = res.results[c]["out"]
    return out
